# version 04
# baseline (speedup 1.0000x reference)
import numpy as np
import contextlib
import concourse.bass as bass
import concourse.mybir as mybir
from concourse.bass_utils import run_bass_kernel_spmd
from concourse.bass_core_rust import IndirectOffsetOnAxis

F32 = mybir.dt.float32
BF16 = mybir.dt.bfloat16
I32 = mybir.dt.int32
U32 = mybir.dt.uint32
AF = mybir.ActivationFunctionType
ALU = mybir.AluOpType
AX = mybir.AxisListType

T = 2048
D = 1024
NT = T // 128
ND = D // 128
MAGIC = float(1.5 * 2 ** 23)
NEG = -1.0e30

ENGINES = ("tensor", "vector", "scalar", "gpsimd", "sync")


class Prog:
    def __init__(self, nc):
        self.nc = nc
        self.items = []

    def op(self, engine, fn):
        self.items.append((engine, "op", fn))

    def dma(self, engine, fn):
        self.items.append((engine, "dma", fn))

    def barrier(self):
        self.items.append((None, "barrier", None))

    def build(self):
        nc = self.nc
        sems = {}
        self._sem_ctxs = []
        for e in ENGINES:
            for kind in ("op", "dma"):
                ctx = nc.semaphore(name=f"sem_{e}_{kind}_{id(self) % 10000}")
                sems[(e, kind)] = ctx.__enter__()
                self._sem_ctxs.append(ctx)
        phases = [[]]
        for it in self.items:
            if it[1] == "barrier":
                if phases[-1]:
                    phases.append([])
            else:
                phases[-1].append(it)
        cum = {k: 0 for k in sems}
        schedule = {e: [] for e in ENGINES}
        for ph in phases:
            used = {}
            for (e, kind, fn) in ph:
                used.setdefault(e, []).append((kind, fn))
            wait_targets = [(k, cum[k]) for k in sems if cum[k] > 0]
            for e, ops in used.items():
                kinds = {k for k, _ in ops}
                assert len(kinds) == 1, f"phase mixes dma+op on engine {e}"
                is_dma = kinds == {"dma"}
                key = (e, "dma" if is_dma else "op")
                for (sk, val) in wait_targets:
                    schedule[e].append(("wait", sk, val))
                for i, (kind, fn) in enumerate(ops):
                    last = i == len(ops) - 1
                    inc = 16 if is_dma else (1 if last else 0)
                    schedule[e].append(("exe", fn, inc, key))
                cum[key] += 16 * len(ops) if is_dma else 1
        for sk in sems:
            if cum[sk] > 0:
                schedule["sync"].append(("wait", sk, cum[sk]))
        block_ctx = nc.Block()
        block = block_ctx.__enter__()

        def run_engine(ename, eng):
            for entry in schedule[ename]:
                if entry[0] == "wait":
                    _, sk, val = entry
                    eng.wait_ge(sems[sk], val)
                else:
                    _, fn, inc, key = entry
                    inst = fn(eng)
                    if inc:
                        inst.then_inc(sems[key], inc)

        if schedule["sync"]:
            @block.sync
            def _(sync):
                run_engine("sync", sync)
        if schedule["gpsimd"]:
            @block.gpsimd
            def _(gpsimd):
                run_engine("gpsimd", gpsimd)
        if schedule["scalar"]:
            @block.scalar
            def _(scalar):
                run_engine("scalar", scalar)
        if schedule["vector"]:
            @block.vector
            def _(vector):
                run_engine("vector", vector)
        if schedule["tensor"]:
            @block.tensor
            def _(tensor):
                run_engine("tensor", tensor)
        block_ctx.__exit__(None, None, None)
        for ctx in reversed(self._sem_ctxs):
            ctx.__exit__(None, None, None)


def emit_ln(p, xt, xnt, sqt, stat, eps_ap):
    p.op("scalar", lambda e: e.activation(sqt[:], xt[:], AF.Square, accum_out=stat[:, 1:2]))
    p.op("vector", lambda e: e.reduce_sum(stat[:, 0:1], xt[:], axis=AX.X))
    p.barrier()
    p.op("vector", lambda e: e.tensor_scalar_mul(stat[:, 2:3], stat[:, 0:1], 1.0 / D))
    p.barrier()
    p.op("vector", lambda e: e.tensor_mul(stat[:, 4:5], stat[:, 2:3], stat[:, 2:3]))
    p.op("vector", lambda e: e.tensor_scalar(stat[:, 1:2], stat[:, 1:2], 1.0 / D, None, ALU.mult))
    p.barrier()
    p.op("vector", lambda e: e.tensor_sub(stat[:, 5:6], stat[:, 1:2], stat[:, 4:5]))
    p.barrier()
    p.op("scalar", lambda e: e.activation(stat[:, 6:7], stat[:, 5:6], AF.Ln, bias=eps_ap))
    p.barrier()
    p.op("scalar", lambda e: e.activation(stat[:, 3:4], stat[:, 6:7], AF.Exp, scale=-0.5))
    p.barrier()
    p.op("vector", lambda e: e.tensor_scalar(xnt[:], xt[:], stat[:, 2:3], stat[:, 3:4],
                                             ALU.subtract, ALU.mult))
    p.barrier()


def emit_gemm_tok(p, ps_list, lhs_fm, lhs_off_fn, w_sb, w_off_fn, bias_row, bias_off, nb_count,
                  tt, onesrow, out_fn, nk=ND):
    for nb in range(nb_count):
        ps = ps_list[nb % 2]
        for kc in range(nk):
            p.op("tensor", (lambda nb, kc, ps: lambda e: e.matmul(
                ps[:, 0:512],
                lhs_fm[:, lhs_off_fn(kc) + tt * 128: lhs_off_fn(kc) + tt * 128 + 128],
                w_sb[:, w_off_fn(kc, nb): w_off_fn(kc, nb) + 512],
                start=(kc == 0), stop=False))(nb, kc, ps))
        p.op("tensor", (lambda nb, ps: lambda e: e.matmul(
            ps[:, 0:512], onesrow[0:1, 0:128],
            bias_row[0:1, bias_off + nb * 512: bias_off + nb * 512 + 512],
            start=False, stop=True))(nb, ps))
        p.barrier()
        out_fn(nb, ps)
        p.barrier()


def build_l0(nc):
    H, DK, W = 16, 64, 135

    x = nc.dram_tensor("x", [T, D], F32, kind="ExternalInput").ap()
    wqkvT = nc.dram_tensor("wqkvT", [D, 3 * D], F32, kind="ExternalInput").ap()
    bqkv = nc.dram_tensor("bqkv", [3 * D], F32, kind="ExternalInput").ap()
    wlatT = nc.dram_tensor("wlatT", [DK, DK], F32, kind="ExternalInput").ap()
    blat = nc.dram_tensor("blat", [DK], F32, kind="ExternalInput").ap()
    wprojT = nc.dram_tensor("wprojT", [D, D], F32, kind="ExternalInput").ap()
    bproj = nc.dram_tensor("bproj", [D], F32, kind="ExternalInput").ap()
    wffnT = nc.dram_tensor("wffnT", [D, D], I32, kind="ExternalInput").ap()
    wscal = nc.dram_tensor("wscal", [1], F32, kind="ExternalInput").ap()
    bffn = nc.dram_tensor("bffn", [D], F32, kind="ExternalInput").ap()
    maskc = nc.dram_tensor("maskc", [3 * 128, W], F32, kind="ExternalInput").ap()
    multc = nc.dram_tensor("multc", [2 * 128, 1], F32, kind="ExternalInput").ap()

    x1 = nc.dram_tensor("x1", [T, D], F32, kind="ExternalOutput").ap()

    xnbf_d = nc.dram_tensor("xnbf_d", [T, D], BF16, kind="Internal").ap()
    xqbf_d = nc.dram_tensor("xqbf_d", [T, D], BF16, kind="Internal").ap()
    vlat_d = nc.dram_tensor("vlat_d", [T, D], BF16, kind="Internal").ap()
    attn_d = nc.dram_tensor("attn_d", [T, D], BF16, kind="Internal").ap()
    cvec_d = nc.dram_tensor("cvec_d", [D], F32, kind="Internal").ap()

    p = Prog(nc)
    es = contextlib.ExitStack()
    xnfm = es.enter_context(nc.sbuf_tensor("xnfm", [128, ND * T], BF16))
    fmA = es.enter_context(nc.sbuf_tensor("fmA", [128, ND * T], BF16))
    fmB = es.enter_context(nc.sbuf_tensor("fmB", [128, ND * T], BF16))
    wstr = es.enter_context(nc.sbuf_tensor("wstr", [128, 2 * 1024], BF16))
    wpro = es.enter_context(nc.sbuf_tensor("wpro", [128, ND * 512 * 2], BF16))
    wffn = es.enter_context(nc.sbuf_tensor("wffn", [128, ND * 512 * 2], BF16))
    wlat1 = es.enter_context(nc.sbuf_tensor("wlat1", [128, 64], BF16))
    wlat1f = es.enter_context(nc.sbuf_tensor("wlat1f", [64, 64], F32))
    wlat2 = es.enter_context(nc.sbuf_tensor("wlat2", [128, 128], BF16))
    bqs = es.enter_context(nc.sbuf_tensor("bqs", [128, ND], F32))
    onesrow = es.enter_context(nc.sbuf_tensor("onesrow", [1, 512], BF16))
    onesf = es.enter_context(nc.sbuf_tensor("onesf", [1, 128], F32))
    brow = es.enter_context(nc.sbuf_tensor("brow", [1, 2 * D], F32))
    browb = es.enter_context(nc.sbuf_tensor("browb", [1, 2 * D], BF16))
    wsb = es.enter_context(nc.sbuf_tensor("wsb", [128, 1], F32))
    cvec = es.enter_context(nc.sbuf_tensor("cvec", [16, 64], F32))
    ccol = es.enter_context(nc.sbuf_tensor("ccol", [128, ND], BF16))
    masks = es.enter_context(nc.sbuf_tensor("masks", [128, 3 * W], F32))
    mults = es.enter_context(nc.sbuf_tensor("mults", [128, 2], F32))
    bqkv_t = es.enter_context(nc.sbuf_tensor("bqkv_t", [64, 16], F32))
    blat_s = es.enter_context(nc.sbuf_tensor("blat_s", [1, 64], F32))
    ident = es.enter_context(nc.sbuf_tensor("ident", [128, 128], BF16))
    stat = es.enter_context(nc.sbuf_tensor("stat", [128, 8], F32))
    epst = es.enter_context(nc.sbuf_tensor("epst", [128, 2], F32))
    magicc = es.enter_context(nc.sbuf_tensor("magicc", [128, 2], F32))
    amax = es.enter_context(nc.sbuf_tensor("amax", [128, 4], F32))
    xt = es.enter_context(nc.sbuf_tensor("xt", [128, D], F32))
    xnt = es.enter_context(nc.sbuf_tensor("xnt", [128, D], F32))
    sqt = es.enter_context(nc.sbuf_tensor("sqt", [128, D], F32))
    bft = es.enter_context(nc.sbuf_tensor("bft", [128, D], BF16))
    kvt = es.enter_context(nc.sbuf_tensor("kvt", [128, T], BF16))
    swide = es.enter_context(nc.sbuf_tensor("swide", [128, H * W], F32))
    pwide = es.enter_context(nc.sbuf_tensor("pwide", [128, H * W], BF16))
    hmax = es.enter_context(nc.sbuf_tensor("hmax", [128, H], F32))
    zrow = es.enter_context(nc.sbuf_tensor("zrow", [128, 2 * H], F32))
    vh0 = es.enter_context(nc.sbuf_tensor("vh0", [128, D], BF16))
    vh1 = es.enter_context(nc.sbuf_tensor("vh1", [8, D], BF16))
    pT0 = es.enter_context(nc.sbuf_tensor("pT0", [128, 128], BF16))
    pT1 = es.enter_context(nc.sbuf_tensor("pT1", [7, 128], BF16))
    st_ = es.enter_context(nc.sbuf_tensor("st_", [128, D], F32))
    gt_ = es.enter_context(nc.sbuf_tensor("gt_", [128, D], F32))
    ps0 = es.enter_context(nc.psum_tensor("ps0", [128, 512], F32))
    ps1 = es.enter_context(nc.psum_tensor("ps1", [128, 512], F32))
    pss0 = es.enter_context(nc.psum_tensor("pss0", [128, W], F32))
    pss1 = es.enter_context(nc.psum_tensor("pss1", [128, W], F32))
    psa0 = es.enter_context(nc.psum_tensor("psa0", [128, 64], F32))
    psa1 = es.enter_context(nc.psum_tensor("psa1", [128, 64], F32))
    ptr0 = es.enter_context(nc.psum_tensor("ptr0", [128, 128], BF16))
    ptr1 = es.enter_context(nc.psum_tensor("ptr1", [7, 128], BF16))
    if True:
        p.dma("gpsimd", lambda e: e.dma_start(out=wpro.rearrange("p (k n) -> p k n", k=ND),
                                              in_=wprojT.rearrange("(k p) n -> p k n", p=128)))
        p.dma("gpsimd", lambda e: e.dma_start(out=wffn.rearrange("p (k n) -> p k n", k=ND),
                                              in_=wffnT.rearrange("(k p) n -> p k n", p=128)))
        p.dma("gpsimd", lambda e: e.dma_start(out=wlat1[0:64, :], in_=wlatT[:]))
        p.dma("gpsimd", lambda e: e.dma_start(out=wlat1[64:128, :], in_=wlatT[:]))
        p.dma("sync", lambda e: e.dma_start(out=wlat1f[:], in_=wlatT[:]))
        p.dma("sync", lambda e: e.dma_start(out=masks.rearrange("p (m w) -> p m w", m=3),
                                            in_=maskc.rearrange("(m p) w -> p m w", p=128)))
        p.dma("sync", lambda e: e.dma_start(out=mults.rearrange("p (m o) -> p m o", o=1),
                                            in_=multc.rearrange("(m p) o -> p m o", p=128),
                                            allow_slow_non_contiguous=True))
        p.dma("sync", lambda e: e.dma_start(out=bqkv_t[:],
                                            in_=bqkv[2048:3072].rearrange("(h i) -> i h", i=64),
                                            allow_slow_non_contiguous=True))
        p.dma("sync", lambda e: e.dma_start(out=blat_s[:], in_=blat.rearrange("(o d) -> o d", o=1)))
        p.dma("sync", lambda e: e.dma_start(out=brow[0:1, 0:D], in_=bproj.rearrange("(o d) -> o d", o=1)))
        p.dma("sync", lambda e: e.dma_start(out=brow[0:1, D:2 * D], in_=bffn.rearrange("(o d) -> o d", o=1)))
        p.dma("sync", lambda e: e.dma_start(out=wsb[0:1, 0:1], in_=wscal.rearrange("(o d) -> o d", o=1)))
        p.dma("sync", lambda e: e.dma_start(out=bqs.rearrange("p (k o) -> p k o", o=1),
                                            in_=bqkv[0:1024].rearrange("(k p o) -> p k o", p=128, o=1),
                                            allow_slow_non_contiguous=True))
        p.barrier()
        p.op("vector", lambda e: e.memset(onesrow[:], 1.0))
        p.op("vector", lambda e: e.memset(epst[:, 0:1], 1e-5))
        p.op("vector", lambda e: e.memset(epst[:, 1:2], 1e-6))
        p.barrier()
        p.op("vector", lambda e: e.memset(magicc[:, 0:1], 1.0))
        p.barrier()
        p.op("vector", lambda e: e.tensor_scalar(magicc[:, 1:2], magicc[:, 0:1],
                                                 9.313225746154785e-10, 1.0, ALU.add, ALU.subtract))
        p.barrier()
        p.op("vector", lambda e: e.tensor_scalar(magicc[:, 1:2], magicc[:, 1:2], 1073741824.0,
                                                 None, ALU.mult))
        p.barrier()
        p.op("vector", lambda e: e.tensor_scalar(magicc[:, 0:1], magicc[:, 1:2],
                                                 6755399441055744.0, 12582912.0, ALU.mult, ALU.add))
        p.barrier()

        p.op("scalar", lambda e: e.memzero(wlat2[:]))
        p.op("gpsimd", lambda e: e.memset(onesf[:], 1.0))
        p.barrier()
        p.op("gpsimd", lambda e: e.memset(ident[:], 1.0))
        p.op("vector", lambda e: e.tensor_copy(wlat2[0:64, 0:64], wlat1[0:64, :]))
        p.barrier()
        p.op("vector", lambda e: e.tensor_copy(wlat2[64:128, 64:128], wlat1[64:128, :]))
        p.op("gpsimd", lambda e: e.affine_select(ident[:], ident[:], [[-1, 128]], ALU.is_equal,
                                                 0.0, base=0, channel_multiplier=1))
        p.op("scalar", lambda e: e.mul(bqs[:], bqs[:], 0.125))
        p.op("vector", lambda e: e.memset(stat[:, 7:8], 1.0))
        p.barrier()
        p.op("tensor", lambda e: e.matmul(psa0[:, 0:1], onesf[0:1, 0:128], wsb[0:1, 0:1],
                                          start=True, stop=True))
        p.barrier()
        p.op("vector", lambda e: e.tensor_copy(wsb[:], psa0[:, 0:1]))
        p.barrier()
        p.op("tensor", lambda e: e.matmul(psa0[0:16, 0:64], bqkv_t[:], wlat1f[:], start=True, stop=False))
        p.op("tensor", lambda e: e.matmul(psa0[0:16, 0:64], onesf[0:1, 0:16], blat_s[:],
                                          start=False, stop=True))
        p.barrier()
        p.op("vector", lambda e: e.tensor_copy(cvec[:], psa0[0:16, 0:64]))
        p.barrier()
        p.dma("sync", lambda e: e.dma_start(out=cvec_d.rearrange("(h i) -> h i", h=16), in_=cvec[:]))
        p.barrier()
        p.dma("gpsimd", lambda e: e.dma_start(out=ccol.rearrange("p (k o) -> p k o", o=1),
                                              in_=cvec_d.rearrange("(k p o) -> p k o", p=128, o=1),
                                              allow_slow_non_contiguous=True))
        p.barrier()
        for nb in range(2):
            for kc in range(ND):
                p.op("tensor", (lambda kc, nb: lambda e: e.matmul(
                    ps0[0:1, 0:512], ccol[:, kc:kc + 1],
                    wpro[:, kc * D + nb * 512: kc * D + nb * 512 + 512],
                    start=(kc == 0), stop=(kc == ND - 1)))(kc, nb))
            p.barrier()
            p.op("vector", (lambda nb: lambda e: e.tensor_add(
                brow[0:1, nb * 512:nb * 512 + 512], brow[0:1, nb * 512:nb * 512 + 512],
                ps0[0:1, 0:512]))(nb))
            p.barrier()
        p.op("vector", lambda e: e.tensor_copy(browb[:], brow[:]))
        p.barrier()

        for tt in range(NT):
            sl = slice(tt * 128, tt * 128 + 128)
            p.dma("sync", (lambda sl: lambda e: e.dma_start(out=xt[:], in_=x[sl, :]))(sl))
            p.barrier()
            emit_ln(p, xt, xnt, sqt, stat, epst[:, 0:1])
            p.op("vector", lambda e: e.tensor_copy(bft[:], xnt[:]))
            p.barrier()
            p.dma("sync", (lambda sl: lambda e: e.dma_start(out=xnbf_d[sl, :], in_=bft[:]))(sl))
            p.op("scalar", lambda e: e.activation(sqt[:], xnt[:], AF.Square, accum_out=stat[:, 1:2]))
            p.barrier()
            p.op("scalar", lambda e: e.activation(stat[:, 1:2], stat[:, 1:2], AF.Ln,
                                                  bias=epst[:, 1:2], scale=1.0 / D))
            p.barrier()
            p.op("scalar", lambda e: e.activation(stat[:, 3:4], stat[:, 1:2], AF.Exp, scale=-0.5))
            p.barrier()
            p.op("vector", lambda e: e.tensor_scalar(sqt[:], xnt[:], stat[:, 3:4], None, ALU.mult))
            p.barrier()
            p.op("vector", lambda e: e.reduce_max(amax[:, 0:1], sqt[:], axis=AX.X,
                                                  apply_absolute_value=True))
            p.barrier()
            p.op("vector", lambda e: e.tensor_scalar_max(amax[:, 0:1], amax[:, 0:1], 1e-5))
            p.barrier()
            p.op("vector", lambda e: e.reciprocal(amax[:, 1:2], amax[:, 0:1]))
            p.op("vector", lambda e: e.tensor_scalar(amax[:, 2:3], amax[:, 0:1], 1.0 / 127.0,
                                                     wsb[:, 0:1], ALU.mult, ALU.mult))
            p.barrier()
            p.op("vector", lambda e: e.tensor_scalar_mul(amax[:, 1:2], amax[:, 1:2], 127.0))
            p.barrier()
            p.op("vector", lambda e: e.tensor_scalar(sqt[:], sqt[:], amax[:, 1:2], None, ALU.mult))
            p.barrier()
            p.op("vector", lambda e: e.tensor_scalar(sqt[:], sqt[:], magicc[:, 0:1], magicc[:, 0:1],
                                                     ALU.add, ALU.subtract))
            p.barrier()
            p.op("vector", lambda e: e.tensor_scalar(sqt[:], sqt[:], -128.0, 127.0, ALU.max, ALU.min))
            p.barrier()
            p.op("vector", lambda e: e.tensor_scalar(bft[:], sqt[:], amax[:, 2:3], None, ALU.mult))
            p.barrier()
            p.dma("sync", (lambda sl: lambda e: e.dma_start(out=xqbf_d[sl, :], in_=bft[:]))(sl))
            p.barrier()

        for kc in range(ND):
            p.dma("sync", (lambda kc: lambda e: e.dma_start(
                out=xnfm[:, kc * T:(kc + 1) * T], in_=xnbf_d[:, kc * 128:(kc + 1) * 128],
                transpose=True))(kc))
        p.barrier()

        def load_wslice(slot, col0):
            p.dma("gpsimd", (lambda slot, col0: lambda e: e.dma_start(
                out=wstr.rearrange("p (s k n) -> p s k n", s=2, k=ND)[:, slot, :, :],
                in_=wqkvT.rearrange("(k p) n -> p k n", p=128)[:, :, col0:col0 + 128]))(slot, col0))

        for oc in range(ND):
            load_wslice(oc % 2, oc * 128)
            p.barrier()
            for tb in range(4):
                ps = [ps0, ps1][tb % 2]
                for kc in range(ND):
                    p.op("tensor", (lambda oc, tb, kc, ps: lambda e: e.matmul(
                        ps[:, 0:512],
                        wstr[:, (oc % 2) * 1024 + kc * 128:(oc % 2) * 1024 + kc * 128 + 128],
                        xnfm[:, kc * T + tb * 512: kc * T + tb * 512 + 512],
                        start=(kc == 0), stop=(kc == ND - 1)))(oc, tb, kc, ps))
                p.barrier()
                p.op("scalar", (lambda oc, tb, ps: lambda e: e.activation(
                    fmA[:, oc * T + tb * 512: oc * T + tb * 512 + 512], ps[:, 0:512],
                    AF.Identity, bias=bqs[:, oc:oc + 1], scale=0.125))(oc, tb, ps))
                p.barrier()
        for oc in range(ND):
            load_wslice(oc % 2, D + oc * 128)
            p.barrier()
            for tb in range(4):
                ps = [ps0, ps1][tb % 2]
                for kc in range(ND):
                    p.op("tensor", (lambda oc, tb, kc, ps: lambda e: e.matmul(
                        ps[:, 0:512],
                        wstr[:, (oc % 2) * 1024 + kc * 128:(oc % 2) * 1024 + kc * 128 + 128],
                        xnfm[:, kc * T + tb * 512: kc * T + tb * 512 + 512],
                        start=(kc == 0), stop=(kc == ND - 1)))(oc, tb, kc, ps))
                p.barrier()
                p.op("scalar", (lambda tb, ps: lambda e: e.copy(
                    kvt[:, tb * 512:tb * 512 + 512], ps[:, 0:512]))(tb, ps))
                p.barrier()
            for tb in range(4):
                ps = [ps0, ps1][tb % 2]
                p.op("tensor", (lambda tb, ps: lambda e: e.matmul(
                    ps[:, 0:512], wlat2[:], kvt[:, tb * 512:tb * 512 + 512],
                    start=True, stop=True))(tb, ps))
                p.barrier()
                p.op("scalar", (lambda oc, tb, ps: lambda e: e.copy(
                    fmB[:, oc * T + tb * 512: oc * T + tb * 512 + 512], ps[:, 0:512]))(oc, tb, ps))
                p.barrier()
        for oc in range(ND):
            load_wslice(oc % 2, 2 * D + oc * 128)
            p.barrier()
            for tb in range(4):
                ps = [ps0, ps1][tb % 2]
                for kc in range(ND):
                    p.op("tensor", (lambda oc, tb, kc, ps: lambda e: e.matmul(
                        ps[:, 0:512],
                        wstr[:, (oc % 2) * 1024 + kc * 128:(oc % 2) * 1024 + kc * 128 + 128],
                        xnfm[:, kc * T + tb * 512: kc * T + tb * 512 + 512],
                        start=(kc == 0), stop=(kc == ND - 1)))(oc, tb, kc, ps))
                p.barrier()
                p.op("scalar", (lambda tb, ps: lambda e: e.copy(
                    kvt[:, tb * 512:tb * 512 + 512], ps[:, 0:512]))(tb, ps))
                p.barrier()
            for hh in range(2):
                h = oc * 2 + hh
                for tt in range(NT):
                    psx = [psa0, psa1][tt % 2]
                    p.op("tensor", (lambda hh, tt, psx: lambda e: e.matmul(
                        psx[:, 0:64], kvt[hh * 64:hh * 64 + 64, tt * 128:tt * 128 + 128],
                        wlat1[hh * 64:hh * 64 + 64, :], start=True, stop=True))(hh, tt, psx))
                    p.barrier()
                    p.op("vector", (lambda psx: lambda e: e.tensor_copy(bft[:, 0:64], psx[:, 0:64]))(psx))
                    p.barrier()
                    p.dma("sync", (lambda h, tt: lambda e: e.dma_start(
                        out=vlat_d[tt * 128:tt * 128 + 128, h * 64:h * 64 + 64],
                        in_=bft[:, 0:64]))(h, tt))
                    p.barrier()

        for tt in range(NT):
            base = 0 if tt == 0 else (1914 if tt == NT - 1 else tt * 128 - 3)
            mi = 0 if tt == 0 else (2 if tt == NT - 1 else 1)
            far = 2047 if tt == 0 else 0
            p.dma("sync", (lambda base: lambda e: e.dma_start(out=vh0[:], in_=vlat_d[base:base + 128, :]))(base))
            p.dma("sync", (lambda base: lambda e: e.dma_start(out=vh1[0:6, :], in_=vlat_d[base + 128:base + 134, :]))(base))
            p.dma("sync", (lambda far: lambda e: e.dma_start(out=vh1[6:7, :], in_=vlat_d[far:far + 1, :]))(far))
            p.barrier()
            for h in range(H):
                oc, hb = h // 2, (h % 2) * 64
                pss = [pss0, pss1][h % 2]
                p.op("tensor", (lambda oc, hb, tt, base, pss: lambda e: e.matmul(
                    pss[:, 0:134], fmA[hb:hb + 64, oc * T + tt * 128: oc * T + tt * 128 + 128],
                    fmB[hb:hb + 64, oc * T + base: oc * T + base + 134],
                    start=True, stop=True))(oc, hb, tt, base, pss))
                p.op("tensor", (lambda oc, hb, tt, far, pss: lambda e: e.matmul(
                    pss[:, 134:135], fmA[hb:hb + 64, oc * T + tt * 128: oc * T + tt * 128 + 128],
                    fmB[hb:hb + 64, oc * T + far: oc * T + far + 1],
                    start=True, stop=True))(oc, hb, tt, far, pss))
                p.barrier()
                p.op("vector", (lambda h, mi, pss: lambda e: e.tensor_add(
                    swide[:, h * W:(h + 1) * W], pss[:, 0:W], masks[:, mi * W:(mi + 1) * W]))(h, mi, pss))
                p.barrier()
            p.op("vector", lambda e: e.reduce_max(hmax[:], swide.rearrange("p (h w) -> p h w", h=H),
                                                  axis=AX.X, negate=True))
            p.barrier()
            p.op("vector", lambda e: e.tensor_tensor(
                swide.rearrange("p (h w) -> p h w", h=H),
                swide.rearrange("p (h w) -> p h w", h=H),
                hmax.rearrange("p (h o) -> p h o", o=1).broadcast_to([128, H, W]),
                ALU.add))
            p.barrier()
            p.op("scalar", lambda e: e.activation(pwide[:], swide[:], AF.Exp))
            p.barrier()
            if tt == 0 or tt == NT - 1:
                mj = 0 if tt == 0 else 1
                p.op("vector", (lambda mj: lambda e: e.tensor_scalar(
                    pwide.rearrange("p (h w) -> p h w", h=H)[:, :, 134:135],
                    pwide.rearrange("p (h w) -> p h w", h=H)[:, :, 134:135],
                    mults[:, mj:mj + 1], None, ALU.mult))(mj))
                p.barrier()
            p.op("vector", lambda e: e.reduce_sum(zrow[:, 0:H], pwide.rearrange("p (h w) -> p h w", h=H),
                                                  axis=AX.X))
            p.barrier()
            p.op("vector", lambda e: e.reciprocal(zrow[:, H:2 * H], zrow[:, 0:H]))
            p.barrier()
            for h in range(H):
                psa = [psa0, psa1][h % 2]
                p.op("tensor", (lambda h: lambda e: e.transpose(
                    ptr0[:], pwide[:, h * W:h * W + 128], ident[:]))(h))
                p.op("tensor", (lambda h: lambda e: e.transpose(
                    ptr1[0:7, :], pwide[:, h * W + 128:h * W + 135], ident[:]))(h))
                p.barrier()
                p.op("vector", lambda e: e.tensor_copy(pT0[:], ptr0[:]))
                p.op("scalar", lambda e: e.copy(pT1[0:7, :], ptr1[0:7, :]))
                p.barrier()
                p.op("tensor", (lambda h, psa: lambda e: e.matmul(
                    psa[:, 0:64], pT0[:], vh0[:, h * 64:h * 64 + 64],
                    start=True, stop=False))(h, psa))
                p.op("tensor", (lambda h, psa: lambda e: e.matmul(
                    psa[:, 0:64], pT1[0:7, :], vh1[0:7, h * 64:h * 64 + 64],
                    start=False, stop=True))(h, psa))
                p.barrier()
                p.op("vector", (lambda h, psa: lambda e: e.tensor_scalar(
                    st_[:, h * 64:h * 64 + 64], psa[:, 0:64], zrow[:, H + h:H + h + 1],
                    None, ALU.mult))(h, psa))
                p.barrier()
            p.op("vector", lambda e: e.tensor_copy(bft[:], st_[:]))
            p.barrier()
            p.dma("sync", (lambda tt: lambda e: e.dma_start(
                out=attn_d[tt * 128:tt * 128 + 128, :], in_=bft[:]))(tt))
            p.barrier()

        for kc in range(ND):
            p.dma("sync", (lambda kc: lambda e: e.dma_start(
                out=fmA[:, kc * T:(kc + 1) * T], in_=xqbf_d[:, kc * 128:(kc + 1) * 128],
                transpose=True))(kc))
            p.dma("sync", (lambda kc: lambda e: e.dma_start(
                out=fmB[:, kc * T:(kc + 1) * T], in_=attn_d[:, kc * 128:(kc + 1) * 128],
                transpose=True))(kc))
        p.barrier()

        for tt in range(NT):
            sl = slice(tt * 128, tt * 128 + 128)

            def evac_s(nb, ps):
                p.op("scalar", (lambda nb, ps: lambda e: e.copy(
                    st_[:, nb * 512:nb * 512 + 512], ps[:, 0:512]))(nb, ps))

            emit_gemm_tok(p, [ps0, ps1], fmB, lambda kc: kc * T, wpro,
                          lambda kc, nb: kc * D + nb * 512, browb, 0, 2, tt, onesrow, evac_s)

            def evac_g(nb, ps):
                p.op("scalar", (lambda nb, ps: lambda e: e.activation(
                    gt_[:, nb * 512:nb * 512 + 512], ps[:, 0:512], AF.Silu))(nb, ps))

            emit_gemm_tok(p, [ps0, ps1], fmA, lambda kc: kc * T, wffn,
                          lambda kc, nb: kc * D + nb * 512, browb, D, 2, tt, onesrow, evac_g)
            p.dma("sync", (lambda sl: lambda e: e.dma_start(out=xt[:], in_=x[sl, :]))(sl))
            p.barrier()
            p.op("vector", lambda e: e.tensor_mul(st_[:], st_[:], gt_[:]))
            p.barrier()
            p.op("vector", lambda e: e.tensor_add(st_[:], st_[:], xt[:]))
            p.barrier()
            p.dma("sync", (lambda sl: lambda e: e.dma_start(out=x1[sl, :], in_=st_[:]))(sl))
            p.barrier()

        p.build()
    es.close()
    return nc


def l0_consts():
    W = 135
    masks = np.full((3, 128, W), NEG, np.float32)
    for t in range(128):
        lo, hi = max(0, t - 3), min(133, t + 3)
        masks[0, t, lo:hi + 1] = 0.0
        if t < 3:
            masks[0, t, 134] = 0.0
    for t in range(128):
        masks[1, t, t:t + 7] = 0.0
    for t in range(128):
        lo, hi = t + 3, min(133, t + 9)
        masks[2, t, lo:hi + 1] = 0.0
        if t >= 125:
            masks[2, t, 134] = 0.0
    mults = np.zeros((2, 128, 1), np.float32)
    mults[0, 0, 0], mults[0, 1, 0], mults[0, 2, 0] = 3.0, 2.0, 1.0
    mults[1, 125, 0], mults[1, 126, 0], mults[1, 127, 0] = 1.0, 2.0, 3.0
    return masks.reshape(3 * 128, W), mults.reshape(2 * 128, 1)


def l0_inmaps(inputs):
    masks, mults = l0_consts()
    xfull = np.ascontiguousarray(inputs["L0_attn_x"], np.float32)
    wqkvT = np.ascontiguousarray(inputs["L0_attn_W_qkv"].T)
    wlatT = np.ascontiguousarray(inputs["L0_attn_W_lat"].T)
    wprojT = np.ascontiguousarray(inputs["L0_attn_W_proj"].T)
    wffnT = np.ascontiguousarray(inputs["L0_ffn_w_int8"].T.astype(np.int32))
    ws = np.asarray(inputs["L0_ffn_weight_scale"], np.float32).reshape(1)
    maps = []
    for c in range(8):
        maps.append({
            "x": np.ascontiguousarray(xfull[c]),
            "wqkvT": wqkvT, "bqkv": inputs["L0_attn_b_qkv"],
            "wlatT": wlatT, "blat": inputs["L0_attn_b_lat"],
            "wprojT": wprojT, "bproj": inputs["L0_attn_b_proj"],
            "wffnT": wffnT, "wscal": ws, "bffn": inputs["L0_ffn_bias"],
            "maskc": masks, "multc": mults,
        })
    return maps


def l1_inmaps(inputs, x1_shards):
    tril = (np.arange(128)[:, None] < np.arange(128)[None, :]).astype(np.float32)
    maps = []
    base = {
        "wprojW": np.ascontiguousarray(inputs["L1_attn_in_proj_w"], np.float32),
        "nsc": inputs["L1_attn_norm_scale"], "nbi": inputs["L1_attn_norm_bias"],
        "alog": inputs["L1_attn_A_log"], "dtb": inputs["L1_attn_dt_bias"],
        "bcB": inputs["L1_attn_bcB_scale"], "bcC": inputs["L1_attn_bcC_scale"],
        "Bbi": inputs["L1_attn_B_bias"], "Cbi": inputs["L1_attn_C_bias"],
        "thw": np.ascontiguousarray(inputs["L1_attn_theta_w"], np.float32),
        "Dsk": inputs["L1_attn_D"],
        "wout": np.ascontiguousarray(inputs["L1_attn_out_proj_w"], np.float32),
        "fker": np.ascontiguousarray(inputs["L1_ffn_kernel"], np.float32),
        "fbi": inputs["L1_ffn_bias"],
        "fa": np.ascontiguousarray(inputs["L1_ffn_a"], np.float32),
        "fb": np.ascontiguousarray(inputs["L1_ffn_b"], np.float32),
        "trilc": tril,
    }
    for c in range(8):
        m = dict(base)
        m["x1"] = np.ascontiguousarray(x1_shards[c], np.float32)
        maps.append(m)
    return maps


def l2_inmaps(inputs, x2_shards):
    H, DK = 16, 64
    wqkv = np.asarray(inputs["L2_attn_w_qkv"], np.float32)
    perm = []
    for c0 in (0, 64, 128):
        for h in range(H):
            perm.extend(range(h * 192 + c0, h * 192 + c0 + 64))
    wqkvT = np.ascontiguousarray(wqkv[perm].T)
    keys = np.asarray(inputs["L2_ffn_keys"], np.float32)
    keysfm = np.ascontiguousarray(
        np.stack([keys[oc % 8, :, oc // 8, :].T for oc in range(16)], 0))
    bd = np.full((128, 128), NEG, np.float32)
    for t in range(16):
        bd[t * 8:(t + 1) * 8, t * 8:(t + 1) * 8] = 0.0
    base = {
        "wqkvT": wqkvT,
        "lnw": inputs["L2_attn_ln_w"], "lnb": inputs["L2_attn_ln_b"],
        "woutT": np.ascontiguousarray(np.asarray(inputs["L2_attn_w_out"], np.float32).T),
        "gamma": inputs["L2_ffn_gamma"], "beta": inputs["L2_ffn_beta"],
        "wqT": np.ascontiguousarray(np.asarray(inputs["L2_ffn_w_q"], np.float32).T),
        "keysfm": keysfm,
        "key_in": np.ascontiguousarray(inputs["L2_ffn_key_in"], np.float32),
        "key_out": np.ascontiguousarray(inputs["L2_ffn_key_out"], np.float32),
        "bdmask": bd,
    }
    maps = []
    for c in range(8):
        m = dict(base)
        m["xsh"] = np.ascontiguousarray(x2_shards[c], np.float32)
        maps.append(m)
    return maps


def l3_inmaps(inputs, x3_shards):
    wbfs = np.concatenate([np.asarray(inputs["L3_attn_Wb"], np.float32),
                           np.asarray(inputs["L3_attn_Wfd"], np.float32),
                           np.asarray(inputs["L3_attn_Wsd"], np.float32)], 0)
    bfs = np.concatenate([np.zeros(16, np.float32),
                          np.asarray(inputs["L3_attn_bfd"], np.float32),
                          np.asarray(inputs["L3_attn_bsd"], np.float32)], 0)
    base = {
        "wqT": np.ascontiguousarray(np.asarray(inputs["L3_attn_Wq"], np.float32).T),
        "wkT": np.ascontiguousarray(np.asarray(inputs["L3_attn_Wk"], np.float32).T),
        "wvT": np.ascontiguousarray(np.asarray(inputs["L3_attn_Wv"], np.float32).T),
        "wgT": np.ascontiguousarray(np.asarray(inputs["L3_attn_Wg"], np.float32).T),
        "woT": np.ascontiguousarray(np.asarray(inputs["L3_attn_Wo"], np.float32).T),
        "wbfsT": np.ascontiguousarray(wbfs.T),
        "bfs": bfs,
        "gatewT": np.ascontiguousarray(np.asarray(inputs["L3_ffn_gate_w"], np.float32).T),
        "gateb": inputs["L3_ffn_gate_bias"],
        "wgate": np.ascontiguousarray(np.asarray(inputs["L3_ffn_w_gate"], np.float32).transpose(0, 2, 1)),
        "wup": np.ascontiguousarray(np.asarray(inputs["L3_ffn_w_up"], np.float32).transpose(0, 2, 1)),
        "wdownT": np.ascontiguousarray(np.asarray(inputs["L3_ffn_w_down"], np.float32).transpose(0, 2, 1)),
        "shgT": np.ascontiguousarray(np.asarray(inputs["L3_ffn_sh_gate"], np.float32).T),
        "shuT": np.ascontiguousarray(np.asarray(inputs["L3_ffn_sh_up"], np.float32).T),
        "shdT": np.ascontiguousarray(np.asarray(inputs["L3_ffn_sh_down"], np.float32).T),
    }
    maps = []
    for c in range(8):
        m = dict(base)
        m["x3"] = np.ascontiguousarray(x3_shards[c], np.float32)
        maps.append(m)
    return maps


def kernel(**inputs):
    nc0 = bass.Bass("TRN2", target_bir_lowering=False)
    build_l0(nc0)
    res0 = run_bass_kernel_spmd(nc0, l0_inmaps(inputs), core_ids=list(range(8)))
    x1 = [res0.results[c]["x1"] for c in range(8)]

    nc1 = bass.Bass("TRN2", target_bir_lowering=False)
    build_l1(nc1)
    res1 = run_bass_kernel_spmd(nc1, l1_inmaps(inputs, x1), core_ids=list(range(8)))
    x2 = np.stack([res1.results[c]["x2"] for c in range(8)], axis=0)

    shards = [np.ascontiguousarray(x2[:, c * 256:(c + 1) * 256, :].transpose(1, 0, 2)
                                   .reshape(2048, 1024)) for c in range(8)]
    nc2 = bass.Bass("TRN2", target_bir_lowering=False)
    build_l2(nc2)
    res2 = run_bass_kernel_spmd(nc2, l2_inmaps(inputs, shards), core_ids=list(range(8)))
    x3 = np.stack([res2.results[c]["x3sh"].reshape(256, 8, 1024).transpose(1, 0, 2)
                   for c in range(8)], axis=1)
    x3 = np.ascontiguousarray(x3.reshape(8, 2048, 1024))

    nc3 = bass.Bass("TRN2", target_bir_lowering=False)
    build_l3(nc3)
    res3 = run_bass_kernel_spmd(nc3, l3_inmaps(inputs, [x3[c] for c in range(8)]),
                                core_ids=list(range(8)))
    x4 = np.stack([res3.results[c]["x4"] for c in range(8)], axis=0)
    return x4



def build_l1(nc):
    H, N64, DI, PROJ = 16, 64, 2048, 4256
    NDI = DI // 128
    NB_P = [512] * 8 + [160]
    NB_OFF = [0, 512, 1024, 1536, 2048, 2560, 3072, 3584, 4096]

    x1 = nc.dram_tensor("x1", [T, D], F32, kind="ExternalInput").ap()
    wprojW = nc.dram_tensor("wprojW", [D, PROJ], F32, kind="ExternalInput").ap()
    nsc = nc.dram_tensor("nsc", [D], F32, kind="ExternalInput").ap()
    nbi = nc.dram_tensor("nbi", [D], F32, kind="ExternalInput").ap()
    alog = nc.dram_tensor("alog", [16], F32, kind="ExternalInput").ap()
    dtb = nc.dram_tensor("dtb", [16], F32, kind="ExternalInput").ap()
    bcB = nc.dram_tensor("bcB", [64], F32, kind="ExternalInput").ap()
    bcC = nc.dram_tensor("bcC", [64], F32, kind="ExternalInput").ap()
    Bbi = nc.dram_tensor("Bbi", [64], F32, kind="ExternalInput").ap()
    Cbi = nc.dram_tensor("Cbi", [64], F32, kind="ExternalInput").ap()
    thw = nc.dram_tensor("thw", [D, 32], F32, kind="ExternalInput").ap()
    Dsk = nc.dram_tensor("Dsk", [16], F32, kind="ExternalInput").ap()
    wout = nc.dram_tensor("wout", [DI, D], F32, kind="ExternalInput").ap()
    fker = nc.dram_tensor("fker", [D, D], F32, kind="ExternalInput").ap()
    fbi = nc.dram_tensor("fbi", [D], F32, kind="ExternalInput").ap()
    fa = nc.dram_tensor("fa", [D, 16], F32, kind="ExternalInput").ap()
    fb = nc.dram_tensor("fb", [16, D], F32, kind="ExternalInput").ap()
    trilc = nc.dram_tensor("trilc", [128, 128], F32, kind="ExternalInput").ap()

    x2 = nc.dram_tensor("x2", [T, D], F32, kind="ExternalOutput").ap()

    xn_d = nc.dram_tensor("xn_d", [T, D], F32, kind="Internal").ap()
    lnbf_d = nc.dram_tensor("lnbf_d", [T, D], BF16, kind="Internal").ap()
    xin_d = nc.dram_tensor("xin_d", [T, DI], BF16, kind="Internal").ap()
    zsil_d = nc.dram_tensor("zsil_d", [T, DI], BF16, kind="Internal").ap()
    gam_d = nc.dram_tensor("gam_d", [T + 128, 16], F32, kind="Internal").ap()
    lc_cm_d = nc.dram_tensor("lc_cm_d", [NT, H, 128], F32, kind="Internal").ap()
    yz_d = nc.dram_tensor("yz_d", [T, DI], BF16, kind="Internal").ap()
    yzfm_d = nc.dram_tensor("yzfm_d", [NDI, 128, T], BF16, kind="Internal").ap()
    st_d = nc.dram_tensor("st_d", [T, D], F32, kind="Internal").ap()

    p = Prog(nc)
    es = contextlib.ExitStack()
    lnfm = es.enter_context(nc.sbuf_tensor("lnfm", [128, ND * T], BF16))
    wslot = es.enter_context(nc.sbuf_tensor("wslot", [128, 2 * 512], BF16))
    gchunk = es.enter_context(nc.sbuf_tensor("gchunk", [128, ND], F32))
    bchunk = es.enter_context(nc.sbuf_tensor("bchunk", [128, ND], BF16))
    brow1 = es.enter_context(nc.sbuf_tensor("brow1", [1, PROJ], F32))
    brow1b = es.enter_context(nc.sbuf_tensor("brow1b", [1, PROJ], BF16))
    onesrow = es.enter_context(nc.sbuf_tensor("onesrow", [1, 512], BF16))
    onesf = es.enter_context(nc.sbuf_tensor("onesf", [1, 128], F32))
    epst = es.enter_context(nc.sbuf_tensor("epst", [128, 4], F32))
    magicc = es.enter_context(nc.sbuf_tensor("magicc", [128, 2], F32))
    stat = es.enter_context(nc.sbuf_tensor("stat", [128, 8], F32))
    xt = es.enter_context(nc.sbuf_tensor("xt", [128, D], F32))
    xnt = es.enter_context(nc.sbuf_tensor("xnt", [128, D], F32))
    sqt = es.enter_context(nc.sbuf_tensor("sqt", [128, D], F32))
    bft = es.enter_context(nc.sbuf_tensor("bft", [128, D], BF16))
    bcBt = es.enter_context(nc.sbuf_tensor("bcBt", [128, 64], F32))
    bcCt = es.enter_context(nc.sbuf_tensor("bcCt", [128, 64], F32))
    Bbit = es.enter_context(nc.sbuf_tensor("Bbit", [128, 64], F32))
    Cbit = es.enter_context(nc.sbuf_tensor("Cbit", [128, 64], F32))
    Abct = es.enter_context(nc.sbuf_tensor("Abct", [128, 16], F32))
    Dbct = es.enter_context(nc.sbuf_tensor("Dbct", [128, 16], F32))
    srow = es.enter_context(nc.sbuf_tensor("srow", [1, 128], F32))
    Btok = es.enter_context(nc.sbuf_tensor("Btok", [128, NT * 64], F32))
    Ctok = es.enter_context(nc.sbuf_tensor("Ctok", [128, NT * 64], F32))
    dttok = es.enter_context(nc.sbuf_tensor("dttok", [128, NT * 16], F32))
    lamtok = es.enter_context(nc.sbuf_tensor("lamtok", [128, NT * 16], F32))
    ctok = es.enter_context(nc.sbuf_tensor("ctok", [128, NT * 16], F32))
    atok = es.enter_context(nc.sbuf_tensor("atok", [128, NT * 16], F32))
    ftok = es.enter_context(nc.sbuf_tensor("ftok", [128, NT * 16], F32))
    lctok = es.enter_context(nc.sbuf_tensor("lctok", [128, NT * 16], F32))
    gdiag = es.enter_context(nc.sbuf_tensor("gdiag", [128, NT], F32))
    coefc = es.enter_context(nc.sbuf_tensor("coefc", [128, NT * 16], F32))
    Bfm = es.enter_context(nc.sbuf_tensor("Bfm", [64, T], BF16))
    Cfm = es.enter_context(nc.sbuf_tensor("Cfm", [64, T], BF16))
    logafm = es.enter_context(nc.sbuf_tensor("logafm", [16, T], F32))
    phfm = es.enter_context(nc.sbuf_tensor("phfm", [32, T], F32))
    thetat = es.enter_context(nc.sbuf_tensor("thetat", [128, 32], F32))
    phtok = es.enter_context(nc.sbuf_tensor("phtok", [128, 32], F32))
    trig = es.enter_context(nc.sbuf_tensor("trig", [128, 128], F32))
    wth = es.enter_context(nc.sbuf_tensor("wth", [128, ND * 32], BF16))
    idnb = es.enter_context(nc.sbuf_tensor("idnb", [128, 128], BF16))
    idnf = es.enter_context(nc.sbuf_tensor("idnf", [128, 128], F32))
    lcrow = es.enter_context(nc.sbuf_tensor("lcrow", [1, H * 128], F32))
    Xc = es.enter_context(nc.sbuf_tensor("Xc", [128, DI], BF16))
    Zc = es.enter_context(nc.sbuf_tensor("Zc", [128, DI], BF16))
    Yc = es.enter_context(nc.sbuf_tensor("Yc", [128, DI], F32))
    G0m = es.enter_context(nc.sbuf_tensor("G0m", [128, 128], F32))
    Mt = es.enter_context(nc.sbuf_tensor("Mt", [128, 128], BF16))
    Bw = es.enter_context(nc.sbuf_tensor("Bw", [128, 64], BF16))
    Sst = es.enter_context(nc.sbuf_tensor("Sst", [64, H * 128], F32))
    Sbf = es.enter_context(nc.sbuf_tensor("Sbf", [64, 128], BF16))
    ecol = es.enter_context(nc.sbuf_tensor("ecol", [64, 1], F32))
    lse = es.enter_context(nc.sbuf_tensor("lse", [128, 4], F32))
    scr = es.enter_context(nc.sbuf_tensor("scr", [128, 160], F32))
    tril = es.enter_context(nc.sbuf_tensor("tril", [128, 128], F32))
    zscan = es.enter_context(nc.sbuf_tensor("zscan", [32, 512], F32))
    wfqb = es.enter_context(nc.sbuf_tensor("wfqb", [128, ND * D], BF16))
    fabf = es.enter_context(nc.sbuf_tensor("fabf", [128, ND * 16], BF16))
    fbbf = es.enter_context(nc.sbuf_tensor("fbbf", [16, D], BF16))
    xafm = es.enter_context(nc.sbuf_tensor("xafm", [16, T], BF16))
    frow = es.enter_context(nc.sbuf_tensor("frow", [1, D], F32))
    frowb = es.enter_context(nc.sbuf_tensor("frowb", [1, D], BF16))
    st_ = es.enter_context(nc.sbuf_tensor("st_", [128, D], F32))
    gt_ = es.enter_context(nc.sbuf_tensor("gt_", [128, D], F32))
    yzg = es.enter_context(nc.sbuf_tensor("yzg", [128, 4 * NDI * 128], BF16))
    ps0 = es.enter_context(nc.psum_tensor("ps0", [128, 512], F32))
    ps1 = es.enter_context(nc.psum_tensor("ps1", [128, 512], F32))
    ps2 = es.enter_context(nc.psum_tensor("ps2", [128, 512], F32))
    ps3 = es.enter_context(nc.psum_tensor("ps3", [128, 512], F32))
    psT = es.enter_context(nc.psum_tensor("psT", [128, 128], BF16))
    psF = es.enter_context(nc.psum_tensor("psF", [128, 128], F32))
    psY = es.enter_context(nc.psum_tensor("psY", [128, 128], F32))
    psS = es.enter_context(nc.psum_tensor("psS", [64, 128], F32))
    PSA = [ps0, ps1, ps2, ps3]
    if True:
        p.dma("sync", lambda e: e.dma_start(out=tril[:], in_=trilc[:]))
        p.dma("sync", lambda e: e.dma_start(out=gchunk.rearrange("p (k o) -> p k o", o=1),
                                            in_=nsc.rearrange("(k p o) -> p k o", p=128, o=1),
                                            allow_slow_non_contiguous=True))
        p.dma("gpsimd", lambda e: e.dma_start(out=bchunk.rearrange("p (k o) -> p k o", o=1),
                                              in_=nbi.rearrange("(k p o) -> p k o", p=128, o=1),
                                              allow_slow_non_contiguous=True))
        p.dma("gpsimd", lambda e: e.dma_start(out=wth.rearrange("p (k n) -> p k n", k=ND),
                                              in_=thw.rearrange("(k p) n -> p k n", p=128)))
        p.dma("gpsimd", lambda e: e.dma_start(out=fabf.rearrange("p (k n) -> p k n", k=ND),
                                              in_=fa.rearrange("(k p) n -> p k n", p=128)))
        p.dma("gpsimd", lambda e: e.dma_start(out=fbbf[:], in_=fb[:]))
        p.dma("sync", lambda e: e.dma_start(out=frow[0:1, :], in_=fbi.rearrange("(o d) -> o d", o=1)))
        p.barrier()
        p.op("vector", lambda e: e.memset(onesrow[:], 1.0))
        p.op("gpsimd", lambda e: e.memset(onesf[:], 1.0))
        p.op("scalar", lambda e: e.memzero(brow1[:]))
        p.barrier()
        p.op("vector", lambda e: e.memset(epst[:, 0:1], 1e-5))
        p.op("gpsimd", lambda e: e.memset(idnb[:], 1.0))
        p.barrier()
        p.op("vector", lambda e: e.memset(epst[:, 1:2], 1e-6))
        p.op("gpsimd", lambda e: e.affine_select(idnb[:], idnb[:], [[-1, 128]], ALU.is_equal,
                                                 0.0, base=0, channel_multiplier=1))
        p.barrier()
        p.op("vector", lambda e: e.memset(epst[:, 2:3], 1.0))
        p.barrier()
        p.op("vector", lambda e: e.memset(epst[:, 3:4], 0.0))
        p.op("scalar", lambda e: e.memzero(zscan[:]))
        p.barrier()
        p.op("vector", lambda e: e.tensor_scalar(tril[:], tril[:], 1.0, 1.0e30,
                                                 ALU.subtract, ALU.mult))
        p.barrier()
        p.op("vector", lambda e: e.memset(magicc[:, 0:1], 1.0))
        p.barrier()
        p.op("vector", lambda e: e.tensor_scalar(magicc[:, 1:2], magicc[:, 0:1],
                                                 9.313225746154785e-10, 1.0, ALU.add, ALU.subtract))
        p.barrier()
        p.op("vector", lambda e: e.tensor_scalar(magicc[:, 1:2], magicc[:, 1:2], 1073741824.0,
                                                 None, ALU.mult))
        p.barrier()
        p.op("vector", lambda e: e.tensor_scalar(magicc[:, 0:1], magicc[:, 1:2],
                                                 6755399441055744.0, 12582912.0, ALU.mult, ALU.add))
        p.barrier()

        p.op("gpsimd", lambda e: e.tensor_copy(idnf[:], idnb[:]))
        p.barrier()

        def bc_row(dst, srcdram, n):
            p.dma("sync", (lambda srcdram, n: lambda e: e.dma_start(
                out=srow[0:1, 0:n], in_=srcdram.rearrange("(o d) -> o d", o=1)))(srcdram, n))
            p.barrier()
            p.op("tensor", (lambda n: lambda e: e.matmul(
                psF[:, 0:n], onesf[0:1, 0:128], srow[0:1, 0:n], start=True, stop=True))(n))
            p.barrier()
            p.op("vector", (lambda dst, n: lambda e: e.tensor_copy(dst[:, 0:n], psF[:, 0:n]))(dst, n))
            p.barrier()

        bc_row(bcBt, bcB, 64); bc_row(bcCt, bcC, 64); bc_row(Bbit, Bbi, 64); bc_row(Cbit, Cbi, 64)
        bc_row(Dbct, Dsk, 16)
        p.dma("sync", lambda e: e.dma_start(out=srow[0:1, 0:16], in_=alog.rearrange("(o d) -> o d", o=1)))
        p.barrier()
        p.op("scalar", lambda e: e.activation(srow[0:1, 16:32], srow[0:1, 0:16], AF.Exp))
        p.barrier()
        p.op("vector", lambda e: e.tensor_scalar_mul(srow[0:1, 16:32], srow[0:1, 16:32], -1.0))
        p.barrier()
        p.op("tensor", lambda e: e.matmul(psF[:, 0:16], onesf[0:1, 0:128], srow[0:1, 16:32],
                                          start=True, stop=True))
        p.barrier()
        p.op("vector", lambda e: e.tensor_copy(Abct[:], psF[:, 0:16]))
        p.barrier()

        for tt in range(NT):
            sl = slice(tt * 128, tt * 128 + 128)
            p.dma("sync", (lambda sl: lambda e: e.dma_start(out=xt[:], in_=x1[sl, :]))(sl))
            p.barrier()
            emit_ln(p, xt, xnt, sqt, stat, epst[:, 0:1])
            p.op("vector", lambda e: e.tensor_copy(bft[:], xnt[:]))
            p.barrier()
            p.dma("sync", (lambda sl: lambda e: e.dma_start(out=xn_d[sl, :], in_=xnt[:]))(sl))
            p.dma("sync", (lambda sl: lambda e: e.dma_start(out=lnbf_d[sl, :], in_=bft[:]))(sl))
            p.barrier()
        for kc in range(ND):
            p.dma("sync", (lambda kc: lambda e: e.dma_start(
                out=lnfm[:, kc * T:(kc + 1) * T], in_=lnbf_d[:, kc * 128:(kc + 1) * 128],
                transpose=True))(kc))
        p.barrier()

        for bi_ in range(9):
            nb0, nbw = NB_OFF[bi_], NB_P[bi_]
            for kc in range(ND):
                p.dma("gpsimd", (lambda kc, nb0, nbw: lambda e: e.dma_start(
                    out=wslot[:, 0:nbw], in_=wprojW[kc * 128:(kc + 1) * 128, nb0:nb0 + nbw]))(kc, nb0, nbw))
                p.barrier()
                p.op("tensor", (lambda kc, nbw: lambda e: e.matmul(
                    ps0[0:1, 0:nbw], bchunk[:, kc:kc + 1], wslot[:, 0:nbw],
                    start=(kc == 0), stop=(kc == ND - 1)))(kc, nbw))
                p.barrier()
            p.op("vector", (lambda nb0, nbw: lambda e: e.tensor_copy(
                brow1[0:1, nb0:nb0 + nbw], ps0[0:1, 0:nbw]))(nb0, nbw))
            p.barrier()
        p.dma("sync", lambda e: e.dma_start(out=srow[0:1, 0:16], in_=dtb.rearrange("(o d) -> o d", o=1)))
        p.barrier()
        p.op("vector", lambda e: e.tensor_add(brow1[0:1, 4224:4240], brow1[0:1, 4224:4240],
                                              srow[0:1, 0:16]))
        p.barrier()
        p.op("vector", lambda e: e.tensor_copy(brow1b[:], brow1[:]))
        p.barrier()

        for grp in range(4):
            tts = list(range(grp * 4, grp * 4 + 4))
            for bi_ in range(9):
                nb0, nbw = NB_OFF[bi_], NB_P[bi_]
                for kc in range(ND):
                    sslot = (kc % 2) * 512
                    p.dma("gpsimd", (lambda kc, nb0, nbw, sslot: lambda e: e.dma_start(
                        out=wslot[:, sslot:sslot + nbw],
                        in_=wprojW[kc * 128:(kc + 1) * 128, nb0:nb0 + nbw]))(kc, nb0, nbw, sslot))
                    p.barrier()
                    p.op("vector", (lambda kc, nbw, sslot: lambda e: e.tensor_scalar(
                        wslot[:, sslot:sslot + nbw], wslot[:, sslot:sslot + nbw],
                        gchunk[:, kc:kc + 1], None, ALU.mult))(kc, nbw, sslot))
                    p.barrier()
                    for ti, tt in enumerate(tts):
                        p.op("tensor", (lambda kc, nbw, ti, tt, sslot: lambda e: e.matmul(
                            PSA[ti][:, 0:nbw],
                            lnfm[:, kc * T + tt * 128: kc * T + tt * 128 + 128],
                            wslot[:, sslot:sslot + nbw],
                            start=(kc == 0), stop=False))(kc, nbw, ti, tt, sslot))
                    p.barrier()
                for ti in range(4):
                    p.op("tensor", (lambda nbw, nb0, ti: lambda e: e.matmul(
                        PSA[ti][:, 0:nbw], onesrow[0:1, 0:128],
                        brow1b[0:1, nb0:nb0 + nbw], start=False, stop=True))(nbw, nb0, ti))
                p.barrier()
                for ti, tt in enumerate(tts):
                    sl = slice(tt * 128, tt * 128 + 128)
                    if bi_ < 4:
                        p.op("scalar", (lambda ti: lambda e: e.activation(
                            Xc[:, 0:512], PSA[ti][:, 0:512], AF.Silu))(ti))
                        p.barrier()
                        p.dma("sync", (lambda sl, bi_: lambda e: e.dma_start(
                            out=xin_d[sl, bi_ * 512:bi_ * 512 + 512], in_=Xc[:, 0:512]))(sl, bi_))
                        p.barrier()
                    elif bi_ < 8:
                        p.op("scalar", (lambda ti: lambda e: e.activation(
                            Zc[:, 0:512], PSA[ti][:, 0:512], AF.Silu))(ti))
                        p.barrier()
                        p.dma("sync", (lambda sl, bi_: lambda e: e.dma_start(
                            out=zsil_d[sl, (bi_ - 4) * 512:(bi_ - 4) * 512 + 512],
                            in_=Zc[:, 0:512]))(sl, bi_))
                        p.barrier()
                    else:
                        p.op("vector", (lambda ti: lambda e: e.tensor_copy(
                            scr[:, 0:160], PSA[ti][:, 0:160]))(ti))
                        p.barrier()
                        for (c0, sct, bt_) in ((0, bcBt, Bbit), (64, bcCt, Cbit)):
                            p.op("scalar", (lambda c0: lambda e: e.activation(
                                trig[:, 0:64], scr[:, c0:c0 + 64], AF.Square,
                                accum_out=stat[:, 0:1]))(c0))
                            p.barrier()
                            p.op("scalar", lambda e: e.activation(stat[:, 1:2], stat[:, 0:1], AF.Ln,
                                                                  bias=epst[:, 1:2], scale=1.0 / 64))
                            p.barrier()
                            p.op("scalar", lambda e: e.activation(stat[:, 2:3], stat[:, 1:2],
                                                                  AF.Exp, scale=-0.5))
                            p.barrier()
                            p.op("vector", (lambda c0, sct: lambda e: e.tensor_scalar(
                                scr[:, c0:c0 + 64], scr[:, c0:c0 + 64], stat[:, 2:3], None,
                                ALU.mult))(c0, sct))
                            p.barrier()
                            p.op("vector", (lambda c0, sct: lambda e: e.tensor_mul(
                                scr[:, c0:c0 + 64], scr[:, c0:c0 + 64], sct[:, 0:64]))(c0, sct))
                            p.barrier()
                            p.op("vector", (lambda c0, bt_: lambda e: e.tensor_add(
                                scr[:, c0:c0 + 64], scr[:, c0:c0 + 64], bt_[:, 0:64]))(c0, bt_))
                            p.barrier()
                        p.op("vector", (lambda tt: lambda e: e.tensor_copy(
                            Btok[:, tt * 64:(tt + 1) * 64], scr[:, 0:64]))(tt))
                        p.op("gpsimd", (lambda tt: lambda e: e.tensor_copy(
                            Ctok[:, tt * 64:(tt + 1) * 64], scr[:, 64:128]))(tt))
                        p.barrier()
                        p.op("scalar", lambda e: e.activation(trig[:, 0:16], scr[:, 128:144], AF.Abs))
                        p.barrier()
                        p.op("scalar", lambda e: e.activation(trig[:, 16:32], trig[:, 0:16],
                                                              AF.Exp, scale=-1.0))
                        p.barrier()
                        p.op("scalar", lambda e: e.activation(trig[:, 32:48], trig[:, 16:32],
                                                              AF.Ln, bias=epst[:, 2:3]))
                        p.barrier()
                        p.op("scalar", lambda e: e.activation(trig[:, 48:64], scr[:, 128:144], AF.Relu))
                        p.barrier()
                        p.op("vector", (lambda tt: lambda e: e.tensor_add(
                            dttok[:, tt * 16:(tt + 1) * 16], trig[:, 32:48], trig[:, 48:64]))(tt))
                        p.barrier()
                        p.op("scalar", (lambda tt: lambda e: e.activation(
                            lamtok[:, tt * 16:(tt + 1) * 16], scr[:, 144:160], AF.Sigmoid))(tt))
                        p.barrier()
                        p.op("vector", (lambda tt: lambda e: e.tensor_mul(
                            atok[:, tt * 16:(tt + 1) * 16], dttok[:, tt * 16:(tt + 1) * 16],
                            lamtok[:, tt * 16:(tt + 1) * 16]))(tt))
                        p.barrier()
                        p.op("vector", (lambda tt: lambda e: e.tensor_sub(
                            trig[:, 0:16], dttok[:, tt * 16:(tt + 1) * 16],
                            atok[:, tt * 16:(tt + 1) * 16]))(tt))
                        p.barrier()
                        p.dma("sync", (lambda tt: lambda e: e.dma_start(
                            out=gam_d[tt * 128:(tt + 1) * 128, :], in_=trig[:, 0:16]))(tt))
                        p.barrier()
        p.op("vector", lambda e: e.memset(trig[:, 0:16], 0.0))
        p.barrier()
        p.dma("sync", lambda e: e.dma_start(out=gam_d[T:T + 128, :], in_=trig[:, 0:16]))
        p.barrier()
        for tt in range(NT):
            p.dma("sync", (lambda tt: lambda e: e.dma_start(
                out=trig[:, 0:16], in_=gam_d[tt * 128 + 1: tt * 128 + 129, :]))(tt))
            p.barrier()
            p.op("vector", (lambda tt: lambda e: e.tensor_add(
                ctok[:, tt * 16:(tt + 1) * 16], atok[:, tt * 16:(tt + 1) * 16], trig[:, 0:16]))(tt))
            p.barrier()

        for tt in range(NT):
            for kc in range(ND):
                p.op("tensor", (lambda tt, kc: lambda e: e.matmul(
                    ps0[:, 0:32], lnfm[:, kc * T + tt * 128: kc * T + tt * 128 + 128],
                    wth[:, kc * 32:(kc + 1) * 32], start=(kc == 0), stop=(kc == ND - 1)))(tt, kc))
            p.barrier()
            p.op("vector", lambda e: e.tensor_copy(thetat[:], ps0[:, 0:32]))
            p.barrier()
            p.op("vector", (lambda tt: lambda e: e.reduce_sum(
                stat[:, 0:1], dttok[:, tt * 16:(tt + 1) * 16], axis=AX.X))(tt))
            p.barrier()
            p.op("vector", lambda e: e.tensor_scalar_mul(stat[:, 0:1], stat[:, 0:1], 1.0 / 16))
            p.barrier()
            p.op("vector", lambda e: e.tensor_scalar(thetat[:], thetat[:], stat[:, 0:1], None, ALU.mult))
            p.barrier()
            p.op("tensor", lambda e: e.transpose(psF[0:32, 0:128], thetat[:], idnf[:]))
            p.barrier()
            p.op("vector", (lambda tt: lambda e: e.tensor_copy(
                phfm[:, tt * 128:(tt + 1) * 128], psF[0:32, 0:128]))(tt))
            p.barrier()
        p.op("vector", lambda e: e.memset(lse[0:32, 0:1], 0.0))
        p.barrier()
        for seg in range(4):
            p.op("vector", (lambda seg: lambda e: e.tensor_tensor_scan(
                Yc[0:32, 0:512], phfm[:, seg * 512:(seg + 1) * 512], zscan[:],
                lse[0:32, 0:1], ALU.add, ALU.add))(seg))
            p.barrier()
            p.op("vector", (lambda seg: lambda e: e.tensor_copy(
                phfm[:, seg * 512:(seg + 1) * 512], Yc[0:32, 0:512]))(seg))
            p.barrier()
            p.op("vector", lambda e: e.tensor_copy(lse[0:32, 0:1], Yc[0:32, 511:512]))
            p.barrier()
        TWO_PI = float(2 * np.pi)
        for tt in range(NT):
            p.op("tensor", (lambda tt: lambda e: e.transpose(
                psF[0:128, 0:32], phfm[:, tt * 128:(tt + 1) * 128], idnf[0:32, 0:32]))(tt))
            p.barrier()
            p.op("vector", lambda e: e.tensor_copy(phtok[:], psF[0:128, 0:32]))
            p.barrier()
            for (dst0, shift) in ((0, 0.0), (32, float(np.pi / 2))):
                p.op("vector", (lambda shift: lambda e: e.tensor_scalar(
                    trig[:, 64:96], phtok[:], 1.0 / TWO_PI, shift / TWO_PI, ALU.mult, ALU.add))(shift))
                p.barrier()
                p.op("vector", lambda e: e.tensor_scalar(trig[:, 64:96], trig[:, 64:96],
                                                         magicc[:, 0:1], magicc[:, 0:1],
                                                         ALU.add, ALU.subtract))
                p.barrier()
                p.op("vector", (lambda shift: lambda e: e.tensor_scalar(
                    trig[:, 96:128], trig[:, 64:96], TWO_PI, -shift, ALU.mult, ALU.add))(shift))
                p.barrier()
                p.op("vector", (lambda dst0: lambda e: e.tensor_sub(
                    trig[:, dst0:dst0 + 32], phtok[:], trig[:, 96:128]))(dst0))
                p.barrier()
                p.op("scalar", (lambda dst0: lambda e: e.activation(
                    trig[:, dst0:dst0 + 32], trig[:, dst0:dst0 + 32], AF.Sin))(dst0))
                p.barrier()
            sin0, cos0 = 0, 32
            for tok in (Btok, Ctok):
                p.op("vector", (lambda tok, tt: lambda e: e.tensor_mul(
                    trig[:, 64:96], tok[:, tt * 64:tt * 64 + 32], trig[:, cos0:cos0 + 32]))(tok, tt))
                p.barrier()
                p.op("vector", (lambda tok, tt: lambda e: e.tensor_mul(
                    trig[:, 96:128], tok[:, tt * 64 + 32:tt * 64 + 64], trig[:, sin0:sin0 + 32]))(tok, tt))
                p.barrier()
                p.op("vector", (lambda tok, tt: lambda e: e.tensor_mul(
                    scr[:, 0:32], tok[:, tt * 64:tt * 64 + 32], trig[:, sin0:sin0 + 32]))(tok, tt))
                p.barrier()
                p.op("vector", (lambda tok, tt: lambda e: e.tensor_mul(
                    scr[:, 32:64], tok[:, tt * 64 + 32:tt * 64 + 64], trig[:, cos0:cos0 + 32]))(tok, tt))
                p.barrier()
                p.op("vector", (lambda tok, tt: lambda e: e.tensor_sub(
                    tok[:, tt * 64:tt * 64 + 32], trig[:, 64:96], trig[:, 96:128]))(tok, tt))
                p.barrier()
                p.op("vector", (lambda tok, tt: lambda e: e.tensor_add(
                    tok[:, tt * 64 + 32:tt * 64 + 64], scr[:, 0:32], scr[:, 32:64]))(tok, tt))
                p.barrier()
            p.op("vector", (lambda tt: lambda e: e.tensor_mul(
                scr[:, 0:64], Btok[:, tt * 64:(tt + 1) * 64], Ctok[:, tt * 64:(tt + 1) * 64]))(tt))
            p.barrier()
            p.op("vector", (lambda tt: lambda e: e.reduce_sum(
                gdiag[:, tt:tt + 1], scr[:, 0:64], axis=AX.X))(tt))
            p.barrier()
            for (tok, fm) in ((Btok, Bfm), (Ctok, Cfm)):
                p.op("vector", (lambda tok, tt: lambda e: e.tensor_copy(
                    bft[:, 0:64], tok[:, tt * 64:(tt + 1) * 64]))(tok, tt))
                p.barrier()
                p.op("tensor", lambda e: e.transpose(psT[0:64, 0:128], bft[:, 0:64], idnb[:]))
                p.barrier()
                p.op("vector", (lambda fm, tt: lambda e: e.tensor_copy(
                    fm[:, tt * 128:(tt + 1) * 128], psT[0:64, 0:128]))(fm, tt))
                p.barrier()
            p.op("vector", (lambda tt: lambda e: e.tensor_mul(
                trig[:, 0:16], dttok[:, tt * 16:(tt + 1) * 16], Abct[:]))(tt))
            p.barrier()
            p.op("tensor", lambda e: e.transpose(psF[0:16, 0:128], trig[:, 0:16], idnf[:]))
            p.barrier()
            p.op("vector", (lambda tt: lambda e: e.tensor_copy(
                logafm[:, tt * 128:(tt + 1) * 128], psF[0:16, 0:128]))(tt))
            p.barrier()
        for cc in range(NT):
            p.op("vector", (lambda cc: lambda e: e.tensor_tensor_scan(
                Yc[0:16, 0:128], logafm[:, cc * 128:(cc + 1) * 128],
                zscan[0:16, 0:128], 0.0, ALU.add, ALU.add))(cc))
            p.barrier()
            p.op("vector", (lambda cc: lambda e: e.tensor_copy(
                logafm[:, cc * 128:(cc + 1) * 128], Yc[0:16, 0:128]))(cc))
            p.barrier()
            p.dma("sync", (lambda cc: lambda e: e.dma_start(
                out=lc_cm_d[cc, :, :], in_=logafm[:, cc * 128:(cc + 1) * 128]))(cc))
            p.barrier()
            p.op("tensor", (lambda cc: lambda e: e.transpose(
                psF[0:128, 0:16], logafm[:, cc * 128:(cc + 1) * 128], idnf[0:16, 0:16]))(cc))
            p.barrier()
            p.op("vector", (lambda cc: lambda e: e.tensor_copy(
                lctok[:, cc * 16:(cc + 1) * 16], psF[0:128, 0:16]))(cc))
            p.barrier()
        for tt in range(NT):
            p.op("scalar", (lambda tt: lambda e: e.activation(
                ftok[:, tt * 16:(tt + 1) * 16], ctok[:, tt * 16:(tt + 1) * 16], AF.Ln))(tt))
            p.barrier()
            p.op("vector", (lambda tt: lambda e: e.tensor_sub(
                ftok[:, tt * 16:(tt + 1) * 16], ftok[:, tt * 16:(tt + 1) * 16],
                lctok[:, tt * 16:(tt + 1) * 16]))(tt))
            p.barrier()
            p.op("vector", (lambda tt: lambda e: e.tensor_scalar(
                coefc[:, tt * 16:(tt + 1) * 16], atok[:, tt * 16:(tt + 1) * 16],
                gdiag[:, tt:tt + 1], None, ALU.mult))(tt))
            p.barrier()
            p.op("vector", (lambda tt: lambda e: e.tensor_add(
                coefc[:, tt * 16:(tt + 1) * 16], coefc[:, tt * 16:(tt + 1) * 16], Dbct[:]))(tt))
            p.barrier()

        p.op("vector", lambda e: e.memset(Sst[:], 0.0))
        p.barrier()
        for cc in range(NT):
            sl = slice(cc * 128, cc * 128 + 128)
            p.dma("sync", (lambda sl: lambda e: e.dma_start(out=Xc[:], in_=xin_d[sl, :]))(sl))
            p.dma("sync", (lambda sl: lambda e: e.dma_start(out=Zc[:], in_=zsil_d[sl, :]))(sl))
            p.dma("sync", (lambda cc: lambda e: e.dma_start(
                out=lcrow.rearrange("o (h t) -> o h t", h=H),
                in_=lc_cm_d[cc, :, :].rearrange("(o h) t -> o h t", o=1)))(cc))
            p.barrier()
            p.op("tensor", (lambda cc: lambda e: e.matmul(
                psF[:, 0:128], Bfm[:, cc * 128:(cc + 1) * 128], Cfm[:, cc * 128:(cc + 1) * 128],
                start=True, stop=True))(cc))
            p.barrier()
            p.op("vector", lambda e: e.tensor_copy(G0m[:], psF[:, 0:128]))
            p.barrier()
            for h in range(H):
                ic = cc * 16 + h
                p.op("tensor", (lambda h: lambda e: e.matmul(
                    psF[:, 0:128], onesf[0:1, 0:128], lcrow[0:1, h * 128:(h + 1) * 128],
                    start=True, stop=True))(h))
                p.barrier()
                p.op("vector", (lambda ic: lambda e: e.tensor_scalar(
                    sqt[:, 0:128], psF[:, 0:128], ftok[:, ic:ic + 1], None, ALU.add))(ic))
                p.barrier()
                p.op("vector", lambda e: e.tensor_add(sqt[:, 0:128], sqt[:, 0:128], tril[:]))
                p.barrier()
                p.op("scalar", lambda e: e.activation(sqt[:, 128:256], sqt[:, 0:128], AF.Exp))
                p.barrier()
                p.op("vector", lambda e: e.tensor_mul(sqt[:, 128:256], sqt[:, 128:256], G0m[:]))
                p.barrier()
                p.op("vector", lambda e: e.tensor_copy(Mt[:], sqt[:, 128:256]))
                p.barrier()
                p.op("tensor", (lambda h: lambda e: e.matmul(
                    psY[:, 0:128], Mt[:], Xc[:, h * 128:(h + 1) * 128],
                    start=True, stop=True))(h))
                p.barrier()
                p.op("vector", (lambda h: lambda e: e.tensor_copy(
                    Sbf[:], Sst[:, h * 128:(h + 1) * 128]))(h))
                p.barrier()
                p.op("tensor", (lambda cc: lambda e: e.matmul(
                    psS[0:64, 0:128] if False else psF[:, 0:128],
                    Cfm[:, cc * 128:(cc + 1) * 128], Sbf[:],
                    start=True, stop=True))(cc))
                p.barrier()
                p.op("vector", (lambda ic, h: lambda e: e.scalar_tensor_tensor(
                    Yc[:, h * 128:(h + 1) * 128], Xc[:, h * 128:(h + 1) * 128],
                    coefc[:, ic:ic + 1], psY[:, 0:128], ALU.mult, ALU.add))(ic, h))
                p.barrier()
                p.op("scalar", (lambda ic: lambda e: e.activation(
                    lse[:, 0:1], lctok[:, ic:ic + 1], AF.Exp))(ic))
                p.barrier()
                p.op("vector", (lambda: lambda e: e.tensor_scalar(
                    sqt[:, 0:128], psF[:, 0:128], lse[:, 0:1], None, ALU.mult))())
                p.barrier()
                p.op("vector", (lambda h: lambda e: e.tensor_add(
                    Yc[:, h * 128:(h + 1) * 128], Yc[:, h * 128:(h + 1) * 128],
                    sqt[:, 0:128]))(h))
                p.barrier()
                p.op("tensor", (lambda h: lambda e: e.matmul(
                    psF[:, 0:1], onesf[0:1, 0:128], lcrow[0:1, h * 128 + 127:h * 128 + 128],
                    start=True, stop=True))(h))
                p.barrier()
                p.op("vector", lambda e: e.tensor_copy(lse[:, 1:2], psF[:, 0:1]))
                p.barrier()
                p.op("scalar", lambda e: e.activation(lse[:, 2:3], lse[:, 1:2], AF.Exp))
                p.barrier()
                p.op("vector", (lambda ic: lambda e: e.tensor_add(
                    lse[:, 3:4], ftok[:, ic:ic + 1], lse[:, 1:2]))(ic))
                p.barrier()
                p.op("scalar", lambda e: e.activation(lse[:, 3:4], lse[:, 3:4], AF.Exp))
                p.barrier()
                p.op("vector", (lambda cc: lambda e: e.tensor_scalar(
                    Bw[:, 0:64], Btok[:, cc * 64:(cc + 1) * 64], lse[:, 3:4], None,
                    ALU.mult))(cc))
                p.barrier()
                p.op("tensor", (lambda h: lambda e: e.matmul(
                    psS[0:64, 0:128], Bw[:, 0:64], Xc[:, h * 128:(h + 1) * 128],
                    start=True, stop=True))(h))
                p.barrier()
                p.op("vector", (lambda h: lambda e: e.scalar_tensor_tensor(
                    Sst[:, h * 128:(h + 1) * 128], Sst[:, h * 128:(h + 1) * 128],
                    lse[0:64, 2:3], psS[0:64, 0:128], ALU.mult, ALU.add))(h))
                p.barrier()
            p.op("vector", lambda e: e.tensor_mul(Yc[:], Yc[:], Zc[:]))
            p.barrier()
            p.op("vector", lambda e: e.tensor_copy(Xc[:], Yc[:]))
            p.barrier()
            p.dma("sync", (lambda sl: lambda e: e.dma_start(out=yz_d[sl, :], in_=Xc[:]))(sl))
            p.barrier()
        for kc in range(NDI):
            p.dma("sync", (lambda kc: lambda e: e.dma_start(
                out=Xc[:], in_=yz_d[:, kc * 128:(kc + 1) * 128], transpose=True))(kc))
            p.barrier()
            p.dma("sync", (lambda kc: lambda e: e.dma_start(out=yzfm_d[kc, :, :], in_=Xc[:]))(kc))
            p.barrier()

        for kc in range(ND):
            p.dma("sync", (lambda kc: lambda e: e.dma_start(
                out=sqt[:], in_=fker[kc * 128:(kc + 1) * 128, :]))(kc))
            p.barrier()
            p.op("vector", (lambda kc: lambda e: e.reduce_max(
                stat[:, 5:6] if kc else stat[:, 4:5],
                sqt[:], axis=AX.X, apply_absolute_value=True))(kc))
            p.barrier()
            if kc:
                p.op("vector", lambda e: e.tensor_max(stat[:, 4:5], stat[:, 4:5], stat[:, 5:6]))
                p.barrier()
        p.op("tensor", lambda e: e.transpose(psF[0:1, 0:128], stat[:, 4:5], idnf[:]))
        p.barrier()
        p.op("vector", lambda e: e.reduce_max(stat[0:1, 6:7], psF[0:1, 0:128], axis=AX.X))
        p.barrier()
        p.op("tensor", lambda e: e.matmul(psF[:, 0:1], onesf[0:1, 0:128], stat[0:1, 6:7],
                                          start=True, stop=True))
        p.barrier()
        p.op("vector", lambda e: e.tensor_scalar_mul(stat[:, 4:5], psF[:, 0:1], 1.0 / 7.0))
        p.barrier()
        p.op("vector", lambda e: e.reciprocal(stat[:, 5:6], stat[:, 4:5]))
        p.barrier()
        for kc in range(ND):
            p.dma("sync", (lambda kc: lambda e: e.dma_start(
                out=sqt[:], in_=fker[kc * 128:(kc + 1) * 128, :]))(kc))
            p.barrier()
            p.op("vector", lambda e: e.tensor_scalar(sqt[:], sqt[:], stat[:, 5:6], None, ALU.mult))
            p.barrier()
            p.op("vector", lambda e: e.tensor_scalar(sqt[:], sqt[:], magicc[:, 0:1], magicc[:, 0:1],
                                                     ALU.add, ALU.subtract))
            p.barrier()
            p.op("vector", lambda e: e.tensor_scalar(sqt[:], sqt[:], -8.0, 7.0, ALU.max, ALU.min))
            p.barrier()
            p.op("vector", (lambda kc: lambda e: e.tensor_scalar(
                wfqb[:, kc * D:(kc + 1) * D], sqt[:], stat[:, 4:5], None, ALU.mult))(kc))
            p.barrier()
        p.op("vector", lambda e: e.tensor_copy(frowb[:], frow[:]))
        p.barrier()

        for tt in range(NT):
            for kc in range(ND):
                p.op("tensor", (lambda tt, kc: lambda e: e.matmul(
                    ps0[:, 0:16], lnfm[:, kc * T + tt * 128: kc * T + tt * 128 + 128],
                    fabf[:, kc * 16:(kc + 1) * 16], start=(kc == 0), stop=(kc == ND - 1)))(tt, kc))
            p.barrier()
            p.op("vector", lambda e: e.tensor_copy(bft[:, 0:16], ps0[:, 0:16]))
            p.barrier()
            p.op("tensor", lambda e: e.transpose(psT[0:16, 0:128], bft[:, 0:16], idnb[:]))
            p.barrier()
            p.op("vector", (lambda tt: lambda e: e.tensor_copy(
                xafm[:, tt * 128:(tt + 1) * 128], psT[0:16, 0:128]))(tt))
            p.barrier()

        for grp in range(4):
            tts = list(range(grp * 4, grp * 4 + 4))
            for ti, tt in enumerate(tts):
                for kc in range(NDI):
                    p.dma("sync", (lambda ti, tt, kc: lambda e: e.dma_start(
                        out=yzg[:, ti * NDI * 128 + kc * 128: ti * NDI * 128 + kc * 128 + 128],
                        in_=yzfm_d[kc, :, tt * 128:(tt + 1) * 128]))(ti, tt, kc))
            p.barrier()
            for nb in range(2):
                for kc in range(NDI):
                    sslot = (kc % 2) * 512
                    p.dma("gpsimd", (lambda kc, nb, sslot: lambda e: e.dma_start(
                        out=wslot[:, sslot:sslot + 512],
                        in_=wout[kc * 128:(kc + 1) * 128, nb * 512:nb * 512 + 512]))(kc, nb, sslot))
                    p.barrier()
                    for ti in range(4):
                        p.op("tensor", (lambda kc, ti, sslot: lambda e: e.matmul(
                            PSA[ti][:, 0:512],
                            yzg[:, ti * NDI * 128 + kc * 128: ti * NDI * 128 + kc * 128 + 128],
                            wslot[:, sslot:sslot + 512],
                            start=(kc == 0), stop=(kc == NDI - 1)))(kc, ti, sslot))
                    p.barrier()
                for ti in range(4):
                    p.op("scalar", (lambda ti, nb: lambda e: e.copy(
                        st_[:, nb * 512:nb * 512 + 512], PSA[ti][:, 0:512]))(ti, nb))
                    p.barrier()
                    tt_ = tts[ti]
                    p.dma("sync", (lambda tt_, nb: lambda e: e.dma_start(
                        out=st_d[tt_ * 128:(tt_ + 1) * 128, nb * 512:nb * 512 + 512],
                        in_=st_[:, nb * 512:nb * 512 + 512]))(tt_, nb))
                    p.barrier()
            for ti, tt in enumerate(tts):
                sl = slice(tt * 128, tt * 128 + 128)
                for nb in range(2):
                    ps = [ps0, ps1][nb]
                    for kc in range(ND):
                        p.op("tensor", (lambda tt, kc, nb, ps: lambda e: e.matmul(
                            ps[:, 0:512],
                            lnfm[:, kc * T + tt * 128: kc * T + tt * 128 + 128],
                            wfqb[:, kc * D + nb * 512: kc * D + nb * 512 + 512],
                            start=(kc == 0), stop=False))(tt, kc, nb, ps))
                    p.op("tensor", (lambda tt, nb, ps: lambda e: e.matmul(
                        ps[:, 0:512], xafm[:, tt * 128:(tt + 1) * 128],
                        fbbf[:, nb * 512:nb * 512 + 512], start=False, stop=False))(tt, nb, ps))
                    p.op("tensor", (lambda nb, ps: lambda e: e.matmul(
                        ps[:, 0:512], onesrow[0:1, 0:128],
                        frowb[0:1, nb * 512:nb * 512 + 512], start=False, stop=True))(nb, ps))
                    p.barrier()
                    p.op("scalar", (lambda nb, ps: lambda e: e.activation(
                        gt_[:, nb * 512:nb * 512 + 512], ps[:, 0:512], AF.Silu))(nb, ps))
                    p.barrier()
                p.dma("sync", (lambda sl: lambda e: e.dma_start(out=xnt[:], in_=xn_d[sl, :]))(sl))
                p.dma("sync", (lambda sl: lambda e: e.dma_start(out=xt[:], in_=x1[sl, :]))(sl))
                p.barrier()
                p.dma("sync", (lambda sl: lambda e: e.dma_start(out=st_[:], in_=st_d[sl, :]))(sl))
                p.barrier()
                p.op("vector", lambda e: e.tensor_add(st_[:], st_[:], xnt[:]))
                p.barrier()
                p.op("vector", lambda e: e.tensor_mul(st_[:], st_[:], gt_[:]))
                p.barrier()
                p.op("vector", lambda e: e.tensor_add(st_[:], st_[:], xt[:]))
                p.barrier()
                p.dma("sync", (lambda sl: lambda e: e.dma_start(out=x2[sl, :], in_=st_[:]))(sl))
                p.barrier()
        p.build()
    es.close()
    return nc

def build_l2(nc, n_ranks=8):
    H, DK = 16, 64
    PH, NK, KD, TOPK = 2, 256, 128, 8

    xsh = nc.dram_tensor("xsh", [T, D], F32, kind="ExternalInput").ap()
    wqkvT = nc.dram_tensor("wqkvT", [D, 3 * D], F32, kind="ExternalInput").ap()
    lnw = nc.dram_tensor("lnw", [D], F32, kind="ExternalInput").ap()
    lnb = nc.dram_tensor("lnb", [D], F32, kind="ExternalInput").ap()
    woutT = nc.dram_tensor("woutT", [D, D], F32, kind="ExternalInput").ap()
    gamma = nc.dram_tensor("gamma", [D], F32, kind="ExternalInput").ap()
    beta = nc.dram_tensor("beta", [D], F32, kind="ExternalInput").ap()
    wqT = nc.dram_tensor("wqT", [D, 2 * D], F32, kind="ExternalInput").ap()
    keysfm = nc.dram_tensor("keysfm", [16, KD, NK], F32, kind="ExternalInput").ap()
    key_in = nc.dram_tensor("key_in", [65536, D], F32, kind="ExternalInput").ap()
    key_out = nc.dram_tensor("key_out", [65536, D], F32, kind="ExternalInput").ap()
    bdmask = nc.dram_tensor("bdmask", [128, 128], F32, kind="ExternalInput").ap()

    x3sh = nc.dram_tensor("x3sh", [T, D], F32, kind="ExternalOutput").ap()

    xn_d = nc.dram_tensor("xn_d", [T, D], F32, kind="Internal").ap()
    lnbf_d = nc.dram_tensor("lnbf_d", [T, D], BF16, kind="Internal").ap()
    vtok_d = nc.dram_tensor("vtok_d", [T, D], BF16, kind="Internal").ap()
    attn_d = nc.dram_tensor("attn_d", [T, D], BF16, kind="Internal").ap()
    cc_in = nc.dram_tensor("cc_in", [1, 2 * D], F32, kind="Internal")
    cc_out = nc.dram_tensor("cc_out", [1, 2 * D], F32, kind="Internal")
    sg_d = nc.dram_tensor("sg_d", [T, D], F32, kind="Internal").ap()
    bnc_d = nc.dram_tensor("bnc_d", [8, D], F32, kind="Internal").ap()
    ln2bf_d = nc.dram_tensor("ln2bf_d", [T, D], BF16, kind="Internal").ap()
    xnfm_d = nc.dram_tensor("xnfm_d", [ND, 128, T], BF16, kind="Internal").ap()
    dbg_d = nc.dram_tensor("dbg_d", [128, 8192], F32, kind="Internal").ap()

    p = Prog(nc)
    es = contextlib.ExitStack()
    qfm = es.enter_context(nc.sbuf_tensor("qfm", [128, ND * T], BF16))
    kfm = es.enter_context(nc.sbuf_tensor("kfm", [128, ND * T], BF16))
    wstr = es.enter_context(nc.sbuf_tensor("wstr", [128, 2 * 1024], BF16))
    keyssb = es.enter_context(nc.sbuf_tensor("keyssb", [128, 16 * NK], BF16))
    gln = es.enter_context(nc.sbuf_tensor("gln", [128, ND], F32))
    bln = es.enter_context(nc.sbuf_tensor("bln", [128, ND], BF16))
    gbn = es.enter_context(nc.sbuf_tensor("gbn", [128, ND], F32))
    onesrow = es.enter_context(nc.sbuf_tensor("onesrow", [1, 512], BF16))
    onesf = es.enter_context(nc.sbuf_tensor("onesf", [128, 1], F32))
    epst = es.enter_context(nc.sbuf_tensor("epst", [128, 4], F32))
    stat = es.enter_context(nc.sbuf_tensor("stat", [128, 8], F32))
    browb = es.enter_context(nc.sbuf_tensor("browb", [1, 3 * D], BF16))
    brow2b = es.enter_context(nc.sbuf_tensor("brow2b", [1, 2 * D], BF16))
    statrow = es.enter_context(nc.sbuf_tensor("statrow", [1, 2 * D], F32))
    xt = es.enter_context(nc.sbuf_tensor("xt", [128, D], F32))
    xnt = es.enter_context(nc.sbuf_tensor("xnt", [128, D], F32))
    sqt = es.enter_context(nc.sbuf_tensor("sqt", [128, D], F32))
    bft = es.enter_context(nc.sbuf_tensor("bft", [128, D], BF16))
    xbft = es.enter_context(nc.sbuf_tensor("xbft", [128, D], BF16))
    mask2 = es.enter_context(nc.sbuf_tensor("mask2", [128, 128], F32))
    swide = es.enter_context(nc.sbuf_tensor("swide", [128, H * 128], F32))
    pwide = es.enter_context(nc.sbuf_tensor("pwide", [128, H * 128], BF16))
    hmax = es.enter_context(nc.sbuf_tensor("hmax", [128, H], F32))
    zrow = es.enter_context(nc.sbuf_tensor("zrow", [128, 2 * H], F32))
    pT0 = es.enter_context(nc.sbuf_tensor("pT0", [128, 128], BF16))
    idn = es.enter_context(nc.sbuf_tensor("idn", [128, 128], BF16))
    ix8 = es.enter_context(nc.sbuf_tensor("ix8", [128, 2 * 8 * 8], U32))
    asc = es.enter_context(nc.sbuf_tensor("asc", [128, 8 * 64], F32))
    st8 = es.enter_context(nc.sbuf_tensor("st8", [128, 8 * 8], F32))
    pk8 = es.enter_context(nc.sbuf_tensor("pk8", [128, 8 * 8], U32))
    kxi = es.enter_context(nc.sbuf_tensor("kxi", [128, 8 * 8], U32))
    idxf = es.enter_context(nc.sbuf_tensor("idxf", [128, 64], U32))
    idxi = es.enter_context(nc.sbuf_tensor("idxi", [128, 64], I32))
    iacc = es.enter_context(nc.sbuf_tensor("iacc", [128, 64], U32))
    msk = es.enter_context(nc.sbuf_tensor("msk", [128, 64], U32))
    cmb = es.enter_context(nc.sbuf_tensor("cmb", [128, 64], F32))
    ocol = es.enter_context(nc.sbuf_tensor("ocol", [128, 64], F32))
    gath = es.enter_context(nc.sbuf_tensor("gath", [128, D], BF16))
    yacc = es.enter_context(nc.sbuf_tensor("yacc", [128, D], F32))
    st_ = es.enter_context(nc.sbuf_tensor("st_", [128, D], F32))
    gt_ = es.enter_context(nc.sbuf_tensor("gt_", [128, D], F32))
    scrj = es.enter_context(nc.sbuf_tensor("scrj", [128, 64], F32))
    ps0 = es.enter_context(nc.psum_tensor("ps0", [128, 512], F32))
    ps1 = es.enter_context(nc.psum_tensor("ps1", [128, 512], F32))
    ps2 = es.enter_context(nc.psum_tensor("ps2", [128, 512], F32))
    ps3 = es.enter_context(nc.psum_tensor("ps3", [128, 512], F32))
    pss0 = es.enter_context(nc.psum_tensor("pss0", [128, 128], F32))
    pss1 = es.enter_context(nc.psum_tensor("pss1", [128, 128], F32))
    psT = es.enter_context(nc.psum_tensor("psT", [128, 128], BF16))
    psa = es.enter_context(nc.psum_tensor("psa", [128, 64], F32))
    PSA = [ps0, ps1, ps2, ps3]
    if True:
        p.dma("sync", lambda e: e.dma_start(out=mask2[:], in_=bdmask[:]))
        p.dma("sync", lambda e: e.dma_start(out=gln.rearrange("p (k o) -> p k o", o=1),
                                            in_=lnw.rearrange("(k p o) -> p k o", p=128, o=1),
                                            allow_slow_non_contiguous=True))
        p.dma("gpsimd", lambda e: e.dma_start(out=bln.rearrange("p (k o) -> p k o", o=1),
                                              in_=lnb.rearrange("(k p o) -> p k o", p=128, o=1),
                                              allow_slow_non_contiguous=True))
        p.dma("gpsimd", lambda e: e.dma_start(out=keyssb.rearrange("p (g n) -> p g n", g=16),
                                              in_=keysfm.rearrange("g p n -> p g n")))
        p.barrier()
        p.op("vector", lambda e: e.memset(onesrow[:], 1.0))
        p.op("gpsimd", lambda e: e.memset(onesf[:], 1.0))
        p.barrier()
        p.op("vector", lambda e: e.memset(epst[:, 0:1], 1e-5))
        p.op("gpsimd", lambda e: e.memset(idn[:], 1.0))
        p.barrier()
        p.op("vector", lambda e: e.memset(epst[:, 3:4], 0.0))
        p.op("gpsimd", lambda e: e.affine_select(idn[:], idn[:], [[-1, 128]], ALU.is_equal,
                                                 0.0, base=0, channel_multiplier=1))
        p.barrier()

        for tt in range(NT):
            sl = slice(tt * 128, tt * 128 + 128)
            p.dma("sync", (lambda sl: lambda e: e.dma_start(out=xt[:], in_=xsh[sl, :]))(sl))
            p.barrier()
            emit_ln(p, xt, xnt, sqt, stat, epst[:, 0:1])
            p.op("vector", lambda e: e.tensor_copy(bft[:], xnt[:]))
            p.barrier()
            p.dma("sync", (lambda sl: lambda e: e.dma_start(out=xn_d[sl, :], in_=xnt[:]))(sl))
            p.dma("sync", (lambda sl: lambda e: e.dma_start(out=lnbf_d[sl, :], in_=bft[:]))(sl))
            p.barrier()
            p.op("tensor", (lambda tt: lambda e: e.matmul(
                ps0[0:1, 0:512], onesf[:, 0:1], xnt[:, 0:512],
                start=(tt == 0), stop=(tt == NT - 1)))(tt))
            p.op("tensor", (lambda tt: lambda e: e.matmul(
                ps1[0:1, 0:512], onesf[:, 0:1], xnt[:, 512:1024],
                start=(tt == 0), stop=(tt == NT - 1)))(tt))
            p.barrier()
            p.op("scalar", lambda e: e.activation(sqt[:], xnt[:], AF.Square))
            p.barrier()
            p.op("tensor", (lambda tt: lambda e: e.matmul(
                ps2[0:1, 0:512], onesf[:, 0:1], sqt[:, 0:512],
                start=(tt == 0), stop=(tt == NT - 1)))(tt))
            p.op("tensor", (lambda tt: lambda e: e.matmul(
                ps3[0:1, 0:512], onesf[:, 0:1], sqt[:, 512:1024],
                start=(tt == 0), stop=(tt == NT - 1)))(tt))
            p.barrier()
            emit_ln(p, xnt, sqt, xt, stat, epst[:, 0:1])
            p.op("vector", lambda e: e.tensor_copy(bft[:], sqt[:]))
            p.barrier()
            p.dma("sync", (lambda sl: lambda e: e.dma_start(out=ln2bf_d[sl, :], in_=bft[:]))(sl))
            p.barrier()
        p.op("vector", lambda e: e.tensor_copy(statrow[0:1, 0:512], ps0[0:1, 0:512]))
        p.barrier()
        p.op("vector", lambda e: e.tensor_copy(statrow[0:1, 512:1024], ps1[0:1, 0:512]))
        p.barrier()
        p.op("vector", lambda e: e.tensor_copy(statrow[0:1, D:D + 512], ps2[0:1, 0:512]))
        p.barrier()
        p.op("vector", lambda e: e.tensor_copy(statrow[0:1, D + 512:2 * D], ps3[0:1, 0:512]))
        p.barrier()
        p.dma("sync", lambda e: e.dma_start(out=cc_in.ap(), in_=statrow[0:1, :]))
        p.barrier()
        p.op("gpsimd", lambda e: e.collective_compute(
            "AllReduce", ALU.add, replica_groups=[list(range(n_ranks))],
            ins=[cc_in.ap().opt()], outs=[cc_out.ap().opt()]))
        p.barrier()
        p.dma("sync", lambda e: e.dma_start(out=statrow[0:1, :], in_=cc_out.ap()))
        p.barrier()
        p.op("vector", lambda e: e.tensor_scalar_mul(statrow[0:1, :], statrow[0:1, :], 1.0 / 16384.0))
        p.barrier()
        p.op("vector", lambda e: e.tensor_mul(sqt[0:1, 0:D], statrow[0:1, 0:D], statrow[0:1, 0:D]))
        p.barrier()
        p.op("vector", lambda e: e.tensor_sub(statrow[0:1, D:2 * D], statrow[0:1, D:2 * D], sqt[0:1, 0:D]))
        p.barrier()
        p.op("scalar", lambda e: e.activation(statrow[0:1, D:2 * D], statrow[0:1, D:2 * D], AF.Ln,
                                              bias=epst[0:1, 0:1]))
        p.barrier()
        p.op("scalar", lambda e: e.activation(statrow[0:1, D:2 * D], statrow[0:1, D:2 * D], AF.Exp, scale=-0.5))
        p.barrier()
        p.dma("sync", lambda e: e.dma_start(out=sqt[0:1, 0:D], in_=gamma.rearrange("(o d) -> o d", o=1)))
        p.barrier()
        p.op("vector", lambda e: e.tensor_mul(statrow[0:1, D:2 * D], statrow[0:1, D:2 * D], sqt[0:1, 0:D]))
        p.barrier()
        p.op("vector", lambda e: e.tensor_mul(statrow[0:1, 0:D], statrow[0:1, 0:D], statrow[0:1, D:2 * D]))
        p.barrier()
        p.dma("sync", lambda e: e.dma_start(out=sqt[0:1, 0:D], in_=beta.rearrange("(o d) -> o d", o=1)))
        p.barrier()
        p.op("vector", lambda e: e.tensor_sub(statrow[0:1, 0:D], sqt[0:1, 0:D], statrow[0:1, 0:D]))
        p.barrier()
        p.dma("sync", lambda e: e.dma_start(out=bnc_d[0:1, 0:D], in_=statrow[0:1, D:2 * D]))
        p.barrier()
        p.dma("sync", lambda e: e.dma_start(out=gbn.rearrange("p (k o) -> p k o", o=1),
                                            in_=bnc_d[0:1, 0:D].rearrange("o (k p) -> p k o", p=128),
                                            allow_slow_non_contiguous=True))
        p.barrier()

        ln2fm = es.enter_context(nc.sbuf_tensor("ln2fm", [128, ND * T], BF16))
        for kc in range(ND):
            p.dma("sync", (lambda kc: lambda e: e.dma_start(
                out=ln2fm[:, kc * T:(kc + 1) * T], in_=ln2bf_d[:, kc * 128:(kc + 1) * 128],
                transpose=True))(kc))
            p.barrier()
            p.dma("sync", (lambda kc: lambda e: e.dma_start(
                out=pwide[:], in_=lnbf_d[:, kc * 128:(kc + 1) * 128], transpose=True))(kc))
            p.barrier()
            p.dma("sync", (lambda kc: lambda e: e.dma_start(
                out=xnfm_d[kc, :, :], in_=pwide[:]))(kc))
            p.barrier()

        for bi_ in range(6):
            nb0 = bi_ * 512
            for kc in range(ND):
                p.dma("gpsimd", (lambda kc, nb0: lambda e: e.dma_start(
                    out=wstr[:, 0:512], in_=wqkvT[kc * 128:(kc + 1) * 128, nb0:nb0 + 512]))(kc, nb0))
                p.barrier()
                p.op("tensor", (lambda kc: lambda e: e.matmul(
                    ps0[0:1, 0:512], bln[:, kc:kc + 1], wstr[:, 0:512],
                    start=(kc == 0), stop=(kc == ND - 1)))(kc))
                p.barrier()
            p.op("vector", (lambda nb0: lambda e: e.tensor_copy(
                browb[0:1, nb0:nb0 + 512], ps0[0:1, 0:512]))(nb0))
            p.barrier()

        p.dma("gpsimd", lambda e: e.dma_start(out=bnc_d[2:3, 0:D], in_=browb[0:1, 0:D]))
        p.dma("gpsimd", lambda e: e.dma_start(out=bnc_d[3:4, 0:D], in_=browb[0:1, D:2 * D]))
        p.barrier()
        bcol = es.enter_context(nc.sbuf_tensor("bcol", [128, 16], F32))
        p.dma("sync", lambda e: e.dma_start(out=bcol[:, 0:8].rearrange("p (k o) -> p k o", o=1),
                                            in_=bnc_d[2:3, 0:D].rearrange("o (k p) -> p k o", p=128),
                                            allow_slow_non_contiguous=True))
        p.dma("sync", lambda e: e.dma_start(out=bcol[:, 8:16].rearrange("p (k o) -> p k o", o=1),
                                            in_=bnc_d[3:4, 0:D].rearrange("o (k p) -> p k o", p=128),
                                            allow_slow_non_contiguous=True))
        p.barrier()
        p.op("vector", lambda e: e.tensor_scalar_mul(bcol[:, 0:8], bcol[:, 0:8], 1.0 / 32.0))
        p.barrier()

        for part in range(2):
            for oc in range(ND):
                for kc in range(ND):
                    sslot = (kc % 2) * 1024
                    p.dma("gpsimd", (lambda part, oc, kc, sslot: lambda e: e.dma_start(
                        out=wstr[:, sslot:sslot + 128],
                        in_=wqkvT[kc * 128:(kc + 1) * 128,
                                  part * D + oc * 128: part * D + oc * 128 + 128]))(part, oc, kc, sslot))
                    p.barrier()
                    p.op("vector", (lambda kc, sslot: lambda e: e.tensor_scalar(
                        wstr[:, sslot:sslot + 128], wstr[:, sslot:sslot + 128],
                        gln[:, kc:kc + 1], None, ALU.mult))(kc, sslot))
                    p.barrier()
                    for tb in range(4):
                        p.op("tensor", (lambda kc, tb, sslot: lambda e: e.matmul(
                            PSA[tb][:, 0:512], wstr[:, sslot:sslot + 128],
                            ln2fm[:, kc * T + tb * 512: kc * T + tb * 512 + 512],
                            start=(kc == 0), stop=(kc == ND - 1)))(kc, tb, sslot))
                    p.barrier()
                for tb in range(4):
                    dst = qfm if part == 0 else kfm
                    sc = 1.0 / 32.0 if part == 0 else 1.0
                    p.op("scalar", (lambda oc, tb, dst, sc, part: lambda e: e.activation(
                        dst[:, oc * T + tb * 512: oc * T + tb * 512 + 512],
                        PSA[tb][:, 0:512], AF.Identity,
                        bias=bcol[:, part * 8 + oc: part * 8 + oc + 1], scale=sc))(oc, tb, dst, sc, part))
                p.barrier()

        for grp in range(4):
            tts = list(range(grp * 4, grp * 4 + 4))
            for nb in range(2):
                for kc in range(ND):
                    sslot = (kc % 2) * 1024
                    p.dma("gpsimd", (lambda nb, kc, sslot: lambda e: e.dma_start(
                        out=wstr[:, sslot:sslot + 512],
                        in_=wqkvT[kc * 128:(kc + 1) * 128,
                                  2 * D + nb * 512: 2 * D + nb * 512 + 512]))(nb, kc, sslot))
                    p.barrier()
                    p.op("vector", (lambda kc, sslot: lambda e: e.tensor_scalar(
                        wstr[:, sslot:sslot + 512], wstr[:, sslot:sslot + 512],
                        gln[:, kc:kc + 1], None, ALU.mult))(kc, sslot))
                    p.barrier()
                    for ti, tt in enumerate(tts):
                        p.op("tensor", (lambda kc, ti, tt, sslot: lambda e: e.matmul(
                            PSA[ti][:, 0:512],
                            ln2fm[:, kc * T + tt * 128: kc * T + tt * 128 + 128],
                            wstr[:, sslot:sslot + 512],
                            start=(kc == 0), stop=False))(kc, ti, tt, sslot))
                    p.barrier()
                for ti in range(4):
                    p.op("tensor", (lambda nb, ti: lambda e: e.matmul(
                        PSA[ti][:, 0:512], onesrow[0:1, 0:128],
                        browb[0:1, 2 * D + nb * 512: 2 * D + nb * 512 + 512],
                        start=False, stop=True))(nb, ti))
                p.barrier()
                for ti, tt in enumerate(tts):
                    p.op("vector", (lambda ti: lambda e: e.tensor_copy(
                        xbft[:, 0:512], PSA[ti][:, 0:512]))(ti))
                    p.barrier()
                    p.dma("sync", (lambda tt, nb: lambda e: e.dma_start(
                        out=vtok_d[tt * 128:(tt + 1) * 128, nb * 512:nb * 512 + 512],
                        in_=xbft[:, 0:512]))(tt, nb))
                    p.barrier()

        vh = es.enter_context(nc.sbuf_tensor("vh", [128, D], BF16))
        for tt in range(NT):
            sl = slice(tt * 128, tt * 128 + 128)
            p.dma("sync", (lambda sl: lambda e: e.dma_start(out=vh[:], in_=vtok_d[sl, :]))(sl))
            p.barrier()
            for h in range(H):
                oc, hb = h // 2, (h % 2) * 64
                pss = [pss0, pss1][h % 2]
                p.op("tensor", (lambda oc, hb, tt, pss: lambda e: e.matmul(
                    pss[:, 0:128], qfm[hb:hb + 64, oc * T + tt * 128: oc * T + tt * 128 + 128],
                    kfm[hb:hb + 64, oc * T + tt * 128: oc * T + tt * 128 + 128],
                    start=True, stop=True))(oc, hb, tt, pss))
                p.barrier()
                p.op("vector", (lambda h, pss: lambda e: e.tensor_add(
                    swide[:, h * 128:(h + 1) * 128], pss[:, 0:128], mask2[:]))(h, pss))
                p.barrier()
            p.op("vector", lambda e: e.reduce_max(hmax[:], swide.rearrange("p (h w) -> p h w", h=H),
                                                  axis=AX.X, negate=True))
            p.barrier()
            p.op("vector", lambda e: e.tensor_tensor(
                swide.rearrange("p (h w) -> p h w", h=H),
                swide.rearrange("p (h w) -> p h w", h=H),
                hmax.rearrange("p (h o) -> p h o", o=1).broadcast_to([128, H, 128]),
                ALU.add))
            p.barrier()
            p.op("scalar", lambda e: e.activation(pwide[:], swide[:], AF.Exp))
            p.barrier()
            p.op("vector", lambda e: e.reduce_sum(zrow[:, 0:H],
                                                  pwide.rearrange("p (h w) -> p h w", h=H), axis=AX.X))
            p.barrier()
            p.op("vector", lambda e: e.reciprocal(zrow[:, H:2 * H], zrow[:, 0:H]))
            p.barrier()
            for h in range(H):
                p.op("tensor", (lambda h: lambda e: e.transpose(
                    psT[:], pwide[:, h * 128:(h + 1) * 128], idn[:]))(h))
                p.barrier()
                p.op("vector", lambda e: e.tensor_copy(pT0[:], psT[:]))
                p.barrier()
                p.op("tensor", (lambda h: lambda e: e.matmul(
                    psa[:, 0:64], pT0[:], vh[:, h * 64:h * 64 + 64],
                    start=True, stop=True))(h))
                p.barrier()
                p.op("vector", (lambda h: lambda e: e.tensor_scalar(
                    st_[:, h * 64:h * 64 + 64], psa[:, 0:64], zrow[:, H + h:H + h + 1],
                    None, ALU.mult))(h))
                p.barrier()
            p.op("vector", lambda e: e.tensor_copy(bft[:], st_[:]))
            p.barrier()
            p.dma("sync", (lambda sl: lambda e: e.dma_start(out=attn_d[sl, :], in_=bft[:]))(sl))
            p.barrier()

        attnfm = qfm
        for kc in range(ND):
            p.dma("sync", (lambda kc: lambda e: e.dma_start(
                out=attnfm[:, kc * T:(kc + 1) * T], in_=attn_d[:, kc * 128:(kc + 1) * 128],
                transpose=True))(kc))
        p.barrier()
        for tt in range(NT):
            sl = slice(tt * 128, tt * 128 + 128)
            for nb in range(2):
                ps = [ps0, ps1][nb]
                for kc in range(ND):
                    sslot = (kc % 2) * 1024
                    p.dma("gpsimd", (lambda kc, nb, sslot: lambda e: e.dma_start(
                        out=wstr[:, sslot:sslot + 512],
                        in_=woutT[kc * 128:(kc + 1) * 128, nb * 512:nb * 512 + 512]))(kc, nb, sslot))
                    p.barrier()
                    p.op("tensor", (lambda tt, kc, nb, ps, sslot: lambda e: e.matmul(
                        ps[:, 0:512],
                        attnfm[:, kc * T + tt * 128: kc * T + tt * 128 + 128],
                        wstr[:, sslot:sslot + 512],
                        start=(kc == 0), stop=(kc == ND - 1)))(tt, kc, nb, ps, sslot))
                    p.barrier()
                p.barrier()
            p.dma("sync", (lambda sl: lambda e: e.dma_start(out=xnt[:], in_=xn_d[sl, :]))(sl))
            p.barrier()
            for nb in range(2):
                ps = [ps0, ps1][nb]
                p.op("vector", (lambda nb, ps: lambda e: e.tensor_add(
                    st_[:, nb * 512:nb * 512 + 512], ps[:, 0:512],
                    xnt[:, nb * 512:nb * 512 + 512]))(nb, ps))
                p.barrier()
            p.dma("sync", (lambda sl: lambda e: e.dma_start(out=sg_d[sl, :], in_=st_[:]))(sl))
            p.barrier()

        p.dma("sync", lambda e: e.dma_start(out=sg_d[1:2, 0:D] if False else sqt[1:2, 0:D],
                                            in_=sg_d[1:2, 0:D]))
        p.barrier()
        bnbc = es.enter_context(nc.sbuf_tensor("bnbc", [128, ND], BF16))
        p.dma("sync", lambda e: e.dma_start(out=bnc_d[1:2, 0:D], in_=statrow[0:1, 0:D]))
        p.barrier()
        p.dma("gpsimd", lambda e: e.dma_start(out=bnbc.rearrange("p (k o) -> p k o", o=1),
                                              in_=bnc_d[1:2, 0:D].rearrange("o (k p) -> p k o", p=128),
                                              allow_slow_non_contiguous=True))
        p.barrier()
        for nb in range(4):
            for kc in range(ND):
                sslot = (kc % 2) * 1024
                p.dma("gpsimd", (lambda nb, kc, sslot: lambda e: e.dma_start(
                    out=wstr[:, sslot:sslot + 512],
                    in_=wqT[kc * 128:(kc + 1) * 128, nb * 512:nb * 512 + 512]))(nb, kc, sslot))
                p.barrier()
                p.op("tensor", (lambda kc, sslot: lambda e: e.matmul(
                    ps0[0:1, 0:512], bnbc[:, kc:kc + 1], wstr[:, sslot:sslot + 512],
                    start=(kc == 0), stop=(kc == ND - 1)))(kc, sslot))
                p.barrier()
            p.op("vector", (lambda nb: lambda e: e.tensor_copy(
                brow2b[0:1, nb * 512:nb * 512 + 512], ps0[0:1, 0:512]))(nb))
            p.barrier()
        bcol2 = es.enter_context(nc.sbuf_tensor("bcol2", [128, 16], F32))
        p.dma("gpsimd", lambda e: e.dma_start(out=bnc_d[2:3, 0:D], in_=brow2b[0:1, 0:D]))
        p.dma("gpsimd", lambda e: e.dma_start(out=bnc_d[3:4, 0:D], in_=brow2b[0:1, D:2 * D]))
        p.barrier()
        p.dma("sync", lambda e: e.dma_start(out=bcol2[:, 0:8].rearrange("p (k o) -> p k o", o=1),
                                            in_=bnc_d[2:3, 0:D].rearrange("o (k p) -> p k o", p=128),
                                            allow_slow_non_contiguous=True))
        p.dma("sync", lambda e: e.dma_start(out=bcol2[:, 8:16].rearrange("p (k o) -> p k o", o=1),
                                            in_=bnc_d[3:4, 0:D].rearrange("o (k p) -> p k o", p=128),
                                            allow_slow_non_contiguous=True))
        p.barrier()
        q2tb = es.enter_context(nc.sbuf_tensor("q2tb", [128, 16 * 512], BF16))
        xsl2 = es.enter_context(nc.sbuf_tensor("xsl2", [128, 512], BF16))
        simt = es.enter_context(nc.sbuf_tensor("simt2", [128, NK], F32))
        sxb = es.enter_context(nc.sbuf_tensor("sxb", [128, 16 * 8], F32))
        gt4 = [es.enter_context(nc.sbuf_tensor(f"gath{i}", [128, D], BF16)) for i in range(2)]
        for tb in range(4):
            for ocg in range(4):
                for kc in range(ND):
                    p.dma("sync", (lambda kc, tb: lambda e: e.dma_start(
                        out=xsl2[:], in_=xnfm_d[kc, :, tb * 512:(tb + 1) * 512]))(kc, tb))
                    p.barrier()
                    for oi in range(4):
                        oc = ocg * 4 + oi
                        sslot = (oi % 2) * 1024
                        p.dma("gpsimd", (lambda oc, kc, sslot: lambda e: e.dma_start(
                            out=wstr[:, sslot:sslot + 128],
                            in_=wqT[kc * 128:(kc + 1) * 128, oc * 128: oc * 128 + 128]))(oc, kc, sslot))
                        p.barrier()
                        p.op("vector", (lambda kc, sslot: lambda e: e.tensor_scalar(
                            wstr[:, sslot:sslot + 128], wstr[:, sslot:sslot + 128],
                            gbn[:, kc:kc + 1], None, ALU.mult))(kc, sslot))
                        p.barrier()
                        p.op("tensor", (lambda oi, kc, sslot: lambda e: e.matmul(
                            [ps0, ps1, ps2, ps3][oi][:, 0:512], wstr[:, sslot:sslot + 128],
                            xsl2[:], start=(kc == 0), stop=(kc == ND - 1)))(oi, kc, sslot))
                        p.barrier()
                for oi in range(4):
                    oc = ocg * 4 + oi
                    p.op("scalar", (lambda oc, oi: lambda e: e.activation(
                        q2tb[:, oc * 512:(oc + 1) * 512], [ps0, ps1, ps2, ps3][oi][:, 0:512],
                        AF.Identity, bias=bcol2[:, oc:oc + 1], scale=1.0))(oc, oi))
                    p.barrier()
            for rt in range(4):
                tt = tb * 4 + rt
                sl = slice(tt * 128, tt * 128 + 128)
                for oc in range(16):
                    pss = [pss0, pss1][oc % 2]
                    p.op("tensor", (lambda oc, rt, pss: lambda e: e.matmul(
                        pss[:, 0:128],
                        q2tb[:, oc * 512 + rt * 128: oc * 512 + rt * 128 + 128],
                        keyssb[:, oc * NK: oc * NK + 128], start=True, stop=True))(oc, rt, pss))
                    p.barrier()
                    p.op("vector", (lambda pss: lambda e: e.tensor_copy(
                        simt[:, 0:128], pss[:, 0:128]))(pss))
                    p.barrier()
                    p.op("tensor", (lambda oc, rt, pss: lambda e: e.matmul(
                        pss[:, 0:128],
                        q2tb[:, oc * 512 + rt * 128: oc * 512 + rt * 128 + 128],
                        keyssb[:, oc * NK + 128: oc * NK + 256], start=True, stop=True))(oc, rt, pss))
                    p.barrier()
                    p.op("vector", (lambda pss: lambda e: e.tensor_copy(
                        simt[:, 128:256], pss[:, 0:128]))(pss))
                    p.barrier()
                    p.op("vector", (lambda oc: lambda e: e.max(
                        sxb[:, oc * 8:(oc + 1) * 8], simt[:, 0:NK]))(oc))
                    p.barrier()
                    p.op("vector", (lambda oc: lambda e: e.max_index(
                        ix8[:, oc * 8:(oc + 1) * 8], sxb[:, oc * 8:(oc + 1) * 8],
                        simt[:, 0:NK]))(oc))
                    p.barrier()
                for h in range(8):
                    p.op("vector", (lambda h: lambda e: e.tensor_tensor(
                        asc.rearrange("p (h a b) -> p h a b", h=8, a=8)[:, h, :, :],
                        sxb[:, h * 8:(h + 1) * 8].rearrange("p (a b) -> p a b", b=1).broadcast_to([128, 8, 8]),
                        sxb[:, 64 + h * 8: 64 + (h + 1) * 8].rearrange("p (a b) -> p a b", a=1).broadcast_to([128, 8, 8]),
                        ALU.add))(h))
                    p.barrier()
                for h in range(8):
                    p.op("vector", (lambda h: lambda e: e.max(
                        st8[:, h * 8:(h + 1) * 8], asc[:, h * 64:(h + 1) * 64]))(h))
                    p.barrier()
                    p.op("vector", (lambda h: lambda e: e.max_index(
                        pk8[:, h * 8:(h + 1) * 8], st8[:, h * 8:(h + 1) * 8],
                        asc[:, h * 64:(h + 1) * 64]))(h))
                    p.barrier()
                p.op("vector", lambda e: e.tensor_scalar(kxi[:], pk8[:], 3, None, ALU.logical_shift_right))
                p.barrier()
                p.op("vector", lambda e: e.tensor_scalar(pk8[:], pk8[:], 7, None, ALU.bitwise_and))
                p.barrier()
                p.op("vector", lambda e: e.memset(idxf[:], 0))
                p.barrier()
                for j0 in range(8):
                    p.op("vector", (lambda j0: lambda e: e.tensor_scalar(
                        msk[:], kxi[:], j0, None, ALU.is_equal))(j0))
                    p.barrier()
                    p.op("vector", (lambda j0: lambda e: e.tensor_tensor(
                        iacc.rearrange("p (h k) -> p h k", h=8), msk.rearrange("p (h k) -> p h k", h=8),
                        ix8.rearrange("p (s h k) -> p s h k", s=2, h=8)[:, 0, :, j0:j0 + 1].broadcast_to([128, 8, 8]),
                        ALU.mult))(j0))
                    p.barrier()
                    p.op("vector", lambda e: e.tensor_add(idxf[:], idxf[:], iacc[:]))
                    p.barrier()
                p.op("vector", lambda e: e.tensor_scalar(idxf[:], idxf[:], 8, None, ALU.logical_shift_left))
                p.barrier()
                for j0 in range(8):
                    p.op("vector", (lambda j0: lambda e: e.tensor_scalar(
                        msk[:], pk8[:], j0, None, ALU.is_equal))(j0))
                    p.barrier()
                    p.op("vector", (lambda j0: lambda e: e.tensor_tensor(
                        iacc.rearrange("p (h k) -> p h k", h=8), msk.rearrange("p (h k) -> p h k", h=8),
                        ix8.rearrange("p (s h k) -> p s h k", s=2, h=8)[:, 1, :, j0:j0 + 1].broadcast_to([128, 8, 8]),
                        ALU.mult))(j0))
                    p.barrier()
                    p.op("vector", lambda e: e.tensor_add(idxf[:], idxf[:], iacc[:]))
                    p.barrier()
                p.op("vector", lambda e: e.tensor_copy(idxi[:], idxf[:]))
                p.barrier()
                if tt == 0:
                    p.op("vector", lambda e: e.tensor_copy(sqt[:, 0:64], idxf[:]))
                    p.barrier()
                    p.dma("sync", lambda e: e.dma_start(out=dbg_d[:, 1024:1088], in_=sqt[:, 0:64]))
                    p.barrier()
                    p.op("vector", lambda e: e.tensor_copy(sqt[:, 0:64], st8[:]))
                    p.barrier()
                    p.dma("sync", lambda e: e.dma_start(out=dbg_d[:, 1088:1152], in_=sqt[:, 0:64]))
                    p.barrier()
                p.op("vector", lambda e: e.tensor_tensor(
                    cmb.rearrange("p (h k) -> p h k", h=8),
                    st8.rearrange("p (h k) -> p h k", h=8),
                    st8.rearrange("p (h k) -> p h k", h=8)[:, :, 0:1].broadcast_to([128, 8, 8]),
                    ALU.subtract))
                p.barrier()
                p.op("scalar", lambda e: e.activation(cmb[:], cmb[:], AF.Exp))
                p.barrier()
                p.op("vector", lambda e: e.reduce_sum(scrj[:, 0:8], cmb.rearrange("p (h k) -> p h k", h=8),
                                                      axis=AX.X))
                p.barrier()
                p.op("vector", lambda e: e.reciprocal(scrj[:, 8:16], scrj[:, 0:8]))
                p.barrier()
                p.op("vector", lambda e: e.tensor_tensor(
                    cmb.rearrange("p (h k) -> p h k", h=8),
                    cmb.rearrange("p (h k) -> p h k", h=8),
                    scrj[:, 8:16].rearrange("p (h o) -> p h o", o=1).broadcast_to([128, 8, 8]),
                    ALU.mult))
                p.barrier()
                p.dma("sync", (lambda sl: lambda e: e.dma_start(out=xbft[:], in_=lnbf_d[sl, :]))(sl))
                p.barrier()
                for sgrp in range(32):
                    for j in range(2):
                        slot = sgrp * 2 + j
                        p.dma("gpsimd", (lambda slot, j: lambda e: e.indirect_dma_start(
                            out=gt4[j][:], out_offset=None, in_=key_in[:, :],
                            in_offset=IndirectOffsetOnAxis(idxi[:, slot:slot + 1], 0)))(slot, j))
                    p.barrier()
                    for j in range(2):
                        slot = sgrp * 2 + j
                        p.op("vector", (lambda slot, j: lambda e: e.scalar_tensor_tensor(
                            bft[:], gt4[j][:], 1.0, xbft[:], ALU.mult, ALU.mult,
                            accum_out=ocol[:, slot:slot + 1]))(slot, j))
                        p.barrier()
                if tt == 0:
                    p.dma("sync", lambda e: e.dma_start(out=dbg_d[:, 1152:1216], in_=ocol[:]))
                    p.barrier()
                    p.dma("sync", lambda e: e.dma_start(out=dbg_d[:, 1216:1280], in_=cmb[:]))
                    p.barrier()
                p.op("scalar", lambda e: e.activation(ocol[:], ocol[:], AF.Gelu))
                p.barrier()
                p.op("vector", lambda e: e.tensor_mul(cmb[:], cmb[:], ocol[:]))
                p.barrier()
                if tt == 0:
                    p.dma("sync", lambda e: e.dma_start(out=dbg_d[:, 1280:1344], in_=ocol[:]))
                    p.barrier()
                    p.dma("sync", lambda e: e.dma_start(out=dbg_d[:, 1344:1408], in_=cmb[:]))
                    p.barrier()
                p.op("vector", lambda e: e.memset(yacc[:], 0.0))
                p.barrier()
                for sgrp in range(32):
                    for j in range(2):
                        slot = sgrp * 2 + j
                        p.dma("gpsimd", (lambda slot, j: lambda e: e.indirect_dma_start(
                            out=gt4[j][:], out_offset=None, in_=key_out[:, :],
                            in_offset=IndirectOffsetOnAxis(idxi[:, slot:slot + 1], 0)))(slot, j))
                    p.barrier()
                    for j in range(2):
                        slot = sgrp * 2 + j
                        p.op("vector", (lambda slot, j: lambda e: e.scalar_tensor_tensor(
                            yacc[:], gt4[j][:], cmb[:, slot:slot + 1], yacc[:],
                            ALU.mult, ALU.add))(slot, j))
                        p.barrier()
                if tt == 0:
                    p.dma("sync", lambda e: e.dma_start(out=dbg_d[:, 1408:2432], in_=yacc[:]))
                    p.barrier()
                p.op("scalar", lambda e: e.activation(gt_[:], yacc[:], AF.Silu))
                p.barrier()
                p.dma("sync", (lambda sl: lambda e: e.dma_start(out=st_[:], in_=sg_d[sl, :]))(sl))
                p.dma("sync", (lambda sl: lambda e: e.dma_start(out=xt[:], in_=xsh[sl, :]))(sl))
                p.barrier()
                p.op("vector", lambda e: e.tensor_mul(st_[:], st_[:], gt_[:]))
                p.barrier()
                p.op("vector", lambda e: e.tensor_add(st_[:], st_[:], xt[:]))
                p.barrier()
                p.dma("sync", (lambda sl: lambda e: e.dma_start(out=x3sh[sl, :], in_=st_[:]))(sl))
                p.barrier()
        p.build()
    es.close()
    return nc

def build_l3(nc):
    H, HD, MIX = 16, 64, 0.05
    E, I2 = 32, 512

    x3 = nc.dram_tensor("x3", [T, D], F32, kind="ExternalInput").ap()
    wqT = nc.dram_tensor("wqT", [D, D], F32, kind="ExternalInput").ap()
    wkT = nc.dram_tensor("wkT", [D, D], F32, kind="ExternalInput").ap()
    wvT = nc.dram_tensor("wvT", [D, D], F32, kind="ExternalInput").ap()
    wgT = nc.dram_tensor("wgT", [D, D], F32, kind="ExternalInput").ap()
    woT = nc.dram_tensor("woT", [D, D], F32, kind="ExternalInput").ap()
    wbfsT = nc.dram_tensor("wbfsT", [D, 48], F32, kind="ExternalInput").ap()
    bfs = nc.dram_tensor("bfs", [48], F32, kind="ExternalInput").ap()
    gatewT = nc.dram_tensor("gatewT", [D, E], F32, kind="ExternalInput").ap()
    gateb = nc.dram_tensor("gateb", [E], F32, kind="ExternalInput").ap()
    wgate = nc.dram_tensor("wgate", [E, D, I2], F32, kind="ExternalInput").ap()
    wup = nc.dram_tensor("wup", [E, D, I2], F32, kind="ExternalInput").ap()
    wdownT = nc.dram_tensor("wdownT", [E, I2, D], F32, kind="ExternalInput").ap()
    shgT = nc.dram_tensor("shgT", [D, D], F32, kind="ExternalInput").ap()
    shuT = nc.dram_tensor("shuT", [D, D], F32, kind="ExternalInput").ap()
    shdT = nc.dram_tensor("shdT", [D, D], F32, kind="ExternalInput").ap()

    x4 = nc.dram_tensor("x4", [T, D], F32, kind="ExternalOutput").ap()

    xn_d = nc.dram_tensor("xn_d", [T, D], F32, kind="Internal").ap()
    lnbf_d = nc.dram_tensor("lnbf_d", [T, D], BF16, kind="Internal").ap()
    q_d = nc.dram_tensor("q_d", [T, D], BF16, kind="Internal").ap()
    k_d = nc.dram_tensor("k_d", [T, D], BF16, kind="Internal").ap()
    v_d = nc.dram_tensor("v_d", [T, D], BF16, kind="Internal").ap()
    gg_d = nc.dram_tensor("gg_d", [T, D], BF16, kind="Internal").ap()
    fsb_d = nc.dram_tensor("fsb_d", [T, 48], F32, kind="Internal").ap()
    o_d = nc.dram_tensor("o_d", [T, D], BF16, kind="Internal").ap()
    vin_d = nc.dram_tensor("vin_d", [128, H, 2, HD], F32, kind="Internal").ap()
    vcar_d = nc.dram_tensor("vcar_d", [128, H, 2, HD], F32, kind="Internal").ap()
    og_d = nc.dram_tensor("og_d", [8, 128, T], BF16, kind="Internal").ap()
    s3_d = nc.dram_tensor("s3_d", [T, D], F32, kind="Internal").ap()

    p = Prog(nc)
    es = contextlib.ExitStack()
    xnfm = es.enter_context(nc.sbuf_tensor("xnfm", [128, ND * T], BF16))
    wstr = es.enter_context(nc.sbuf_tensor("wstr", [128, 2 * 1024], BF16))
    onesrow = es.enter_context(nc.sbuf_tensor("onesrow", [1, 512], BF16))
    onesf = es.enter_context(nc.sbuf_tensor("onesf", [1, 128], F32))
    epst = es.enter_context(nc.sbuf_tensor("epst", [128, 4], F32))
    stat = es.enter_context(nc.sbuf_tensor("stat", [128, 8], F32))
    xt = es.enter_context(nc.sbuf_tensor("xt", [128, D], F32))
    xnt = es.enter_context(nc.sbuf_tensor("xnt", [128, D], F32))
    sqt = es.enter_context(nc.sbuf_tensor("sqt", [128, D], F32))
    bft = es.enter_context(nc.sbuf_tensor("bft", [128, D], BF16))
    bfsrow = es.enter_context(nc.sbuf_tensor("bfsrow", [1, 64], BF16))
    qg = es.enter_context(nc.sbuf_tensor("qg", [128, 16 * 256], BF16))
    ug = es.enter_context(nc.sbuf_tensor("ug", [128, 16 * 256], BF16))
    og = es.enter_context(nc.sbuf_tensor("og", [128, 16 * 256], F32))
    kg = es.enter_context(nc.sbuf_tensor("kg", [128, 16 * 256], BF16))
    fst = es.enter_context(nc.sbuf_tensor("fst", [128, 16 * 48], F32))
    Sfc = es.enter_context(nc.sbuf_tensor("Sfc", [128, 256], F32))
    Ssc = es.enter_context(nc.sbuf_tensor("Ssc", [128, 256], F32))
    t1g = es.enter_context(nc.sbuf_tensor("t1g", [128, 256], F32))
    t2g = es.enter_context(nc.sbuf_tensor("t2g", [128, 256], F32))
    t3g = es.enter_context(nc.sbuf_tensor("t3g", [128, 256], F32))
    M11 = es.enter_context(nc.sbuf_tensor("M11", [128, 16], F32))
    M12 = es.enter_context(nc.sbuf_tensor("M12", [128, 16], F32))
    M21 = es.enter_context(nc.sbuf_tensor("M21", [128, 16], F32))
    M22 = es.enter_context(nc.sbuf_tensor("M22", [128, 16], F32))
    Ma = es.enter_context(nc.sbuf_tensor("Ma", [128, 16], F32))
    Mb = es.enter_context(nc.sbuf_tensor("Mb", [128, 16], F32))
    r1 = es.enter_context(nc.sbuf_tensor("r1", [128, 16 * 16], F32))
    r2 = es.enter_context(nc.sbuf_tensor("r2", [128, 16 * 16], F32))
    T11 = es.enter_context(nc.sbuf_tensor("T11", [128, 16], F32))
    T12 = es.enter_context(nc.sbuf_tensor("T12", [128, 16], F32))
    T21 = es.enter_context(nc.sbuf_tensor("T21", [128, 16], F32))
    T22 = es.enter_context(nc.sbuf_tensor("T22", [128, 16], F32))
    T11c = es.enter_context(nc.sbuf_tensor("T11c", [16, 128], F32))
    T12c = es.enter_context(nc.sbuf_tensor("T12c", [16, 128], F32))
    T21c = es.enter_context(nc.sbuf_tensor("T21c", [16, 128], F32))
    T22c = es.enter_context(nc.sbuf_tensor("T22c", [16, 128], F32))
    Sfh = es.enter_context(nc.sbuf_tensor("Sfh", [16, HD], F32))
    Ssh = es.enter_context(nc.sbuf_tensor("Ssh", [16, HD], F32))
    vstg = es.enter_context(nc.sbuf_tensor("vstg", [16, 8 * 2 * HD], F32))
    cstg = es.enter_context(nc.sbuf_tensor("cstg", [16, 8 * 2 * HD], F32))
    idnf = es.enter_context(nc.sbuf_tensor("idnf", [128, 128], F32))
    idnb = es.enter_context(nc.sbuf_tensor("idnb", [128, 128], BF16))
    cmb32 = es.enter_context(nc.sbuf_tensor("cmb32", [128, NT * E], F32))
    grow = es.enter_context(nc.sbuf_tensor("grow", [128, E], F32))
    zfm = es.enter_context(nc.sbuf_tensor("zfm", [128, 4 * 512], BF16))
    ufm = es.enter_context(nc.sbuf_tensor("ufm", [128, 4 * 512], BF16))
    yac4 = es.enter_context(nc.sbuf_tensor("yac4", [128, 4 * D], F32))
    st_ = es.enter_context(nc.sbuf_tensor("st_", [128, D], F32))
    gt_ = es.enter_context(nc.sbuf_tensor("gt_", [128, D], F32))
    ps0 = es.enter_context(nc.psum_tensor("ps0", [128, 512], F32))
    ps1 = es.enter_context(nc.psum_tensor("ps1", [128, 512], F32))
    ps2 = es.enter_context(nc.psum_tensor("ps2", [128, 512], F32))
    ps3 = es.enter_context(nc.psum_tensor("ps3", [128, 512], F32))
    psF = es.enter_context(nc.psum_tensor("psF", [128, 128], F32))
    PSA = [ps0, ps1, ps2, ps3]
    if True:
        p.op("vector", lambda e: e.memset(onesrow[:], 1.0))
        p.op("gpsimd", lambda e: e.memset(onesf[:], 1.0))
        p.op("scalar", lambda e: e.memzero(epst[:]))
        p.barrier()
        p.op("vector", lambda e: e.memset(epst[:, 0:1], 1e-5))
        p.op("gpsimd", lambda e: e.memset(idnb[:], 1.0))
        p.barrier()
        p.op("gpsimd", lambda e: e.affine_select(idnb[:], idnb[:], [[-1, 128]], ALU.is_equal,
                                                 0.0, base=0, channel_multiplier=1))
        p.barrier()
        p.op("vector", lambda e: e.tensor_copy(idnf[:], idnb[:]))
        p.barrier()
        p.dma("gpsimd", lambda e: e.dma_start(out=bfsrow[0:1, 0:48],
                                              in_=bfs.rearrange("(o d) -> o d", o=1)))
        p.barrier()

        for tt in range(NT):
            sl = slice(tt * 128, tt * 128 + 128)
            p.dma("sync", (lambda sl: lambda e: e.dma_start(out=xt[:], in_=x3[sl, :]))(sl))
            p.barrier()
            emit_ln(p, xt, xnt, sqt, stat, epst[:, 0:1])
            p.op("vector", lambda e: e.tensor_copy(bft[:], xnt[:]))
            p.barrier()
            p.dma("sync", (lambda sl: lambda e: e.dma_start(out=xn_d[sl, :], in_=xnt[:]))(sl))
            p.dma("sync", (lambda sl: lambda e: e.dma_start(out=lnbf_d[sl, :], in_=bft[:]))(sl))
            p.barrier()
        for kc in range(ND):
            p.dma("sync", (lambda kc: lambda e: e.dma_start(
                out=xnfm[:, kc * T:(kc + 1) * T], in_=lnbf_d[:, kc * 128:(kc + 1) * 128],
                transpose=True))(kc))
        p.barrier()

        def proj_tok(wT, out_d, act):
            for grp in range(4):
                tts = list(range(grp * 4, grp * 4 + 4))
                for nb in range(2):
                    for kc in range(ND):
                        sslot = (kc % 2) * 1024
                        p.dma("gpsimd", (lambda kc, nb, sslot, wT: lambda e: e.dma_start(
                            out=wstr[:, sslot:sslot + 512],
                            in_=wT[kc * 128:(kc + 1) * 128, nb * 512:nb * 512 + 512]))(kc, nb, sslot, wT))
                        p.barrier()
                        for ti, tt in enumerate(tts):
                            p.op("tensor", (lambda kc, ti, tt, sslot: lambda e: e.matmul(
                                PSA[ti][:, 0:512],
                                xnfm[:, kc * T + tt * 128: kc * T + tt * 128 + 128],
                                wstr[:, sslot:sslot + 512],
                                start=(kc == 0), stop=(kc == ND - 1)))(kc, ti, tt, sslot))
                        p.barrier()
                    for ti, tt in enumerate(tts):
                        p.op("scalar", (lambda ti, tt, nb, act: lambda e: e.activation(
                            bft[:, 0:512], PSA[ti][:, 0:512], act))(ti, tt, nb, act))
                        p.barrier()
                        p.dma("sync", (lambda tt, nb, out_d: lambda e: e.dma_start(
                            out=out_d[tt * 128:(tt + 1) * 128, nb * 512:nb * 512 + 512],
                            in_=bft[:, 0:512]))(tt, nb, out_d))
                        p.barrier()

        proj_tok(wqT, q_d, AF.Silu)
        proj_tok(wkT, k_d, AF.Silu)
        proj_tok(wvT, v_d, AF.Silu)
        proj_tok(wgT, gg_d, AF.Sigmoid)

        wbf_sb = es.enter_context(nc.sbuf_tensor("wbf_sb", [128, ND * 48], BF16))
        p.dma("gpsimd", lambda e: e.dma_start(out=wbf_sb.rearrange("p (k n) -> p k n", k=ND),
                                              in_=wbfsT.rearrange("(k p) n -> p k n", p=128)))
        p.barrier()
        for tt in range(NT):
            for kc in range(ND):
                p.op("tensor", (lambda tt, kc: lambda e: e.matmul(
                    ps0[:, 0:48], xnfm[:, kc * T + tt * 128: kc * T + tt * 128 + 128],
                    wbf_sb[:, kc * 48:(kc + 1) * 48], start=(kc == 0), stop=False))(tt, kc))
            p.op("tensor", lambda e: e.matmul(ps0[:, 0:48], onesrow[0:1, 0:128],
                                              bfsrow[0:1, 0:48], start=False, stop=True))
            p.barrier()
            p.op("scalar", lambda e: e.activation(sqt[:, 0:48], ps0[:, 0:48], AF.Sigmoid))
            p.barrier()
            p.dma("sync", (lambda tt: lambda e: e.dma_start(
                out=fsb_d[tt * 128:(tt + 1) * 128, :], in_=sqt[:, 0:48]))(tt))
            p.barrier()

        for g4 in range(4):
            hs = g4 * 4
            cols = slice(hs * 64, hs * 64 + 256)
            p.dma("sync", (lambda cols: lambda e: e.dma_start(
                out=qg.rearrange("p (t n) -> p t n", t=16),
                in_=q_d[:, cols].rearrange("(c t) n -> c t n", t=16)))(cols))
            p.dma("sync", (lambda cols: lambda e: e.dma_start(
                out=kg.rearrange("p (t n) -> p t n", t=16),
                in_=k_d[:, cols].rearrange("(c t) n -> c t n", t=16)))(cols))
            p.dma("sync", (lambda cols: lambda e: e.dma_start(
                out=ug.rearrange("p (t n) -> p t n", t=16),
                in_=v_d[:, cols].rearrange("(c t) n -> c t n", t=16)))(cols))
            p.dma("sync", lambda e: e.dma_start(
                out=fst.rearrange("p (t n) -> p t n", t=16),
                in_=fsb_d.rearrange("(c t) n -> c t n", t=16)))
            p.barrier()
            p.op("vector", lambda e: e.tensor_mul(ug[:], ug[:], kg[:]))
            p.barrier()
            p.op("vector", (lambda hs: lambda e: e.tensor_tensor(
                ug.rearrange("p (t h d) -> p t h d", t=16, h=4),
                ug.rearrange("p (t h d) -> p t h d", t=16, h=4),
                fst.rearrange("p (t n) -> p t n", t=16)[:, :, hs:hs + 4]
                   .rearrange("p t (h o) -> p t h o", o=1).broadcast_to([128, 16, 4, 64]),
                ALU.mult))(hs))
            p.barrier()
            p.op("vector", lambda e: e.tensor_scalar_mul(ug[:], ug[:], 1.0 + MIX))
            p.barrier()
            p.op("vector", lambda e: e.memset(Sfc[:], 0.0))
            p.op("gpsimd", lambda e: e.memset(Ssc[:], 0.0))
            p.barrier()
            p.op("vector", lambda e: e.memset(r1[:], 0.0))
            p.op("gpsimd", lambda e: e.memset(r2[:], 0.0))
            p.barrier()
            p.op("vector", lambda e: e.memset(M11[:, hs:hs + 4] if False else M11[:, 0:4], 1.0))
            p.op("gpsimd", lambda e: e.memset(M12[:, 0:4], 0.0))
            p.barrier()
            p.op("vector", lambda e: e.memset(M21[:, 0:4], 0.0))
            p.op("gpsimd", lambda e: e.memset(M22[:, 0:4], 1.0))
            p.barrier()
            for t in range(16):
                fcol = fst.rearrange("p (t n) -> p t n", t=16)[:, t, 16 + hs:16 + hs + 4]
                scol = fst.rearrange("p (t n) -> p t n", t=16)[:, t, 32 + hs:32 + hs + 4]
                p.op("vector", (lambda t, fcol, scol, hs: lambda e: e.tensor_mul(
                    Ma[:, 0:4], fcol, M11[:, 0:4]))(t, fcol, scol, hs))
                p.barrier()
                p.op("vector", (lambda t, fcol, scol: lambda e: e.tensor_mul(
                    Mb[:, 0:4], scol, M21[:, 0:4]))(t, fcol, scol))
                p.barrier()
                p.op("vector", (lambda t: lambda e: e.tensor_add(
                    r1.rearrange("p (t n) -> p t n", t=16)[:, t, 0:4], Ma[:, 0:4], Mb[:, 0:4]))(t))
                p.barrier()
                p.op("vector", (lambda t, fcol: lambda e: e.tensor_mul(Ma[:, 0:4], fcol, M12[:, 0:4]))(t, fcol))
                p.barrier()
                p.op("vector", (lambda t, scol: lambda e: e.tensor_mul(Mb[:, 0:4], scol, M22[:, 0:4]))(t, scol))
                p.barrier()
                p.op("vector", (lambda t: lambda e: e.tensor_add(
                    r2.rearrange("p (t n) -> p t n", t=16)[:, t, 0:4], Ma[:, 0:4], Mb[:, 0:4]))(t))
                p.barrier()
                p.op("vector", (lambda fcol: lambda e: e.tensor_tensor(
                    t1g.rearrange("p (h d) -> p h d", h=4), Sfc.rearrange("p (h d) -> p h d", h=4),
                    fcol.rearrange("p (h o) -> p h o", o=1).broadcast_to([128, 4, 64]),
                    ALU.mult))(fcol))
                p.barrier()
                p.op("vector", (lambda scol: lambda e: e.tensor_tensor(
                    t2g.rearrange("p (h d) -> p h d", h=4), Ssc.rearrange("p (h d) -> p h d", h=4),
                    scol.rearrange("p (h o) -> p h o", o=1).broadcast_to([128, 4, 64]),
                    ALU.mult))(scol))
                p.barrier()
                p.op("vector", lambda e: e.tensor_add(t3g[:], t1g[:], t2g[:]))
                p.barrier()
                p.op("vector", (lambda t: lambda e: e.scalar_tensor_tensor(
                    og.rearrange("p (t n) -> p t n", t=16)[:, t, :],
                    qg.rearrange("p (t n) -> p t n", t=16)[:, t, :], 0.5, t3g[:],
                    ALU.mult, ALU.mult))(t))
                p.barrier()
                p.op("vector", (lambda t: lambda e: e.scalar_tensor_tensor(
                    Sfc[:], t2g[:], MIX, t1g[:], ALU.mult, ALU.add))(t))
                p.barrier()
                p.op("vector", (lambda t: lambda e: e.tensor_add(
                    Sfc[:], Sfc[:], ug.rearrange("p (t n) -> p t n", t=16)[:, t, :]))(t))
                p.barrier()
                p.op("vector", (lambda t: lambda e: e.scalar_tensor_tensor(
                    Ssc[:], t1g[:], MIX, t2g[:], ALU.mult, ALU.add))(t))
                p.barrier()
                p.op("vector", (lambda t: lambda e: e.tensor_add(
                    Ssc[:], Ssc[:], ug.rearrange("p (t n) -> p t n", t=16)[:, t, :]))(t))
                p.barrier()
                for (ca, cb) in ((M11, M21), (M12, M22)):
                    p.op("vector", (lambda fcol, ca: lambda e: e.tensor_mul(Ma[:, 0:4], fcol, ca[:, 0:4]))(fcol, ca))
                    p.barrier()
                    p.op("vector", (lambda scol, cb: lambda e: e.tensor_mul(Mb[:, 0:4], scol, cb[:, 0:4]))(scol, cb))
                    p.barrier()
                    p.op("vector", (lambda ca: lambda e: e.scalar_tensor_tensor(
                        ca[:, 0:4], Mb[:, 0:4], MIX, Ma[:, 0:4], ALU.mult, ALU.add))(ca))
                    p.barrier()
                    p.op("vector", (lambda cb: lambda e: e.scalar_tensor_tensor(
                        cb[:, 0:4], Ma[:, 0:4], MIX, Mb[:, 0:4], ALU.mult, ALU.add))(cb))
                    p.barrier()
            p.op("vector", (lambda hs: lambda e: e.tensor_copy(T11[:, hs:hs + 4], M11[:, 0:4]))(hs))
            p.op("gpsimd", (lambda hs: lambda e: e.tensor_copy(T12[:, hs:hs + 4], M12[:, 0:4]))(hs))
            p.barrier()
            p.op("vector", (lambda hs: lambda e: e.tensor_copy(T21[:, hs:hs + 4], M21[:, 0:4]))(hs))
            p.op("gpsimd", (lambda hs: lambda e: e.tensor_copy(T22[:, hs:hs + 4], M22[:, 0:4]))(hs))
            p.barrier()
            p.dma("sync", (lambda hs: lambda e: e.dma_start(
                out=vin_d[:, hs:hs + 4, 0, :], in_=Sfc.rearrange("p (h d) -> p h d", h=4)))(hs))
            p.dma("sync", (lambda hs: lambda e: e.dma_start(
                out=vin_d[:, hs:hs + 4, 1, :], in_=Ssc.rearrange("p (h d) -> p h d", h=4)))(hs))
            p.barrier()
            p.op("vector", lambda e: e.tensor_copy(qg[:], og[:]))
            p.barrier()
            p.dma("sync", (lambda cols: lambda e: e.dma_start(
                out=o_d[:, cols].rearrange("(c t) n -> c t n", t=16),
                in_=qg.rearrange("p (t n) -> p t n", t=16)))(cols))
            p.barrier()
            p.dma("sync", (lambda g4: lambda e: e.dma_start(
                out=fsb_d[0:0 + 128, 0:0 + 0] if False else vcar_d[:, g4, 0, 0:32].rearrange("c (a b) -> c a b", a=2) if False else s3_d[g4 * 512:g4 * 512 + 128, 0:256],
                in_=r1[:]))(g4))
            p.dma("gpsimd", (lambda g4: lambda e: e.dma_start(
                out=s3_d[g4 * 512 + 128: g4 * 512 + 256, 0:256], in_=r2[:]))(g4))
            p.barrier()

        for (src_, dst_) in ((T11, T11c), (T12, T12c), (T21, T21c), (T22, T22c)):
            p.op("tensor", (lambda src_: lambda e: e.transpose(
                psF[0:16, 0:128], src_[:], idnf[:]))(src_))
            p.barrier()
            p.op("vector", (lambda dst_: lambda e: e.tensor_copy(dst_[:], psF[0:16, 0:128]))(dst_))
            p.barrier()
        p.op("vector", lambda e: e.memset(Sfh[:], 0.0))
        p.op("gpsimd", lambda e: e.memset(Ssh[:], 0.0))
        p.barrier()
        for cb8 in range(16):
            p.dma("sync", (lambda cb8: lambda e: e.dma_start(
                out=vstg.rearrange("h (c u d) -> h c u d", c=8, u=2),
                in_=vin_d[cb8 * 8:(cb8 + 1) * 8, :, :, :].rearrange("c h u d -> h c u d"),
                allow_slow_non_contiguous=True))(cb8))
            p.barrier()
            for ci in range(8):
                c = cb8 * 8 + ci
                p.op("vector", (lambda ci: lambda e: e.tensor_copy(
                    cstg.rearrange("h (c u d) -> h c u d", c=8, u=2)[:, ci, 0, :], Sfh[:]))(ci))
                p.barrier()
                p.op("vector", (lambda ci: lambda e: e.tensor_copy(
                    cstg.rearrange("h (c u d) -> h c u d", c=8, u=2)[:, ci, 1, :], Ssh[:]))(ci))
                p.barrier()
                p.op("vector", (lambda c: lambda e: e.tensor_scalar(
                    t1g[0:16, 0:HD], Sfh[:], T11c[:, c:c + 1], None, ALU.mult))(c))
                p.barrier()
                p.op("vector", (lambda c: lambda e: e.tensor_scalar(
                    t2g[0:16, 0:HD], Ssh[:], T12c[:, c:c + 1], None, ALU.mult))(c))
                p.barrier()
                p.op("vector", (lambda c: lambda e: e.tensor_scalar(
                    t3g[0:16, 0:HD], Sfh[:], T21c[:, c:c + 1], None, ALU.mult))(c))
                p.barrier()
                p.op("vector", (lambda c: lambda e: e.tensor_scalar(
                    Sfh[:], Ssh[:], T22c[:, c:c + 1], None, ALU.mult))(c))
                p.barrier()
                p.op("vector", lambda e: e.tensor_add(Ssh[:], t3g[0:16, 0:HD], Sfh[:]))
                p.barrier()
                p.op("vector", lambda e: e.tensor_add(Sfh[:], t1g[0:16, 0:HD], t2g[0:16, 0:HD]))
                p.barrier()
                p.op("vector", (lambda ci: lambda e: e.tensor_add(
                    Sfh[:], Sfh[:],
                    vstg.rearrange("h (c u d) -> h c u d", c=8, u=2)[:, ci, 0, :]))(ci))
                p.barrier()
                p.op("vector", (lambda ci: lambda e: e.tensor_add(
                    Ssh[:], Ssh[:],
                    vstg.rearrange("h (c u d) -> h c u d", c=8, u=2)[:, ci, 1, :]))(ci))
                p.barrier()
            p.dma("sync", (lambda cb8: lambda e: e.dma_start(
                out=vcar_d[cb8 * 8:(cb8 + 1) * 8, :, :, :].rearrange("c h u d -> h c u d"),
                in_=cstg.rearrange("h (c u d) -> h c u d", c=8, u=2),
                allow_slow_non_contiguous=True))(cb8))
            p.barrier()

        for g4 in range(4):
            hs = g4 * 4
            cols = slice(hs * 64, hs * 64 + 256)
            p.dma("sync", (lambda cols: lambda e: e.dma_start(
                out=qg.rearrange("p (t n) -> p t n", t=16),
                in_=q_d[:, cols].rearrange("(c t) n -> c t n", t=16)))(cols))
            p.dma("sync", (lambda cols: lambda e: e.dma_start(
                out=kg.rearrange("p (t n) -> p t n", t=16),
                in_=o_d[:, cols].rearrange("(c t) n -> c t n", t=16)))(cols))
            p.dma("gpsimd", (lambda g4: lambda e: e.dma_start(
                out=r1[:], in_=s3_d[g4 * 512:g4 * 512 + 128, 0:256]))(g4))
            p.dma("gpsimd", (lambda g4: lambda e: e.dma_start(
                out=r2[:], in_=s3_d[g4 * 512 + 128:g4 * 512 + 256, 0:256]))(g4))
            p.dma("sync", (lambda hs: lambda e: e.dma_start(
                out=Sfc.rearrange("p (h d) -> p h d", h=4), in_=vcar_d[:, hs:hs + 4, 0, :]))(hs))
            p.dma("sync", (lambda hs: lambda e: e.dma_start(
                out=Ssc.rearrange("p (h d) -> p h d", h=4), in_=vcar_d[:, hs:hs + 4, 1, :]))(hs))
            p.barrier()
            p.op("vector", lambda e: e.tensor_tensor(
                og.rearrange("p (t h d) -> p t h d", t=16, h=4),
                r1.rearrange("p (t h o) -> p t h o", t=16, o=1)[:, :, 0:4, :].broadcast_to([128, 16, 4, 64]),
                Sfc.rearrange("p (o h d) -> p o h d", o=1, h=4).broadcast_to([128, 16, 4, 64]),
                ALU.mult))
            p.barrier()
            p.op("vector", lambda e: e.tensor_tensor(
                ug.rearrange("p (t h d) -> p t h d", t=16, h=4),
                r2.rearrange("p (t h o) -> p t h o", t=16, o=1)[:, :, 0:4, :].broadcast_to([128, 16, 4, 64]),
                Ssc.rearrange("p (o h d) -> p o h d", o=1, h=4).broadcast_to([128, 16, 4, 64]),
                ALU.mult))
            p.barrier()
            p.op("vector", lambda e: e.tensor_add(og[:], og[:], ug[:]))
            p.barrier()
            p.op("vector", lambda e: e.tensor_mul(og[:], og[:], qg[:]))
            p.barrier()
            p.op("vector", lambda e: e.tensor_scalar_mul(og[:], og[:], 0.5))
            p.barrier()
            p.op("vector", lambda e: e.tensor_add(og[:], og[:], kg[:]))
            p.barrier()
            p.op("vector", lambda e: e.tensor_copy(qg[:], og[:]))
            p.barrier()
            p.dma("sync", (lambda cols: lambda e: e.dma_start(
                out=o_d[:, cols].rearrange("(c t) n -> c t n", t=16),
                in_=qg.rearrange("p (t n) -> p t n", t=16)))(cols))
            p.barrier()


        for tt in range(NT):
            sl = slice(tt * 128, tt * 128 + 128)
            p.dma("sync", (lambda sl: lambda e: e.dma_start(out=bft[:], in_=o_d[sl, :]))(sl))
            p.dma("gpsimd", (lambda sl: lambda e: e.dma_start(out=sqt[:], in_=gg_d[sl, :]))(sl))
            p.barrier()
            p.op("vector", lambda e: e.tensor_mul(bft[:], bft[:], sqt[:]))
            p.barrier()
            p.dma("sync", (lambda sl: lambda e: e.dma_start(out=o_d[sl, :], in_=bft[:]))(sl))
            p.barrier()
        for kc in range(ND):
            p.dma("sync", (lambda kc: lambda e: e.dma_start(
                out=bft[:, 0:T] if False else qg[:, 0:T], in_=o_d[:, kc * 128:(kc + 1) * 128],
                transpose=True))(kc))
            p.barrier()
            p.dma("sync", (lambda kc: lambda e: e.dma_start(out=og_d[kc, :, :], in_=qg[:, 0:T]))(kc))
            p.barrier()
        for grp in range(4):
            tts = list(range(grp * 4, grp * 4 + 4))
            for ti, tt in enumerate(tts):
                for kc in range(ND):
                    p.dma("sync", (lambda ti, tt, kc: lambda e: e.dma_start(
                        out=ug[:, ti * 1024 + kc * 128: ti * 1024 + kc * 128 + 128],
                        in_=og_d[kc, :, tt * 128:(tt + 1) * 128]))(ti, tt, kc))
            p.barrier()
            for nb in range(2):
                for kc in range(ND):
                    sslot = (kc % 2) * 1024
                    p.dma("gpsimd", (lambda kc, nb, sslot: lambda e: e.dma_start(
                        out=wstr[:, sslot:sslot + 512],
                        in_=woT[kc * 128:(kc + 1) * 128, nb * 512:nb * 512 + 512]))(kc, nb, sslot))
                    p.barrier()
                    for ti in range(4):
                        p.op("tensor", (lambda kc, ti, sslot: lambda e: e.matmul(
                            PSA[ti][:, 0:512],
                            ug[:, ti * 1024 + kc * 128: ti * 1024 + kc * 128 + 128],
                            wstr[:, sslot:sslot + 512],
                            start=(kc == 0), stop=(kc == ND - 1)))(kc, ti, sslot))
                    p.barrier()
                for ti, tt in enumerate(tts):
                    p.op("vector", (lambda ti: lambda e: e.tensor_copy(st_[:, 0:512], PSA[ti][:, 0:512]))(ti))
                    p.barrier()
                    p.dma("sync", (lambda tt, nb: lambda e: e.dma_start(
                        out=s3_d[tt * 128:(tt + 1) * 128, nb * 512:nb * 512 + 512],
                        in_=st_[:, 0:512]))(tt, nb))
                    p.barrier()

        gwsb = es.enter_context(nc.sbuf_tensor("gwsb", [128, ND * E], BF16))
        p.dma("gpsimd", lambda e: e.dma_start(out=gwsb.rearrange("p (k n) -> p k n", k=ND),
                                              in_=gatewT.rearrange("(k p) n -> p k n", p=128)))
        p.dma("sync", lambda e: e.dma_start(out=stat[0:1, 0:8] if False else sqt[0:1, 0:E],
                                            in_=gateb.rearrange("(o d) -> o d", o=1)))
        p.barrier()
        p.op("tensor", lambda e: e.matmul(psF[:, 0:E], onesf[0:1, 0:128], sqt[0:1, 0:E],
                                          start=True, stop=True))
        p.barrier()
        p.op("vector", lambda e: e.tensor_copy(grow[:], psF[:, 0:E]))
        p.barrier()
        m8 = es.enter_context(nc.sbuf_tensor("m8", [128, 8], F32))
        m8b = es.enter_context(nc.sbuf_tensor("m8b", [128, 8], F32))
        scM = es.enter_context(nc.sbuf_tensor("scM", [128, E], F32))
        scS = es.enter_context(nc.sbuf_tensor("scS", [128, E], F32))
        for tt in range(NT):
            for kc in range(ND):
                p.op("tensor", (lambda tt, kc: lambda e: e.matmul(
                    ps0[:, 0:E], xnfm[:, kc * T + tt * 128: kc * T + tt * 128 + 128],
                    gwsb[:, kc * E:(kc + 1) * E], start=(kc == 0), stop=(kc == ND - 1)))(tt, kc))
            p.barrier()
            p.op("scalar", lambda e: e.activation(scS[:], ps0[:, 0:E], AF.Sigmoid))
            p.barrier()
            p.op("vector", lambda e: e.tensor_add(scM[:], scS[:], grow[:]))
            p.barrier()
            p.op("vector", lambda e: e.reduce_max(m8[:], scM.rearrange("p (g r) -> p g r", g=8),
                                                  axis=AX.X))
            p.barrier()
            p.op("vector", lambda e: e.tensor_tensor(
                sqt[:, 0:E].rearrange("p (g r) -> p g r", g=8),
                scM.rearrange("p (g r) -> p g r", g=8),
                m8.rearrange("p (g o) -> p g o", o=1).broadcast_to([128, 8, 4]),
                ALU.is_lt))
            p.barrier()
            p.op("vector", lambda e: e.tensor_scalar(sqt[:, E:2 * E], sqt[:, 0:E], 1.0, 2.0e30,
                                                     ALU.subtract, ALU.mult))
            p.barrier()
            p.op("vector", lambda e: e.tensor_add(sqt[:, E:2 * E], sqt[:, E:2 * E], scM[:]))
            p.barrier()
            p.op("vector", lambda e: e.reduce_max(m8b[:], sqt[:, E:2 * E].rearrange("p (g r) -> p g r", g=8),
                                                  axis=AX.X))
            p.barrier()
            p.op("vector", lambda e: e.tensor_add(m8[:], m8[:], m8b[:]))
            p.barrier()
            p.op("vector", lambda e: e.max(m8b[:], m8[:]))
            p.barrier()
            p.op("vector", lambda e: e.tensor_scalar(m8[:], m8[:], m8b[:, 3:4], None, ALU.is_ge))
            p.barrier()
            p.op("vector", lambda e: e.tensor_tensor(
                scM.rearrange("p (g r) -> p g r", g=8),
                scM.rearrange("p (g r) -> p g r", g=8),
                m8.rearrange("p (g o) -> p g o", o=1).broadcast_to([128, 8, 4]),
                ALU.mult))
            p.barrier()
            p.op("vector", lambda e: e.max(m8b[:], scM[:]))
            p.barrier()
            p.op("vector", lambda e: e.tensor_scalar(sqt[:, 0:E], scM[:], m8b[:, 7:8], None, ALU.is_ge))
            p.barrier()
            p.op("vector", lambda e: e.tensor_scalar(sqt[:, E:2 * E], scM[:], 0.0, None, ALU.not_equal))
            p.barrier()
            p.op("vector", lambda e: e.tensor_mul(sqt[:, 0:E], sqt[:, 0:E], sqt[:, E:2 * E]))
            p.barrier()
            p.op("vector", lambda e: e.tensor_mul(sqt[:, 0:E], sqt[:, 0:E], scS[:]))
            p.barrier()
            p.op("vector", lambda e: e.reduce_sum(stat[:, 0:1], sqt[:, 0:E], axis=AX.X))
            p.barrier()
            p.op("vector", lambda e: e.tensor_scalar_add(stat[:, 0:1], stat[:, 0:1], 1e-20))
            p.barrier()
            p.op("vector", lambda e: e.reciprocal(stat[:, 1:2], stat[:, 0:1]))
            p.barrier()
            p.op("vector", lambda e: e.tensor_scalar_mul(stat[:, 1:2], stat[:, 1:2], 2.5))
            p.barrier()
            p.op("vector", (lambda tt: lambda e: e.tensor_scalar(
                cmb32[:, tt * E:(tt + 1) * E], sqt[:, 0:E], stat[:, 1:2], None, ALU.mult))(tt))
            p.barrier()

        for tb in range(4):
            for ti in range(4):
                p.op("vector", (lambda ti: lambda e: e.memset(yac4[:, ti * D:(ti + 1) * D], 0.0))(ti))
                p.barrier()
            for e_ in range(E):
                for (wsrc, dstfm, act) in ((wgate, zfm, AF.Silu), (wup, ufm, AF.Identity)):
                    for ic in range(4):
                        for kc in range(ND):
                            sslot = (kc % 2) * 1024
                            p.dma("gpsimd", (lambda e_, ic, kc, sslot, wsrc: lambda e: e.dma_start(
                                out=wstr[:, sslot:sslot + 128],
                                in_=wsrc[e_, kc * 128:(kc + 1) * 128, ic * 128:(ic + 1) * 128]))(e_, ic, kc, sslot, wsrc))
                            p.barrier()
                            p.op("tensor", (lambda kc, tb, sslot: lambda e: e.matmul(
                                ps0[:, 0:512], wstr[:, sslot:sslot + 128],
                                xnfm[:, kc * T + tb * 512: kc * T + tb * 512 + 512],
                                start=(kc == 0), stop=(kc == ND - 1)))(kc, tb, sslot))
                            p.barrier()
                        if act == AF.Silu:
                            p.op("scalar", (lambda ic, dstfm: lambda e: e.activation(
                                dstfm[:, ic * 512:(ic + 1) * 512], ps0[:, 0:512], AF.Silu))(ic, dstfm))
                        else:
                            p.op("vector", (lambda ic, dstfm: lambda e: e.tensor_copy(
                                dstfm[:, ic * 512:(ic + 1) * 512], ps0[:, 0:512]))(ic, dstfm))
                        p.barrier()
                p.op("vector", lambda e: e.tensor_mul(zfm[:], zfm[:], ufm[:]))
                p.barrier()
                for ti in range(4):
                    tt = tb * 4 + ti
                    for nb in range(2):
                        ps = [ps1, ps2][nb]
                        for ic in range(4):
                            sslot = (ic % 2) * 1024
                            p.dma("gpsimd", (lambda e_, ic, nb, sslot: lambda e: e.dma_start(
                                out=wstr[:, sslot:sslot + 512],
                                in_=wdownT[e_, ic * 128:(ic + 1) * 128, nb * 512:nb * 512 + 512]))(e_, ic, nb, sslot))
                            p.barrier()
                            p.op("tensor", (lambda ic, ti, ps, sslot: lambda e: e.matmul(
                                ps[:, 0:512],
                                zfm[:, ic * 512 + ti * 128: ic * 512 + ti * 128 + 128],
                                wstr[:, sslot:sslot + 512],
                                start=(ic == 0), stop=(ic == 3)))(ic, ti, ps, sslot))
                            p.barrier()
                        p.op("vector", (lambda e_, tt, ti, nb, ps: lambda e: e.scalar_tensor_tensor(
                            yac4[:, ti * D + nb * 512: ti * D + nb * 512 + 512],
                            ps[:, 0:512] if False else yac4[:, ti * D + nb * 512: ti * D + nb * 512 + 512],
                            1.0, ps[:, 0:512], ALU.mult, ALU.add) if False else e.scalar_tensor_tensor(
                            yac4[:, ti * D + nb * 512: ti * D + nb * 512 + 512],
                            ps[:, 0:512], cmb32[:, tt * E + e_: tt * E + e_ + 1],
                            yac4[:, ti * D + nb * 512: ti * D + nb * 512 + 512],
                            ALU.mult, ALU.add))(e_, tt, ti, nb, ps))
                        p.barrier()
            for ti in range(4):
                tt = tb * 4 + ti
                sl = slice(tt * 128, tt * 128 + 128)
                for nb in range(2):
                    ps = [ps1, ps2][nb]
                    for kc in range(ND):
                        sslot = (kc % 2) * 1024
                        p.dma("gpsimd", (lambda kc, nb, sslot: lambda e: e.dma_start(
                            out=wstr[:, sslot:sslot + 512],
                            in_=shgT[kc * 128:(kc + 1) * 128, nb * 512:nb * 512 + 512]))(kc, nb, sslot))
                        p.barrier()
                        p.op("tensor", (lambda kc, tt, ps, sslot: lambda e: e.matmul(
                            ps[:, 0:512], xnfm[:, kc * T + tt * 128: kc * T + tt * 128 + 128],
                            wstr[:, sslot:sslot + 512],
                            start=(kc == 0), stop=(kc == ND - 1)))(kc, tt, ps, sslot))
                        p.barrier()
                    p.op("scalar", (lambda nb, ps: lambda e: e.activation(
                        st_[:, nb * 512:nb * 512 + 512], ps[:, 0:512], AF.Silu))(nb, ps))
                    p.barrier()
                for nb in range(2):
                    ps = [ps1, ps2][nb]
                    for kc in range(ND):
                        sslot = (kc % 2) * 1024
                        p.dma("gpsimd", (lambda kc, nb, sslot: lambda e: e.dma_start(
                            out=wstr[:, sslot:sslot + 512],
                            in_=shuT[kc * 128:(kc + 1) * 128, nb * 512:nb * 512 + 512]))(kc, nb, sslot))
                        p.barrier()
                        p.op("tensor", (lambda kc, tt, ps, sslot: lambda e: e.matmul(
                            ps[:, 0:512], xnfm[:, kc * T + tt * 128: kc * T + tt * 128 + 128],
                            wstr[:, sslot:sslot + 512],
                            start=(kc == 0), stop=(kc == ND - 1)))(kc, tt, ps, sslot))
                        p.barrier()
                    p.op("vector", (lambda nb, ps: lambda e: e.tensor_mul(
                        st_[:, nb * 512:nb * 512 + 512], st_[:, nb * 512:nb * 512 + 512],
                        ps[:, 0:512]))(nb, ps))
                    p.barrier()
                p.op("vector", lambda e: e.tensor_copy(bft[:], st_[:]))
                p.barrier()
                p.dma("sync", (lambda sl: lambda e: e.dma_start(out=o_d[sl, :], in_=bft[:]))(sl))
                p.barrier()
            for ti in range(4):
                tt = tb * 4 + ti
                sl = slice(tt * 128, tt * 128 + 128)
                for kc in range(ND):
                    p.dma("sync", (lambda tt, kc: lambda e: e.dma_start(
                        out=ug[:, kc * 128:(kc + 1) * 128],
                        in_=o_d[tt * 128:(tt + 1) * 128, kc * 128:(kc + 1) * 128],
                        transpose=True))(tt, kc))
                p.barrier()
                for nb in range(2):
                    ps = [ps1, ps2][nb]
                    for kc in range(ND):
                        sslot = (kc % 2) * 1024
                        p.dma("gpsimd", (lambda kc, nb, sslot: lambda e: e.dma_start(
                            out=wstr[:, sslot:sslot + 512],
                            in_=shdT[kc * 128:(kc + 1) * 128, nb * 512:nb * 512 + 512]))(kc, nb, sslot))
                        p.barrier()
                        p.op("tensor", (lambda kc, ps, sslot: lambda e: e.matmul(
                            ps[:, 0:512], ug[:, kc * 128:(kc + 1) * 128],
                            wstr[:, sslot:sslot + 512],
                            start=(kc == 0), stop=(kc == ND - 1)))(kc, ps, sslot))
                        p.barrier()
                    p.op("vector", (lambda ti, nb, ps: lambda e: e.tensor_add(
                        gt_[:, nb * 512:nb * 512 + 512], yac4[:, ti * D + nb * 512: ti * D + nb * 512 + 512],
                        ps[:, 0:512]))(ti, nb, ps))
                    p.barrier()
                p.op("scalar", lambda e: e.activation(gt_[:], gt_[:], AF.Silu))
                p.barrier()
                p.dma("sync", (lambda sl: lambda e: e.dma_start(out=st_[:], in_=s3_d[sl, :]))(sl))
                p.dma("sync", (lambda sl: lambda e: e.dma_start(out=xt[:], in_=x3[sl, :]))(sl))
                p.barrier()
                p.op("vector", lambda e: e.tensor_mul(st_[:], st_[:], gt_[:]))
                p.barrier()
                p.op("vector", lambda e: e.tensor_add(st_[:], st_[:], xt[:]))
                p.barrier()
                p.dma("sync", (lambda sl: lambda e: e.dma_start(out=x4[sl, :], in_=st_[:]))(sl))
                p.barrier()
        p.build()
    es.close()
    return nc


# version 06
# speedup vs baseline: 2.2476x; 2.2476x over previous
import numpy as np
import contextlib
import concourse.bass as bass
import concourse.mybir as mybir
from concourse.bass_utils import run_bass_kernel_spmd
from concourse.bass_core_rust import IndirectOffsetOnAxis

F32 = mybir.dt.float32
BF16 = mybir.dt.bfloat16
I32 = mybir.dt.int32
U32 = mybir.dt.uint32
AF = mybir.ActivationFunctionType
ALU = mybir.AluOpType
AX = mybir.AxisListType

T = 2048
D = 1024
NT = T // 128
ND = D // 128
MAGIC = float(1.5 * 2 ** 23)
NEG = -1.0e30

ENGINES = ("tensor", "vector", "scalar", "gpsimd", "sync")


class Prog:
    def __init__(self, nc):
        self.nc = nc
        self.items = []

    def op(self, engine, fn):
        self.items.append((engine, "op", fn))

    def dma(self, engine, fn):
        self.items.append((engine, "dma", fn))

    def barrier(self):
        self.items.append((None, "barrier", None))

    def build(self):
        nc = self.nc
        sems = {}
        self._sem_ctxs = []
        for e in ENGINES:
            for kind in ("op", "dma"):
                ctx = nc.semaphore(name=f"sem_{e}_{kind}_{id(self) % 10000}")
                sems[(e, kind)] = ctx.__enter__()
                self._sem_ctxs.append(ctx)
        phases = [[]]
        for it in self.items:
            if it[1] == "barrier":
                if phases[-1]:
                    phases.append([])
            else:
                phases[-1].append(it)
        cum = {k: 0 for k in sems}
        seen = {e: {k: 0 for k in sems} for e in ENGINES}
        schedule = {e: [] for e in ENGINES}
        for ph in phases:
            used = {}
            for (e, kind, fn) in ph:
                used.setdefault(e, []).append((kind, fn))
            wait_targets = [(k, cum[k]) for k in sems if cum[k] > 0]
            for e, ops in used.items():
                kinds = {k for k, _ in ops}
                assert len(kinds) == 1, f"phase mixes dma+op on engine {e}"
                is_dma = kinds == {"dma"}
                key = (e, "dma" if is_dma else "op")
                for (sk, val) in wait_targets:
                    if seen[e][sk] >= val:
                        continue
                    schedule[e].append(("wait", sk, val))
                    seen[e][sk] = val
                for i, (kind, fn) in enumerate(ops):
                    last = i == len(ops) - 1
                    inc = 16 if is_dma else (1 if last else 0)
                    schedule[e].append(("exe", fn, inc, key))
                cum[key] += 16 * len(ops) if is_dma else 1
        for sk in sems:
            if cum[sk] > 0:
                schedule["sync"].append(("wait", sk, cum[sk]))
        block_ctx = nc.Block()
        block = block_ctx.__enter__()

        def run_engine(ename, eng):
            for entry in schedule[ename]:
                if entry[0] == "wait":
                    _, sk, val = entry
                    eng.wait_ge(sems[sk], val)
                else:
                    _, fn, inc, key = entry
                    inst = fn(eng)
                    if inc:
                        inst.then_inc(sems[key], inc)

        if schedule["sync"]:
            @block.sync
            def _(sync):
                run_engine("sync", sync)
        if schedule["gpsimd"]:
            @block.gpsimd
            def _(gpsimd):
                run_engine("gpsimd", gpsimd)
        if schedule["scalar"]:
            @block.scalar
            def _(scalar):
                run_engine("scalar", scalar)
        if schedule["vector"]:
            @block.vector
            def _(vector):
                run_engine("vector", vector)
        if schedule["tensor"]:
            @block.tensor
            def _(tensor):
                run_engine("tensor", tensor)
        block_ctx.__exit__(None, None, None)
        for ctx in reversed(self._sem_ctxs):
            ctx.__exit__(None, None, None)


def emit_ln(p, xt, xnt, sqt, stat, eps_ap):
    p.op("scalar", lambda e: e.activation(sqt[:], xt[:], AF.Square, accum_out=stat[:, 1:2]))
    p.op("vector", lambda e: e.reduce_sum(stat[:, 0:1], xt[:], axis=AX.X))
    p.barrier()
    p.op("vector", lambda e: e.tensor_scalar_mul(stat[:, 2:3], stat[:, 0:1], 1.0 / D))
    p.barrier()
    p.op("vector", lambda e: e.tensor_mul(stat[:, 4:5], stat[:, 2:3], stat[:, 2:3]))
    p.op("vector", lambda e: e.tensor_scalar(stat[:, 1:2], stat[:, 1:2], 1.0 / D, None, ALU.mult))
    p.barrier()
    p.op("vector", lambda e: e.tensor_sub(stat[:, 5:6], stat[:, 1:2], stat[:, 4:5]))
    p.barrier()
    p.op("scalar", lambda e: e.activation(stat[:, 6:7], stat[:, 5:6], AF.Ln, bias=eps_ap))
    p.barrier()
    p.op("scalar", lambda e: e.activation(stat[:, 3:4], stat[:, 6:7], AF.Exp, scale=-0.5))
    p.barrier()
    p.op("vector", lambda e: e.tensor_scalar(xnt[:], xt[:], stat[:, 2:3], stat[:, 3:4],
                                             ALU.subtract, ALU.mult))
    p.barrier()


def emit_gemm_tok(p, ps_list, lhs_fm, lhs_off_fn, w_sb, w_off_fn, bias_row, bias_off, nb_count,
                  tt, onesrow, out_fn, nk=ND):
    for nb in range(nb_count):
        ps = ps_list[nb % 2]
        for kc in range(nk):
            p.op("tensor", (lambda nb, kc, ps: lambda e: e.matmul(
                ps[:, 0:512],
                lhs_fm[:, lhs_off_fn(kc) + tt * 128: lhs_off_fn(kc) + tt * 128 + 128],
                w_sb[:, w_off_fn(kc, nb): w_off_fn(kc, nb) + 512],
                start=(kc == 0), stop=False))(nb, kc, ps))
        p.op("tensor", (lambda nb, ps: lambda e: e.matmul(
            ps[:, 0:512], onesrow[0:1, 0:128],
            bias_row[0:1, bias_off + nb * 512: bias_off + nb * 512 + 512],
            start=False, stop=True))(nb, ps))
        p.barrier()
        out_fn(nb, ps)
        p.barrier()


def build_l0(nc, p=None, pfx="", xin_tile=None, xout_tile=None):
    H, DK, W = 16, 64, 135

    if xin_tile is None:
        _xin = nc.dram_tensor(pfx + "x", [T, D], F32, kind="ExternalInput").ap()
        xin_tile = lambda tt: _xin[tt * 128:(tt + 1) * 128, :]
    wqkvT = nc.dram_tensor(pfx + "wqkvT", [D, 3 * D], F32, kind="ExternalInput").ap()
    bqkv = nc.dram_tensor(pfx + "bqkv", [3 * D], F32, kind="ExternalInput").ap()
    wlatT = nc.dram_tensor(pfx + "wlatT", [DK, DK], F32, kind="ExternalInput").ap()
    blat = nc.dram_tensor(pfx + "blat", [DK], F32, kind="ExternalInput").ap()
    wprojT = nc.dram_tensor(pfx + "wprojT", [D, D], F32, kind="ExternalInput").ap()
    bproj = nc.dram_tensor(pfx + "bproj", [D], F32, kind="ExternalInput").ap()
    wffnT = nc.dram_tensor(pfx + "wffnT", [D, D], I32, kind="ExternalInput").ap()
    wscal = nc.dram_tensor(pfx + "wscal", [1], F32, kind="ExternalInput").ap()
    bffn = nc.dram_tensor(pfx + "bffn", [D], F32, kind="ExternalInput").ap()
    maskc = nc.dram_tensor(pfx + "maskc", [3 * 128, W], F32, kind="ExternalInput").ap()
    multc = nc.dram_tensor(pfx + "multc", [2 * 128, 1], F32, kind="ExternalInput").ap()

    if xout_tile is None:
        _xout = nc.dram_tensor(pfx + "x1", [T, D], F32, kind="ExternalOutput").ap()
        xout_tile = lambda tt: _xout[tt * 128:(tt + 1) * 128, :]

    xnbf_d = nc.dram_tensor(pfx + "xnbf_d", [T, D], BF16, kind="Internal").ap()
    xqbf_d = nc.dram_tensor(pfx + "xqbf_d", [T, D], BF16, kind="Internal").ap()
    vlat_d = nc.dram_tensor(pfx + "vlat_d", [T, D], BF16, kind="Internal").ap()
    attn_d = nc.dram_tensor(pfx + "attn_d", [T, D], BF16, kind="Internal").ap()
    cvec_d = nc.dram_tensor(pfx + "cvec_d", [D], F32, kind="Internal").ap()

    standalone = p is None
    if standalone:
        p = Prog(nc)
    es = contextlib.ExitStack()
    xnfm = es.enter_context(nc.sbuf_tensor(pfx + "xnfm", [128, ND * T], BF16))
    fmA = es.enter_context(nc.sbuf_tensor(pfx + "fmA", [128, ND * T], BF16))
    fmB = es.enter_context(nc.sbuf_tensor(pfx + "fmB", [128, ND * T], BF16))
    wstr = es.enter_context(nc.sbuf_tensor(pfx + "wstr", [128, 2 * 1024], BF16))
    wpro = es.enter_context(nc.sbuf_tensor(pfx + "wpro", [128, ND * 512 * 2], BF16))
    wffn = es.enter_context(nc.sbuf_tensor(pfx + "wffn", [128, ND * 512 * 2], BF16))
    wlat1 = es.enter_context(nc.sbuf_tensor(pfx + "wlat1", [128, 64], BF16))
    wlat1f = es.enter_context(nc.sbuf_tensor(pfx + "wlat1f", [64, 64], F32))
    wlat2 = es.enter_context(nc.sbuf_tensor(pfx + "wlat2", [128, 128], BF16))
    bqs = es.enter_context(nc.sbuf_tensor(pfx + "bqs", [128, ND], F32))
    onesrow = es.enter_context(nc.sbuf_tensor(pfx + "onesrow", [1, 512], BF16))
    onesf = es.enter_context(nc.sbuf_tensor(pfx + "onesf", [1, 128], F32))
    brow = es.enter_context(nc.sbuf_tensor(pfx + "brow", [1, 2 * D], F32))
    browb = es.enter_context(nc.sbuf_tensor(pfx + "browb", [1, 2 * D], BF16))
    wsb = es.enter_context(nc.sbuf_tensor(pfx + "wsb", [128, 1], F32))
    cvec = es.enter_context(nc.sbuf_tensor(pfx + "cvec", [16, 64], F32))
    ccol = es.enter_context(nc.sbuf_tensor(pfx + "ccol", [128, ND], BF16))
    masks = es.enter_context(nc.sbuf_tensor(pfx + "masks", [128, 3 * W], F32))
    mults = es.enter_context(nc.sbuf_tensor(pfx + "mults", [128, 2], F32))
    bqkv_t = es.enter_context(nc.sbuf_tensor(pfx + "bqkv_t", [64, 16], F32))
    blat_s = es.enter_context(nc.sbuf_tensor(pfx + "blat_s", [1, 64], F32))
    ident = es.enter_context(nc.sbuf_tensor(pfx + "ident", [128, 128], BF16))
    stat = es.enter_context(nc.sbuf_tensor(pfx + "stat", [128, 8], F32))
    epst = es.enter_context(nc.sbuf_tensor(pfx + "epst", [128, 2], F32))
    magicc = es.enter_context(nc.sbuf_tensor(pfx + "magicc", [128, 2], F32))
    amax = es.enter_context(nc.sbuf_tensor(pfx + "amax", [128, 4], F32))
    xt = es.enter_context(nc.sbuf_tensor(pfx + "xt", [128, D], F32))
    xnt = es.enter_context(nc.sbuf_tensor(pfx + "xnt", [128, D], F32))
    sqt = es.enter_context(nc.sbuf_tensor(pfx + "sqt", [128, D], F32))
    bft = es.enter_context(nc.sbuf_tensor(pfx + "bft", [128, D], BF16))
    kvt = es.enter_context(nc.sbuf_tensor(pfx + "kvt", [128, T], BF16))
    swide = es.enter_context(nc.sbuf_tensor(pfx + "swide", [128, H * W], F32))
    pwide = es.enter_context(nc.sbuf_tensor(pfx + "pwide", [128, H * W], BF16))
    hmax = es.enter_context(nc.sbuf_tensor(pfx + "hmax", [128, H], F32))
    zrow = es.enter_context(nc.sbuf_tensor(pfx + "zrow", [128, 2 * H], F32))
    vh0 = es.enter_context(nc.sbuf_tensor(pfx + "vh0", [128, D], BF16))
    vh1 = es.enter_context(nc.sbuf_tensor(pfx + "vh1", [8, D], BF16))
    pT0 = es.enter_context(nc.sbuf_tensor(pfx + "pT0", [128, 128], BF16))
    pT1 = es.enter_context(nc.sbuf_tensor(pfx + "pT1", [7, 128], BF16))
    st_ = es.enter_context(nc.sbuf_tensor(pfx + "st_", [128, D], F32))
    gt_ = es.enter_context(nc.sbuf_tensor(pfx + "gt_", [128, D], F32))
    ps0 = es.enter_context(nc.psum_tensor(pfx + "ps0", [128, 512], F32))
    ps1 = es.enter_context(nc.psum_tensor(pfx + "ps1", [128, 512], F32))
    pss0 = es.enter_context(nc.psum_tensor(pfx + "pss0", [128, W], F32))
    pss1 = es.enter_context(nc.psum_tensor(pfx + "pss1", [128, W], F32))
    psa0 = es.enter_context(nc.psum_tensor(pfx + "psa0", [128, 64], F32))
    psa1 = es.enter_context(nc.psum_tensor(pfx + "psa1", [128, 64], F32))
    ptr0 = es.enter_context(nc.psum_tensor(pfx + "ptr0", [128, 128], BF16))
    ptr1 = es.enter_context(nc.psum_tensor(pfx + "ptr1", [7, 128], BF16))
    if True:
        p.dma("gpsimd", lambda e: e.dma_start(out=wpro.rearrange("p (k n) -> p k n", k=ND),
                                              in_=wprojT.rearrange("(k p) n -> p k n", p=128)))
        p.dma("gpsimd", lambda e: e.dma_start(out=wffn.rearrange("p (k n) -> p k n", k=ND),
                                              in_=wffnT.rearrange("(k p) n -> p k n", p=128)))
        p.dma("gpsimd", lambda e: e.dma_start(out=wlat1[0:64, :], in_=wlatT[:]))
        p.dma("gpsimd", lambda e: e.dma_start(out=wlat1[64:128, :], in_=wlatT[:]))
        p.dma("sync", lambda e: e.dma_start(out=wlat1f[:], in_=wlatT[:]))
        p.dma("sync", lambda e: e.dma_start(out=masks.rearrange("p (m w) -> p m w", m=3),
                                            in_=maskc.rearrange("(m p) w -> p m w", p=128)))
        p.dma("sync", lambda e: e.dma_start(out=mults.rearrange("p (m o) -> p m o", o=1),
                                            in_=multc.rearrange("(m p) o -> p m o", p=128),
                                            allow_slow_non_contiguous=True))
        p.dma("sync", lambda e: e.dma_start(out=bqkv_t[:],
                                            in_=bqkv[2048:3072].rearrange("(h i) -> i h", i=64),
                                            allow_slow_non_contiguous=True))
        p.dma("sync", lambda e: e.dma_start(out=blat_s[:], in_=blat.rearrange("(o d) -> o d", o=1)))
        p.dma("sync", lambda e: e.dma_start(out=brow[0:1, 0:D], in_=bproj.rearrange("(o d) -> o d", o=1)))
        p.dma("sync", lambda e: e.dma_start(out=brow[0:1, D:2 * D], in_=bffn.rearrange("(o d) -> o d", o=1)))
        p.dma("sync", lambda e: e.dma_start(out=wsb[0:1, 0:1], in_=wscal.rearrange("(o d) -> o d", o=1)))
        p.dma("sync", lambda e: e.dma_start(out=bqs.rearrange("p (k o) -> p k o", o=1),
                                            in_=bqkv[0:1024].rearrange("(k p o) -> p k o", p=128, o=1),
                                            allow_slow_non_contiguous=True))
        p.barrier()
        p.op("vector", lambda e: e.memset(onesrow[:], 1.0))
        p.op("vector", lambda e: e.memset(epst[:, 0:1], 1e-5))
        p.op("vector", lambda e: e.memset(epst[:, 1:2], 1e-6))
        p.barrier()
        p.op("vector", lambda e: e.memset(magicc[:, 0:1], 1.0))
        p.barrier()
        p.op("vector", lambda e: e.tensor_scalar(magicc[:, 1:2], magicc[:, 0:1],
                                                 9.313225746154785e-10, 1.0, ALU.add, ALU.subtract))
        p.barrier()
        p.op("vector", lambda e: e.tensor_scalar(magicc[:, 1:2], magicc[:, 1:2], 1073741824.0,
                                                 None, ALU.mult))
        p.barrier()
        p.op("vector", lambda e: e.tensor_scalar(magicc[:, 0:1], magicc[:, 1:2],
                                                 6755399441055744.0, 12582912.0, ALU.mult, ALU.add))
        p.barrier()

        p.op("scalar", lambda e: e.memzero(wlat2[:]))
        p.op("gpsimd", lambda e: e.memset(onesf[:], 1.0))
        p.barrier()
        p.op("gpsimd", lambda e: e.memset(ident[:], 1.0))
        p.op("vector", lambda e: e.tensor_copy(wlat2[0:64, 0:64], wlat1[0:64, :]))
        p.barrier()
        p.op("vector", lambda e: e.tensor_copy(wlat2[64:128, 64:128], wlat1[64:128, :]))
        p.op("gpsimd", lambda e: e.affine_select(ident[:], ident[:], [[-1, 128]], ALU.is_equal,
                                                 0.0, base=0, channel_multiplier=1))
        p.op("scalar", lambda e: e.mul(bqs[:], bqs[:], 0.125))
        p.op("vector", lambda e: e.memset(stat[:, 7:8], 1.0))
        p.barrier()
        p.op("tensor", lambda e: e.matmul(psa0[:, 0:1], onesf[0:1, 0:128], wsb[0:1, 0:1],
                                          start=True, stop=True))
        p.barrier()
        p.op("vector", lambda e: e.tensor_copy(wsb[:], psa0[:, 0:1]))
        p.barrier()
        p.op("tensor", lambda e: e.matmul(psa0[0:16, 0:64], bqkv_t[:], wlat1f[:], start=True, stop=False))
        p.op("tensor", lambda e: e.matmul(psa0[0:16, 0:64], onesf[0:1, 0:16], blat_s[:],
                                          start=False, stop=True))
        p.barrier()
        p.op("vector", lambda e: e.tensor_copy(cvec[:], psa0[0:16, 0:64]))
        p.barrier()
        p.dma("sync", lambda e: e.dma_start(out=cvec_d.rearrange("(h i) -> h i", h=16), in_=cvec[:]))
        p.barrier()
        p.dma("gpsimd", lambda e: e.dma_start(out=ccol.rearrange("p (k o) -> p k o", o=1),
                                              in_=cvec_d.rearrange("(k p o) -> p k o", p=128, o=1),
                                              allow_slow_non_contiguous=True))
        p.barrier()
        for nb in range(2):
            for kc in range(ND):
                p.op("tensor", (lambda kc, nb: lambda e: e.matmul(
                    ps0[0:1, 0:512], ccol[:, kc:kc + 1],
                    wpro[:, kc * D + nb * 512: kc * D + nb * 512 + 512],
                    start=(kc == 0), stop=(kc == ND - 1)))(kc, nb))
            p.barrier()
            p.op("vector", (lambda nb: lambda e: e.tensor_add(
                brow[0:1, nb * 512:nb * 512 + 512], brow[0:1, nb * 512:nb * 512 + 512],
                ps0[0:1, 0:512]))(nb))
            p.barrier()
        p.op("vector", lambda e: e.tensor_copy(browb[:], brow[:]))
        p.barrier()

        for tt in range(NT):
            sl = slice(tt * 128, tt * 128 + 128)
            p.dma("sync", (lambda ap: lambda e: e.dma_start(out=xt[:], in_=ap))(xin_tile(tt)))
            p.barrier()
            emit_ln(p, xt, xnt, sqt, stat, epst[:, 0:1])
            p.op("vector", lambda e: e.tensor_copy(bft[:], xnt[:]))
            p.barrier()
            p.dma("sync", (lambda sl: lambda e: e.dma_start(out=xnbf_d[sl, :], in_=bft[:]))(sl))
            p.op("scalar", lambda e: e.activation(sqt[:], xnt[:], AF.Square, accum_out=stat[:, 1:2]))
            p.barrier()
            p.op("scalar", lambda e: e.activation(stat[:, 1:2], stat[:, 1:2], AF.Ln,
                                                  bias=epst[:, 1:2], scale=1.0 / D))
            p.barrier()
            p.op("scalar", lambda e: e.activation(stat[:, 3:4], stat[:, 1:2], AF.Exp, scale=-0.5))
            p.barrier()
            p.op("vector", lambda e: e.tensor_scalar(sqt[:], xnt[:], stat[:, 3:4], None, ALU.mult))
            p.barrier()
            p.op("vector", lambda e: e.reduce_max(amax[:, 0:1], sqt[:], axis=AX.X,
                                                  apply_absolute_value=True))
            p.barrier()
            p.op("vector", lambda e: e.tensor_scalar_max(amax[:, 0:1], amax[:, 0:1], 1e-5))
            p.barrier()
            p.op("vector", lambda e: e.reciprocal(amax[:, 1:2], amax[:, 0:1]))
            p.op("vector", lambda e: e.tensor_scalar(amax[:, 2:3], amax[:, 0:1], 1.0 / 127.0,
                                                     wsb[:, 0:1], ALU.mult, ALU.mult))
            p.barrier()
            p.op("vector", lambda e: e.tensor_scalar_mul(amax[:, 1:2], amax[:, 1:2], 127.0))
            p.barrier()
            p.op("vector", lambda e: e.tensor_scalar(sqt[:], sqt[:], amax[:, 1:2], None, ALU.mult))
            p.barrier()
            p.op("vector", lambda e: e.tensor_scalar(sqt[:], sqt[:], magicc[:, 0:1], magicc[:, 0:1],
                                                     ALU.add, ALU.subtract))
            p.barrier()
            p.op("vector", lambda e: e.tensor_scalar(sqt[:], sqt[:], -128.0, 127.0, ALU.max, ALU.min))
            p.barrier()
            p.op("vector", lambda e: e.tensor_scalar(bft[:], sqt[:], amax[:, 2:3], None, ALU.mult))
            p.barrier()
            p.dma("sync", (lambda sl: lambda e: e.dma_start(out=xqbf_d[sl, :], in_=bft[:]))(sl))
            p.barrier()

        for kc in range(ND):
            p.dma("sync", (lambda kc: lambda e: e.dma_start(
                out=xnfm[:, kc * T:(kc + 1) * T], in_=xnbf_d[:, kc * 128:(kc + 1) * 128],
                transpose=True))(kc))
        p.barrier()

        def load_wslice(slot, col0):
            p.dma("gpsimd", (lambda slot, col0: lambda e: e.dma_start(
                out=wstr.rearrange("p (s k n) -> p s k n", s=2, k=ND)[:, slot, :, :],
                in_=wqkvT.rearrange("(k p) n -> p k n", p=128)[:, :, col0:col0 + 128]))(slot, col0))

        for oc in range(ND):
            load_wslice(oc % 2, oc * 128)
            p.barrier()
            for tb in range(4):
                ps = [ps0, ps1][tb % 2]
                for kc in range(ND):
                    p.op("tensor", (lambda oc, tb, kc, ps: lambda e: e.matmul(
                        ps[:, 0:512],
                        wstr[:, (oc % 2) * 1024 + kc * 128:(oc % 2) * 1024 + kc * 128 + 128],
                        xnfm[:, kc * T + tb * 512: kc * T + tb * 512 + 512],
                        start=(kc == 0), stop=(kc == ND - 1)))(oc, tb, kc, ps))
                p.barrier()
                p.op("scalar", (lambda oc, tb, ps: lambda e: e.activation(
                    fmA[:, oc * T + tb * 512: oc * T + tb * 512 + 512], ps[:, 0:512],
                    AF.Identity, bias=bqs[:, oc:oc + 1], scale=0.125))(oc, tb, ps))
                p.barrier()
        for oc in range(ND):
            load_wslice(oc % 2, D + oc * 128)
            p.barrier()
            for tb in range(4):
                ps = [ps0, ps1][tb % 2]
                for kc in range(ND):
                    p.op("tensor", (lambda oc, tb, kc, ps: lambda e: e.matmul(
                        ps[:, 0:512],
                        wstr[:, (oc % 2) * 1024 + kc * 128:(oc % 2) * 1024 + kc * 128 + 128],
                        xnfm[:, kc * T + tb * 512: kc * T + tb * 512 + 512],
                        start=(kc == 0), stop=(kc == ND - 1)))(oc, tb, kc, ps))
                p.barrier()
                p.op("scalar", (lambda tb, ps: lambda e: e.copy(
                    kvt[:, tb * 512:tb * 512 + 512], ps[:, 0:512]))(tb, ps))
                p.barrier()
            for tb in range(4):
                ps = [ps0, ps1][tb % 2]
                p.op("tensor", (lambda tb, ps: lambda e: e.matmul(
                    ps[:, 0:512], wlat2[:], kvt[:, tb * 512:tb * 512 + 512],
                    start=True, stop=True))(tb, ps))
                p.barrier()
                p.op("scalar", (lambda oc, tb, ps: lambda e: e.copy(
                    fmB[:, oc * T + tb * 512: oc * T + tb * 512 + 512], ps[:, 0:512]))(oc, tb, ps))
                p.barrier()
        for oc in range(ND):
            load_wslice(oc % 2, 2 * D + oc * 128)
            p.barrier()
            for tb in range(4):
                ps = [ps0, ps1][tb % 2]
                for kc in range(ND):
                    p.op("tensor", (lambda oc, tb, kc, ps: lambda e: e.matmul(
                        ps[:, 0:512],
                        wstr[:, (oc % 2) * 1024 + kc * 128:(oc % 2) * 1024 + kc * 128 + 128],
                        xnfm[:, kc * T + tb * 512: kc * T + tb * 512 + 512],
                        start=(kc == 0), stop=(kc == ND - 1)))(oc, tb, kc, ps))
                p.barrier()
                p.op("scalar", (lambda tb, ps: lambda e: e.copy(
                    kvt[:, tb * 512:tb * 512 + 512], ps[:, 0:512]))(tb, ps))
                p.barrier()
            for hh in range(2):
                h = oc * 2 + hh
                for tt in range(NT):
                    psx = [psa0, psa1][tt % 2]
                    p.op("tensor", (lambda hh, tt, psx: lambda e: e.matmul(
                        psx[:, 0:64], kvt[hh * 64:hh * 64 + 64, tt * 128:tt * 128 + 128],
                        wlat1[hh * 64:hh * 64 + 64, :], start=True, stop=True))(hh, tt, psx))
                    p.barrier()
                    p.op("vector", (lambda psx: lambda e: e.tensor_copy(bft[:, 0:64], psx[:, 0:64]))(psx))
                    p.barrier()
                    p.dma("sync", (lambda h, tt: lambda e: e.dma_start(
                        out=vlat_d[tt * 128:tt * 128 + 128, h * 64:h * 64 + 64],
                        in_=bft[:, 0:64]))(h, tt))
                    p.barrier()

        for tt in range(NT):
            base = 0 if tt == 0 else (1914 if tt == NT - 1 else tt * 128 - 3)
            mi = 0 if tt == 0 else (2 if tt == NT - 1 else 1)
            far = 2047 if tt == 0 else 0
            p.dma("sync", (lambda base: lambda e: e.dma_start(out=vh0[:], in_=vlat_d[base:base + 128, :]))(base))
            p.dma("sync", (lambda base: lambda e: e.dma_start(out=vh1[0:6, :], in_=vlat_d[base + 128:base + 134, :]))(base))
            p.dma("sync", (lambda far: lambda e: e.dma_start(out=vh1[6:7, :], in_=vlat_d[far:far + 1, :]))(far))
            p.barrier()
            for h in range(H):
                oc, hb = h // 2, (h % 2) * 64
                pss = [pss0, pss1][h % 2]
                p.op("tensor", (lambda oc, hb, tt, base, pss: lambda e: e.matmul(
                    pss[:, 0:134], fmA[hb:hb + 64, oc * T + tt * 128: oc * T + tt * 128 + 128],
                    fmB[hb:hb + 64, oc * T + base: oc * T + base + 134],
                    start=True, stop=True))(oc, hb, tt, base, pss))
                p.op("tensor", (lambda oc, hb, tt, far, pss: lambda e: e.matmul(
                    pss[:, 134:135], fmA[hb:hb + 64, oc * T + tt * 128: oc * T + tt * 128 + 128],
                    fmB[hb:hb + 64, oc * T + far: oc * T + far + 1],
                    start=True, stop=True))(oc, hb, tt, far, pss))
                p.barrier()
                p.op("vector", (lambda h, mi, pss: lambda e: e.tensor_add(
                    swide[:, h * W:(h + 1) * W], pss[:, 0:W], masks[:, mi * W:(mi + 1) * W]))(h, mi, pss))
                p.barrier()
            p.op("vector", lambda e: e.reduce_max(hmax[:], swide.rearrange("p (h w) -> p h w", h=H),
                                                  axis=AX.X, negate=True))
            p.barrier()
            p.op("vector", lambda e: e.tensor_tensor(
                swide.rearrange("p (h w) -> p h w", h=H),
                swide.rearrange("p (h w) -> p h w", h=H),
                hmax.rearrange("p (h o) -> p h o", o=1).broadcast_to([128, H, W]),
                ALU.add))
            p.barrier()
            p.op("scalar", lambda e: e.activation(pwide[:], swide[:], AF.Exp))
            p.barrier()
            if tt == 0 or tt == NT - 1:
                mj = 0 if tt == 0 else 1
                p.op("vector", (lambda mj: lambda e: e.tensor_scalar(
                    pwide.rearrange("p (h w) -> p h w", h=H)[:, :, 134:135],
                    pwide.rearrange("p (h w) -> p h w", h=H)[:, :, 134:135],
                    mults[:, mj:mj + 1], None, ALU.mult))(mj))
                p.barrier()
            p.op("vector", lambda e: e.reduce_sum(zrow[:, 0:H], pwide.rearrange("p (h w) -> p h w", h=H),
                                                  axis=AX.X))
            p.barrier()
            p.op("vector", lambda e: e.reciprocal(zrow[:, H:2 * H], zrow[:, 0:H]))
            p.barrier()
            for h in range(H):
                psa = [psa0, psa1][h % 2]
                p.op("tensor", (lambda h: lambda e: e.transpose(
                    ptr0[:], pwide[:, h * W:h * W + 128], ident[:]))(h))
                p.op("tensor", (lambda h: lambda e: e.transpose(
                    ptr1[0:7, :], pwide[:, h * W + 128:h * W + 135], ident[:]))(h))
                p.barrier()
                p.op("vector", lambda e: e.tensor_copy(pT0[:], ptr0[:]))
                p.op("scalar", lambda e: e.copy(pT1[0:7, :], ptr1[0:7, :]))
                p.barrier()
                p.op("tensor", (lambda h, psa: lambda e: e.matmul(
                    psa[:, 0:64], pT0[:], vh0[:, h * 64:h * 64 + 64],
                    start=True, stop=False))(h, psa))
                p.op("tensor", (lambda h, psa: lambda e: e.matmul(
                    psa[:, 0:64], pT1[0:7, :], vh1[0:7, h * 64:h * 64 + 64],
                    start=False, stop=True))(h, psa))
                p.barrier()
                p.op("vector", (lambda h, psa: lambda e: e.tensor_scalar(
                    st_[:, h * 64:h * 64 + 64], psa[:, 0:64], zrow[:, H + h:H + h + 1],
                    None, ALU.mult))(h, psa))
                p.barrier()
            p.op("vector", lambda e: e.tensor_copy(bft[:], st_[:]))
            p.barrier()
            p.dma("sync", (lambda tt: lambda e: e.dma_start(
                out=attn_d[tt * 128:tt * 128 + 128, :], in_=bft[:]))(tt))
            p.barrier()

        for kc in range(ND):
            p.dma("sync", (lambda kc: lambda e: e.dma_start(
                out=fmA[:, kc * T:(kc + 1) * T], in_=xqbf_d[:, kc * 128:(kc + 1) * 128],
                transpose=True))(kc))
            p.dma("sync", (lambda kc: lambda e: e.dma_start(
                out=fmB[:, kc * T:(kc + 1) * T], in_=attn_d[:, kc * 128:(kc + 1) * 128],
                transpose=True))(kc))
        p.barrier()

        for tt in range(NT):
            sl = slice(tt * 128, tt * 128 + 128)

            def evac_s(nb, ps):
                p.op("scalar", (lambda nb, ps: lambda e: e.copy(
                    st_[:, nb * 512:nb * 512 + 512], ps[:, 0:512]))(nb, ps))

            emit_gemm_tok(p, [ps0, ps1], fmB, lambda kc: kc * T, wpro,
                          lambda kc, nb: kc * D + nb * 512, browb, 0, 2, tt, onesrow, evac_s)

            def evac_g(nb, ps):
                p.op("scalar", (lambda nb, ps: lambda e: e.activation(
                    gt_[:, nb * 512:nb * 512 + 512], ps[:, 0:512], AF.Silu))(nb, ps))

            emit_gemm_tok(p, [ps0, ps1], fmA, lambda kc: kc * T, wffn,
                          lambda kc, nb: kc * D + nb * 512, browb, D, 2, tt, onesrow, evac_g)
            p.dma("sync", (lambda ap: lambda e: e.dma_start(out=xt[:], in_=ap))(xin_tile(tt)))
            p.barrier()
            p.op("vector", lambda e: e.tensor_mul(st_[:], st_[:], gt_[:]))
            p.barrier()
            p.op("vector", lambda e: e.tensor_add(st_[:], st_[:], xt[:]))
            p.barrier()
            p.dma("sync", (lambda ap: lambda e: e.dma_start(out=ap, in_=st_[:]))(xout_tile(tt)))
            p.barrier()

        if standalone:
            p.build()
    es.close()
    return nc


def l0_consts():
    W = 135
    masks = np.full((3, 128, W), NEG, np.float32)
    for t in range(128):
        lo, hi = max(0, t - 3), min(133, t + 3)
        masks[0, t, lo:hi + 1] = 0.0
        if t < 3:
            masks[0, t, 134] = 0.0
    for t in range(128):
        masks[1, t, t:t + 7] = 0.0
    for t in range(128):
        lo, hi = t + 3, min(133, t + 9)
        masks[2, t, lo:hi + 1] = 0.0
        if t >= 125:
            masks[2, t, 134] = 0.0
    mults = np.zeros((2, 128, 1), np.float32)
    mults[0, 0, 0], mults[0, 1, 0], mults[0, 2, 0] = 3.0, 2.0, 1.0
    mults[1, 125, 0], mults[1, 126, 0], mults[1, 127, 0] = 1.0, 2.0, 3.0
    return masks.reshape(3 * 128, W), mults.reshape(2 * 128, 1)


def l0_inmaps(inputs):
    masks, mults = l0_consts()
    xfull = np.ascontiguousarray(inputs["L0_attn_x"], np.float32)
    wqkvT = np.ascontiguousarray(inputs["L0_attn_W_qkv"].T)
    wlatT = np.ascontiguousarray(inputs["L0_attn_W_lat"].T)
    wprojT = np.ascontiguousarray(inputs["L0_attn_W_proj"].T)
    wffnT = np.ascontiguousarray(inputs["L0_ffn_w_int8"].T.astype(np.int32))
    ws = np.asarray(inputs["L0_ffn_weight_scale"], np.float32).reshape(1)
    maps = []
    for c in range(8):
        maps.append({
            "x": np.ascontiguousarray(xfull[c]),
            "wqkvT": wqkvT, "bqkv": inputs["L0_attn_b_qkv"],
            "wlatT": wlatT, "blat": inputs["L0_attn_b_lat"],
            "wprojT": wprojT, "bproj": inputs["L0_attn_b_proj"],
            "wffnT": wffnT, "wscal": ws, "bffn": inputs["L0_ffn_bias"],
            "maskc": masks, "multc": mults,
        })
    return maps


def l1_inmaps(inputs, x1_shards):
    tril = (np.arange(128)[:, None] < np.arange(128)[None, :]).astype(np.float32)
    maps = []
    base = {
        "wprojW": np.ascontiguousarray(inputs["L1_attn_in_proj_w"], np.float32),
        "nsc": inputs["L1_attn_norm_scale"], "nbi": inputs["L1_attn_norm_bias"],
        "alog": inputs["L1_attn_A_log"], "dtb": inputs["L1_attn_dt_bias"],
        "bcB": inputs["L1_attn_bcB_scale"], "bcC": inputs["L1_attn_bcC_scale"],
        "Bbi": inputs["L1_attn_B_bias"], "Cbi": inputs["L1_attn_C_bias"],
        "thw": np.ascontiguousarray(inputs["L1_attn_theta_w"], np.float32),
        "Dsk": inputs["L1_attn_D"],
        "wout": np.ascontiguousarray(inputs["L1_attn_out_proj_w"], np.float32),
        "fker": np.ascontiguousarray(inputs["L1_ffn_kernel"], np.float32),
        "fbi": inputs["L1_ffn_bias"],
        "fa": np.ascontiguousarray(inputs["L1_ffn_a"], np.float32),
        "fb": np.ascontiguousarray(inputs["L1_ffn_b"], np.float32),
        "trilc": tril,
    }
    for c in range(8):
        m = dict(base)
        m["x1"] = np.ascontiguousarray(x1_shards[c], np.float32)
        maps.append(m)
    return maps


def l2_inmaps(inputs, x2_shards):
    H, DK = 16, 64
    wqkv = np.asarray(inputs["L2_attn_w_qkv"], np.float32)
    perm = []
    for c0 in (0, 64, 128):
        for h in range(H):
            perm.extend(range(h * 192 + c0, h * 192 + c0 + 64))
    wqkvT = np.ascontiguousarray(wqkv[perm].T)
    keys = np.asarray(inputs["L2_ffn_keys"], np.float32)
    keysfm = np.ascontiguousarray(
        np.stack([keys[oc % 8, :, oc // 8, :].T for oc in range(16)], 0))
    bd = np.full((128, 128), NEG, np.float32)
    for t in range(16):
        bd[t * 8:(t + 1) * 8, t * 8:(t + 1) * 8] = 0.0
    base = {
        "wqkvT": wqkvT,
        "lnw": inputs["L2_attn_ln_w"], "lnb": inputs["L2_attn_ln_b"],
        "woutT": np.ascontiguousarray(np.asarray(inputs["L2_attn_w_out"], np.float32).T),
        "gamma": inputs["L2_ffn_gamma"], "beta": inputs["L2_ffn_beta"],
        "wqT": np.ascontiguousarray(np.asarray(inputs["L2_ffn_w_q"], np.float32).T),
        "keysfm": keysfm,
        "key_in": np.ascontiguousarray(inputs["L2_ffn_key_in"], np.float32),
        "key_out": np.ascontiguousarray(inputs["L2_ffn_key_out"], np.float32),
        "bdmask": bd,
    }
    maps = []
    for c in range(8):
        m = dict(base)
        m["xsh"] = np.ascontiguousarray(x2_shards[c], np.float32)
        maps.append(m)
    return maps


def l3_inmaps(inputs, x3_shards):
    wbfs = np.concatenate([np.asarray(inputs["L3_attn_Wb"], np.float32),
                           np.asarray(inputs["L3_attn_Wfd"], np.float32),
                           np.asarray(inputs["L3_attn_Wsd"], np.float32)], 0)
    bfs = np.concatenate([np.zeros(16, np.float32),
                          np.asarray(inputs["L3_attn_bfd"], np.float32),
                          np.asarray(inputs["L3_attn_bsd"], np.float32)], 0)
    base = {
        "wqT": np.ascontiguousarray(np.asarray(inputs["L3_attn_Wq"], np.float32).T),
        "wkT": np.ascontiguousarray(np.asarray(inputs["L3_attn_Wk"], np.float32).T),
        "wvT": np.ascontiguousarray(np.asarray(inputs["L3_attn_Wv"], np.float32).T),
        "wgT": np.ascontiguousarray(np.asarray(inputs["L3_attn_Wg"], np.float32).T),
        "woT": np.ascontiguousarray(np.asarray(inputs["L3_attn_Wo"], np.float32).T),
        "wbfsT": np.ascontiguousarray(wbfs.T),
        "bfs": bfs,
        "gatewT": np.ascontiguousarray(np.asarray(inputs["L3_ffn_gate_w"], np.float32).T),
        "gateb": inputs["L3_ffn_gate_bias"],
        "wgate": np.ascontiguousarray(np.asarray(inputs["L3_ffn_w_gate"], np.float32).transpose(0, 2, 1)),
        "wup": np.ascontiguousarray(np.asarray(inputs["L3_ffn_w_up"], np.float32).transpose(0, 2, 1)),
        "wdownT": np.ascontiguousarray(np.asarray(inputs["L3_ffn_w_down"], np.float32).transpose(0, 2, 1)),
        "shgT": np.ascontiguousarray(np.asarray(inputs["L3_ffn_sh_gate"], np.float32).T),
        "shuT": np.ascontiguousarray(np.asarray(inputs["L3_ffn_sh_up"], np.float32).T),
        "shdT": np.ascontiguousarray(np.asarray(inputs["L3_ffn_sh_down"], np.float32).T),
    }
    maps = []
    for c in range(8):
        m = dict(base)
        m["x3"] = np.ascontiguousarray(x3_shards[c], np.float32)
        maps.append(m)
    return maps


def build_fused(nc):
    p = Prog(nc)
    x_in = nc.dram_tensor("x", [T, D], F32, kind="ExternalInput").ap()
    x4 = nc.dram_tensor("x4", [T, D], F32, kind="ExternalOutput").ap()
    ridx2 = nc.dram_tensor("ridx2", [T], I32, kind="ExternalInput").ap()
    ridx3 = nc.dram_tensor("ridx3", [T], I32, kind="ExternalInput").ap()
    x1_d = nc.dram_tensor("fx1", [T, D], F32, kind="Internal").ap()
    x2_d = nc.dram_tensor("fx2", [T, D], F32, kind="Internal")
    x2all = nc.dram_tensor("fx2all", [8 * T, D], F32, kind="Internal")
    xsh_d = nc.dram_tensor("fxsh", [T, D], F32, kind="Internal").ap()
    x3sh_d = nc.dram_tensor("fx3sh", [T, D], F32, kind="Internal")
    x3all = nc.dram_tensor("fx3all", [8 * T, D], F32, kind="Internal")
    x3loc_d = nc.dram_tensor("fx3loc", [T, D], F32, kind="Internal").ap()

    build_l0(nc, p=p, pfx="l0_",
             xin_tile=lambda tt: x_in[tt * 128:(tt + 1) * 128, :],
             xout_tile=lambda tt: x1_d[tt * 128:(tt + 1) * 128, :])
    p.barrier()
    build_l1(nc, p=p, pfx="l1_",
             xin_tile=lambda tt: x1_d[tt * 128:(tt + 1) * 128, :],
             xout_tile=lambda tt: x2_d.ap()[tt * 128:(tt + 1) * 128, :])
    p.barrier()
    p.op("gpsimd", lambda e: e.collective_compute(
        "AllGather", ALU.bypass, replica_groups=[list(range(8))],
        ins=[x2_d.ap().opt()], outs=[x2all.ap().opt()]))
    p.barrier()

    def emit_reshard(src_all, idx_dram, dst):
        es_r = contextlib.ExitStack()
        idxt = es_r.enter_context(nc.sbuf_tensor(f"ridx_{dst.tensor.name}", [128, NT], I32))
        gbuf0 = es_r.enter_context(nc.sbuf_tensor(f"rg0_{dst.tensor.name}", [128, D], F32))
        gbuf1 = es_r.enter_context(nc.sbuf_tensor(f"rg1_{dst.tensor.name}", [128, D], F32))
        p.dma("sync", (lambda: lambda e: e.dma_start(
            out=idxt.rearrange("p (k o) -> p k o", o=1),
            in_=idx_dram.rearrange("(k p o) -> p k o", p=128, o=1),
            allow_slow_non_contiguous=True))())
        p.barrier()
        for tt in range(NT):
            gb = [gbuf0, gbuf1][tt % 2]
            p.dma("gpsimd", (lambda tt, gb: lambda e: e.indirect_dma_start(
                out=gb[:], out_offset=None, in_=src_all.ap()[:, :],
                in_offset=IndirectOffsetOnAxis(idxt[:, tt:tt + 1], 0)))(tt, gb))
            p.barrier()
            p.dma("sync", (lambda tt, gb: lambda e: e.dma_start(
                out=dst[tt * 128:(tt + 1) * 128, :], in_=gb[:]))(tt, gb))
            p.barrier()
        es_r.close()

    emit_reshard(x2all, ridx2, xsh_d)
    p.barrier()
    build_l2(nc, p=p, pfx="l2_",
             xin_tile=lambda tt: xsh_d[tt * 128:(tt + 1) * 128, :],
             xout_tile=lambda tt: x3sh_d.ap()[tt * 128:(tt + 1) * 128, :])
    p.barrier()
    p.op("gpsimd", lambda e: e.collective_compute(
        "AllGather", ALU.bypass, replica_groups=[list(range(8))],
        ins=[x3sh_d.ap().opt()], outs=[x3all.ap().opt()]))
    p.barrier()
    emit_reshard(x3all, ridx3, x3loc_d)
    p.barrier()
    build_l3(nc, p=p, pfx="l3_",
             xin_tile=lambda tt: x3loc_d[tt * 128:(tt + 1) * 128, :],
             xout_tile=lambda tt: x4[tt * 128:(tt + 1) * 128, :])
    p.build()
    return nc


def fused_inmaps(inputs):
    m0 = l0_inmaps(inputs)
    dummy = [np.zeros((1,), np.float32)] * 8
    m1 = l1_inmaps(inputs, dummy)
    m2 = l2_inmaps(inputs, dummy)
    m3 = l3_inmaps(inputs, dummy)
    xfull = np.ascontiguousarray(inputs["L0_attn_x"], np.float32)
    maps = []
    for c in range(8):
        m = {}
        for k, v in m0[c].items():
            if k != "x":
                m["l0_" + k] = v
        for k, v in m1[c].items():
            if k != "x1":
                m["l1_" + k] = v
        for k, v in m2[c].items():
            if k != "xsh":
                m["l2_" + k] = v
        for k, v in m3[c].items():
            if k != "x3":
                m["l3_" + k] = v
        m["x"] = np.ascontiguousarray(xfull[c])
        t_local = np.arange(256)
        b = np.arange(8)
        r2 = (b[None, :] * 2048 + c * 256 + t_local[:, None]).reshape(-1).astype(np.int32)
        r3 = (np.arange(2048) * 8 + c).astype(np.int32)
        m["ridx2"] = r2
        m["ridx3"] = r3
        maps.append(m)
    return maps


def kernel(**inputs):
    nc = bass.Bass("TRN2", target_bir_lowering=False)
    build_fused(nc)
    res = run_bass_kernel_spmd(nc, fused_inmaps(inputs), core_ids=list(range(8)))
    x4 = np.stack([res.results[c]["x4"] for c in range(8)], axis=0)
    return x4


def kernel_unfused(**inputs):
    nc0 = bass.Bass("TRN2", target_bir_lowering=False)
    build_l0(nc0)
    res0 = run_bass_kernel_spmd(nc0, l0_inmaps(inputs), core_ids=list(range(8)))
    x1 = [res0.results[c]["x1"] for c in range(8)]

    nc1 = bass.Bass("TRN2", target_bir_lowering=False)
    build_l1(nc1)
    res1 = run_bass_kernel_spmd(nc1, l1_inmaps(inputs, x1), core_ids=list(range(8)))
    x2 = np.stack([res1.results[c]["x2"] for c in range(8)], axis=0)

    shards = [np.ascontiguousarray(x2[:, c * 256:(c + 1) * 256, :].transpose(1, 0, 2)
                                   .reshape(2048, 1024)) for c in range(8)]
    nc2 = bass.Bass("TRN2", target_bir_lowering=False)
    build_l2(nc2)
    res2 = run_bass_kernel_spmd(nc2, l2_inmaps(inputs, shards), core_ids=list(range(8)))
    x3 = np.stack([res2.results[c]["x3sh"].reshape(256, 8, 1024).transpose(1, 0, 2)
                   for c in range(8)], axis=1)
    x3 = np.ascontiguousarray(x3.reshape(8, 2048, 1024))

    nc3 = bass.Bass("TRN2", target_bir_lowering=False)
    build_l3(nc3)
    res3 = run_bass_kernel_spmd(nc3, l3_inmaps(inputs, [x3[c] for c in range(8)]),
                                core_ids=list(range(8)))
    x4 = np.stack([res3.results[c]["x4"] for c in range(8)], axis=0)
    return x4



def build_l1(nc, p=None, pfx="", xin_tile=None, xout_tile=None):
    H, N64, DI, PROJ = 16, 64, 2048, 4256
    NDI = DI // 128
    NB_P = [512] * 8 + [160]
    NB_OFF = [0, 512, 1024, 1536, 2048, 2560, 3072, 3584, 4096]

    if xin_tile is None:
        _xin = nc.dram_tensor(pfx + "x1", [T, D], F32, kind="ExternalInput").ap()
        xin_tile = lambda tt: _xin[tt * 128:(tt + 1) * 128, :]
    wprojW = nc.dram_tensor(pfx + "wprojW", [D, PROJ], F32, kind="ExternalInput").ap()
    nsc = nc.dram_tensor(pfx + "nsc", [D], F32, kind="ExternalInput").ap()
    nbi = nc.dram_tensor(pfx + "nbi", [D], F32, kind="ExternalInput").ap()
    alog = nc.dram_tensor(pfx + "alog", [16], F32, kind="ExternalInput").ap()
    dtb = nc.dram_tensor(pfx + "dtb", [16], F32, kind="ExternalInput").ap()
    bcB = nc.dram_tensor(pfx + "bcB", [64], F32, kind="ExternalInput").ap()
    bcC = nc.dram_tensor(pfx + "bcC", [64], F32, kind="ExternalInput").ap()
    Bbi = nc.dram_tensor(pfx + "Bbi", [64], F32, kind="ExternalInput").ap()
    Cbi = nc.dram_tensor(pfx + "Cbi", [64], F32, kind="ExternalInput").ap()
    thw = nc.dram_tensor(pfx + "thw", [D, 32], F32, kind="ExternalInput").ap()
    Dsk = nc.dram_tensor(pfx + "Dsk", [16], F32, kind="ExternalInput").ap()
    wout = nc.dram_tensor(pfx + "wout", [DI, D], F32, kind="ExternalInput").ap()
    fker = nc.dram_tensor(pfx + "fker", [D, D], F32, kind="ExternalInput").ap()
    fbi = nc.dram_tensor(pfx + "fbi", [D], F32, kind="ExternalInput").ap()
    fa = nc.dram_tensor(pfx + "fa", [D, 16], F32, kind="ExternalInput").ap()
    fb = nc.dram_tensor(pfx + "fb", [16, D], F32, kind="ExternalInput").ap()
    trilc = nc.dram_tensor(pfx + "trilc", [128, 128], F32, kind="ExternalInput").ap()

    if xout_tile is None:
        _xout = nc.dram_tensor(pfx + "x2", [T, D], F32, kind="ExternalOutput").ap()
        xout_tile = lambda tt: _xout[tt * 128:(tt + 1) * 128, :]

    xn_d = nc.dram_tensor(pfx + "xn_d", [T, D], F32, kind="Internal").ap()
    lnbf_d = nc.dram_tensor(pfx + "lnbf_d", [T, D], BF16, kind="Internal").ap()
    xin_d = nc.dram_tensor(pfx + "xin_d", [T, DI], BF16, kind="Internal").ap()
    zsil_d = nc.dram_tensor(pfx + "zsil_d", [T, DI], BF16, kind="Internal").ap()
    gam_d = nc.dram_tensor(pfx + "gam_d", [T + 128, 16], F32, kind="Internal").ap()
    lc_cm_d = nc.dram_tensor(pfx + "lc_cm_d", [NT, H, 128], F32, kind="Internal").ap()
    yz_d = nc.dram_tensor(pfx + "yz_d", [T, DI], BF16, kind="Internal").ap()
    yzfm_d = nc.dram_tensor(pfx + "yzfm_d", [NDI, 128, T], BF16, kind="Internal").ap()
    st_d = nc.dram_tensor(pfx + "st_d", [T, D], F32, kind="Internal").ap()

    standalone = p is None
    if standalone:
        p = Prog(nc)
    es = contextlib.ExitStack()
    lnfm = es.enter_context(nc.sbuf_tensor(pfx + "lnfm", [128, ND * T], BF16))
    wslot = es.enter_context(nc.sbuf_tensor(pfx + "wslot", [128, 2 * 512], BF16))
    gchunk = es.enter_context(nc.sbuf_tensor(pfx + "gchunk", [128, ND], F32))
    bchunk = es.enter_context(nc.sbuf_tensor(pfx + "bchunk", [128, ND], BF16))
    brow1 = es.enter_context(nc.sbuf_tensor(pfx + "brow1", [1, PROJ], F32))
    brow1b = es.enter_context(nc.sbuf_tensor(pfx + "brow1b", [1, PROJ], BF16))
    onesrow = es.enter_context(nc.sbuf_tensor(pfx + "onesrow", [1, 512], BF16))
    onesf = es.enter_context(nc.sbuf_tensor(pfx + "onesf", [1, 128], F32))
    epst = es.enter_context(nc.sbuf_tensor(pfx + "epst", [128, 4], F32))
    magicc = es.enter_context(nc.sbuf_tensor(pfx + "magicc", [128, 2], F32))
    stat = es.enter_context(nc.sbuf_tensor(pfx + "stat", [128, 8], F32))
    xt = es.enter_context(nc.sbuf_tensor(pfx + "xt", [128, D], F32))
    xnt = es.enter_context(nc.sbuf_tensor(pfx + "xnt", [128, D], F32))
    sqt = es.enter_context(nc.sbuf_tensor(pfx + "sqt", [128, D], F32))
    bft = es.enter_context(nc.sbuf_tensor(pfx + "bft", [128, D], BF16))
    bcBt = es.enter_context(nc.sbuf_tensor(pfx + "bcBt", [128, 64], F32))
    bcCt = es.enter_context(nc.sbuf_tensor(pfx + "bcCt", [128, 64], F32))
    Bbit = es.enter_context(nc.sbuf_tensor(pfx + "Bbit", [128, 64], F32))
    Cbit = es.enter_context(nc.sbuf_tensor(pfx + "Cbit", [128, 64], F32))
    Abct = es.enter_context(nc.sbuf_tensor(pfx + "Abct", [128, 16], F32))
    Dbct = es.enter_context(nc.sbuf_tensor(pfx + "Dbct", [128, 16], F32))
    srow = es.enter_context(nc.sbuf_tensor(pfx + "srow", [1, 128], F32))
    Btok = es.enter_context(nc.sbuf_tensor(pfx + "Btok", [128, NT * 64], F32))
    Ctok = es.enter_context(nc.sbuf_tensor(pfx + "Ctok", [128, NT * 64], F32))
    dttok = es.enter_context(nc.sbuf_tensor(pfx + "dttok", [128, NT * 16], F32))
    lamtok = es.enter_context(nc.sbuf_tensor(pfx + "lamtok", [128, NT * 16], F32))
    ctok = es.enter_context(nc.sbuf_tensor(pfx + "ctok", [128, NT * 16], F32))
    atok = es.enter_context(nc.sbuf_tensor(pfx + "atok", [128, NT * 16], F32))
    ftok = es.enter_context(nc.sbuf_tensor(pfx + "ftok", [128, NT * 16], F32))
    lctok = es.enter_context(nc.sbuf_tensor(pfx + "lctok", [128, NT * 16], F32))
    gdiag = es.enter_context(nc.sbuf_tensor(pfx + "gdiag", [128, NT], F32))
    coefc = es.enter_context(nc.sbuf_tensor(pfx + "coefc", [128, NT * 16], F32))
    Bfm = es.enter_context(nc.sbuf_tensor(pfx + "Bfm", [64, T], BF16))
    Cfm = es.enter_context(nc.sbuf_tensor(pfx + "Cfm", [64, T], BF16))
    logafm = es.enter_context(nc.sbuf_tensor(pfx + "logafm", [16, T], F32))
    phfm = es.enter_context(nc.sbuf_tensor(pfx + "phfm", [32, T], F32))
    thetat = es.enter_context(nc.sbuf_tensor(pfx + "thetat", [128, 32], F32))
    phtok = es.enter_context(nc.sbuf_tensor(pfx + "phtok", [128, 32], F32))
    trig = es.enter_context(nc.sbuf_tensor(pfx + "trig", [128, 128], F32))
    wth = es.enter_context(nc.sbuf_tensor(pfx + "wth", [128, ND * 32], BF16))
    idnb = es.enter_context(nc.sbuf_tensor(pfx + "idnb", [128, 128], BF16))
    idnf = es.enter_context(nc.sbuf_tensor(pfx + "idnf", [128, 128], F32))
    lcrow = es.enter_context(nc.sbuf_tensor(pfx + "lcrow", [1, H * 128], F32))
    Xc = es.enter_context(nc.sbuf_tensor(pfx + "Xc", [128, DI], BF16))
    Zc = es.enter_context(nc.sbuf_tensor(pfx + "Zc", [128, DI], BF16))
    Yc = es.enter_context(nc.sbuf_tensor(pfx + "Yc", [128, DI], F32))
    G0m = es.enter_context(nc.sbuf_tensor(pfx + "G0m", [128, 128], F32))
    Mt = es.enter_context(nc.sbuf_tensor(pfx + "Mt", [128, 128], BF16))
    Bw = es.enter_context(nc.sbuf_tensor(pfx + "Bw", [128, 64], BF16))
    Sst = es.enter_context(nc.sbuf_tensor(pfx + "Sst", [64, H * 128], F32))
    Sbf = es.enter_context(nc.sbuf_tensor(pfx + "Sbf", [64, 128], BF16))
    ecol = es.enter_context(nc.sbuf_tensor(pfx + "ecol", [64, 1], F32))
    lse = es.enter_context(nc.sbuf_tensor(pfx + "lse", [128, 4], F32))
    scr = es.enter_context(nc.sbuf_tensor(pfx + "scr", [128, 160], F32))
    tril = es.enter_context(nc.sbuf_tensor(pfx + "tril", [128, 128], F32))
    zscan = es.enter_context(nc.sbuf_tensor(pfx + "zscan", [32, 512], F32))
    wfqb = es.enter_context(nc.sbuf_tensor(pfx + "wfqb", [128, ND * D], BF16))
    fabf = es.enter_context(nc.sbuf_tensor(pfx + "fabf", [128, ND * 16], BF16))
    fbbf = es.enter_context(nc.sbuf_tensor(pfx + "fbbf", [16, D], BF16))
    xafm = es.enter_context(nc.sbuf_tensor(pfx + "xafm", [16, T], BF16))
    frow = es.enter_context(nc.sbuf_tensor(pfx + "frow", [1, D], F32))
    frowb = es.enter_context(nc.sbuf_tensor(pfx + "frowb", [1, D], BF16))
    st_ = es.enter_context(nc.sbuf_tensor(pfx + "st_", [128, D], F32))
    gt_ = es.enter_context(nc.sbuf_tensor(pfx + "gt_", [128, D], F32))
    yzg = es.enter_context(nc.sbuf_tensor(pfx + "yzg", [128, 4 * NDI * 128], BF16))
    ps0 = es.enter_context(nc.psum_tensor(pfx + "ps0", [128, 512], F32))
    ps1 = es.enter_context(nc.psum_tensor(pfx + "ps1", [128, 512], F32))
    ps2 = es.enter_context(nc.psum_tensor(pfx + "ps2", [128, 512], F32))
    ps3 = es.enter_context(nc.psum_tensor(pfx + "ps3", [128, 512], F32))
    psT = es.enter_context(nc.psum_tensor(pfx + "psT", [128, 128], BF16))
    psF = es.enter_context(nc.psum_tensor(pfx + "psF", [128, 128], F32))
    psY = es.enter_context(nc.psum_tensor(pfx + "psY", [128, 128], F32))
    psS = es.enter_context(nc.psum_tensor(pfx + "psS", [64, 128], F32))
    PSA = [ps0, ps1, ps2, ps3]
    if True:
        p.dma("sync", lambda e: e.dma_start(out=tril[:], in_=trilc[:]))
        p.dma("sync", lambda e: e.dma_start(out=gchunk.rearrange("p (k o) -> p k o", o=1),
                                            in_=nsc.rearrange("(k p o) -> p k o", p=128, o=1),
                                            allow_slow_non_contiguous=True))
        p.dma("gpsimd", lambda e: e.dma_start(out=bchunk.rearrange("p (k o) -> p k o", o=1),
                                              in_=nbi.rearrange("(k p o) -> p k o", p=128, o=1),
                                              allow_slow_non_contiguous=True))
        p.dma("gpsimd", lambda e: e.dma_start(out=wth.rearrange("p (k n) -> p k n", k=ND),
                                              in_=thw.rearrange("(k p) n -> p k n", p=128)))
        p.dma("gpsimd", lambda e: e.dma_start(out=fabf.rearrange("p (k n) -> p k n", k=ND),
                                              in_=fa.rearrange("(k p) n -> p k n", p=128)))
        p.dma("gpsimd", lambda e: e.dma_start(out=fbbf[:], in_=fb[:]))
        p.dma("sync", lambda e: e.dma_start(out=frow[0:1, :], in_=fbi.rearrange("(o d) -> o d", o=1)))
        p.barrier()
        p.op("vector", lambda e: e.memset(onesrow[:], 1.0))
        p.op("gpsimd", lambda e: e.memset(onesf[:], 1.0))
        p.op("scalar", lambda e: e.memzero(brow1[:]))
        p.barrier()
        p.op("vector", lambda e: e.memset(epst[:, 0:1], 1e-5))
        p.op("gpsimd", lambda e: e.memset(idnb[:], 1.0))
        p.barrier()
        p.op("vector", lambda e: e.memset(epst[:, 1:2], 1e-6))
        p.op("gpsimd", lambda e: e.affine_select(idnb[:], idnb[:], [[-1, 128]], ALU.is_equal,
                                                 0.0, base=0, channel_multiplier=1))
        p.barrier()
        p.op("vector", lambda e: e.memset(epst[:, 2:3], 1.0))
        p.barrier()
        p.op("vector", lambda e: e.memset(epst[:, 3:4], 0.0))
        p.op("scalar", lambda e: e.memzero(zscan[:]))
        p.barrier()
        p.op("vector", lambda e: e.tensor_scalar(tril[:], tril[:], 1.0, 1.0e30,
                                                 ALU.subtract, ALU.mult))
        p.barrier()
        p.op("vector", lambda e: e.memset(magicc[:, 0:1], 1.0))
        p.barrier()
        p.op("vector", lambda e: e.tensor_scalar(magicc[:, 1:2], magicc[:, 0:1],
                                                 9.313225746154785e-10, 1.0, ALU.add, ALU.subtract))
        p.barrier()
        p.op("vector", lambda e: e.tensor_scalar(magicc[:, 1:2], magicc[:, 1:2], 1073741824.0,
                                                 None, ALU.mult))
        p.barrier()
        p.op("vector", lambda e: e.tensor_scalar(magicc[:, 0:1], magicc[:, 1:2],
                                                 6755399441055744.0, 12582912.0, ALU.mult, ALU.add))
        p.barrier()

        p.op("gpsimd", lambda e: e.tensor_copy(idnf[:], idnb[:]))
        p.barrier()

        def bc_row(dst, srcdram, n):
            p.dma("sync", (lambda srcdram, n: lambda e: e.dma_start(
                out=srow[0:1, 0:n], in_=srcdram.rearrange("(o d) -> o d", o=1)))(srcdram, n))
            p.barrier()
            p.op("tensor", (lambda n: lambda e: e.matmul(
                psF[:, 0:n], onesf[0:1, 0:128], srow[0:1, 0:n], start=True, stop=True))(n))
            p.barrier()
            p.op("vector", (lambda dst, n: lambda e: e.tensor_copy(dst[:, 0:n], psF[:, 0:n]))(dst, n))
            p.barrier()

        bc_row(bcBt, bcB, 64); bc_row(bcCt, bcC, 64); bc_row(Bbit, Bbi, 64); bc_row(Cbit, Cbi, 64)
        bc_row(Dbct, Dsk, 16)
        p.dma("sync", lambda e: e.dma_start(out=srow[0:1, 0:16], in_=alog.rearrange("(o d) -> o d", o=1)))
        p.barrier()
        p.op("scalar", lambda e: e.activation(srow[0:1, 16:32], srow[0:1, 0:16], AF.Exp))
        p.barrier()
        p.op("vector", lambda e: e.tensor_scalar_mul(srow[0:1, 16:32], srow[0:1, 16:32], -1.0))
        p.barrier()
        p.op("tensor", lambda e: e.matmul(psF[:, 0:16], onesf[0:1, 0:128], srow[0:1, 16:32],
                                          start=True, stop=True))
        p.barrier()
        p.op("vector", lambda e: e.tensor_copy(Abct[:], psF[:, 0:16]))
        p.barrier()

        for tt in range(NT):
            sl = slice(tt * 128, tt * 128 + 128)
            p.dma("sync", (lambda ap: lambda e: e.dma_start(out=xt[:], in_=ap))(xin_tile(tt)))
            p.barrier()
            emit_ln(p, xt, xnt, sqt, stat, epst[:, 0:1])
            p.op("vector", lambda e: e.tensor_copy(bft[:], xnt[:]))
            p.barrier()
            p.dma("sync", (lambda sl: lambda e: e.dma_start(out=xn_d[sl, :], in_=xnt[:]))(sl))
            p.dma("sync", (lambda sl: lambda e: e.dma_start(out=lnbf_d[sl, :], in_=bft[:]))(sl))
            p.barrier()
        for kc in range(ND):
            p.dma("sync", (lambda kc: lambda e: e.dma_start(
                out=lnfm[:, kc * T:(kc + 1) * T], in_=lnbf_d[:, kc * 128:(kc + 1) * 128],
                transpose=True))(kc))
        p.barrier()

        for bi_ in range(9):
            nb0, nbw = NB_OFF[bi_], NB_P[bi_]
            for kc in range(ND):
                p.dma("gpsimd", (lambda kc, nb0, nbw: lambda e: e.dma_start(
                    out=wslot[:, 0:nbw], in_=wprojW[kc * 128:(kc + 1) * 128, nb0:nb0 + nbw]))(kc, nb0, nbw))
                p.barrier()
                p.op("tensor", (lambda kc, nbw: lambda e: e.matmul(
                    ps0[0:1, 0:nbw], bchunk[:, kc:kc + 1], wslot[:, 0:nbw],
                    start=(kc == 0), stop=(kc == ND - 1)))(kc, nbw))
                p.barrier()
            p.op("vector", (lambda nb0, nbw: lambda e: e.tensor_copy(
                brow1[0:1, nb0:nb0 + nbw], ps0[0:1, 0:nbw]))(nb0, nbw))
            p.barrier()
        p.dma("sync", lambda e: e.dma_start(out=srow[0:1, 0:16], in_=dtb.rearrange("(o d) -> o d", o=1)))
        p.barrier()
        p.op("vector", lambda e: e.tensor_add(brow1[0:1, 4224:4240], brow1[0:1, 4224:4240],
                                              srow[0:1, 0:16]))
        p.barrier()
        p.op("vector", lambda e: e.tensor_copy(brow1b[:], brow1[:]))
        p.barrier()

        for grp in range(4):
            tts = list(range(grp * 4, grp * 4 + 4))
            for bi_ in range(9):
                nb0, nbw = NB_OFF[bi_], NB_P[bi_]
                for kc in range(ND):
                    sslot = (kc % 2) * 512
                    p.dma("gpsimd", (lambda kc, nb0, nbw, sslot: lambda e: e.dma_start(
                        out=wslot[:, sslot:sslot + nbw],
                        in_=wprojW[kc * 128:(kc + 1) * 128, nb0:nb0 + nbw]))(kc, nb0, nbw, sslot))
                    p.barrier()
                    p.op("vector", (lambda kc, nbw, sslot: lambda e: e.tensor_scalar(
                        wslot[:, sslot:sslot + nbw], wslot[:, sslot:sslot + nbw],
                        gchunk[:, kc:kc + 1], None, ALU.mult))(kc, nbw, sslot))
                    p.barrier()
                    for ti, tt in enumerate(tts):
                        p.op("tensor", (lambda kc, nbw, ti, tt, sslot: lambda e: e.matmul(
                            PSA[ti][:, 0:nbw],
                            lnfm[:, kc * T + tt * 128: kc * T + tt * 128 + 128],
                            wslot[:, sslot:sslot + nbw],
                            start=(kc == 0), stop=False))(kc, nbw, ti, tt, sslot))
                    p.barrier()
                for ti in range(4):
                    p.op("tensor", (lambda nbw, nb0, ti: lambda e: e.matmul(
                        PSA[ti][:, 0:nbw], onesrow[0:1, 0:128],
                        brow1b[0:1, nb0:nb0 + nbw], start=False, stop=True))(nbw, nb0, ti))
                p.barrier()
                for ti, tt in enumerate(tts):
                    sl = slice(tt * 128, tt * 128 + 128)
                    if bi_ < 4:
                        p.op("scalar", (lambda ti: lambda e: e.activation(
                            Xc[:, 0:512], PSA[ti][:, 0:512], AF.Silu))(ti))
                        p.barrier()
                        p.dma("sync", (lambda sl, bi_: lambda e: e.dma_start(
                            out=xin_d[sl, bi_ * 512:bi_ * 512 + 512], in_=Xc[:, 0:512]))(sl, bi_))
                        p.barrier()
                    elif bi_ < 8:
                        p.op("scalar", (lambda ti: lambda e: e.activation(
                            Zc[:, 0:512], PSA[ti][:, 0:512], AF.Silu))(ti))
                        p.barrier()
                        p.dma("sync", (lambda sl, bi_: lambda e: e.dma_start(
                            out=zsil_d[sl, (bi_ - 4) * 512:(bi_ - 4) * 512 + 512],
                            in_=Zc[:, 0:512]))(sl, bi_))
                        p.barrier()
                    else:
                        p.op("vector", (lambda ti: lambda e: e.tensor_copy(
                            scr[:, 0:160], PSA[ti][:, 0:160]))(ti))
                        p.barrier()
                        for (c0, sct, bt_) in ((0, bcBt, Bbit), (64, bcCt, Cbit)):
                            p.op("scalar", (lambda c0: lambda e: e.activation(
                                trig[:, 0:64], scr[:, c0:c0 + 64], AF.Square,
                                accum_out=stat[:, 0:1]))(c0))
                            p.barrier()
                            p.op("scalar", lambda e: e.activation(stat[:, 1:2], stat[:, 0:1], AF.Ln,
                                                                  bias=epst[:, 1:2], scale=1.0 / 64))
                            p.barrier()
                            p.op("scalar", lambda e: e.activation(stat[:, 2:3], stat[:, 1:2],
                                                                  AF.Exp, scale=-0.5))
                            p.barrier()
                            p.op("vector", (lambda c0, sct: lambda e: e.tensor_scalar(
                                scr[:, c0:c0 + 64], scr[:, c0:c0 + 64], stat[:, 2:3], None,
                                ALU.mult))(c0, sct))
                            p.barrier()
                            p.op("vector", (lambda c0, sct: lambda e: e.tensor_mul(
                                scr[:, c0:c0 + 64], scr[:, c0:c0 + 64], sct[:, 0:64]))(c0, sct))
                            p.barrier()
                            p.op("vector", (lambda c0, bt_: lambda e: e.tensor_add(
                                scr[:, c0:c0 + 64], scr[:, c0:c0 + 64], bt_[:, 0:64]))(c0, bt_))
                            p.barrier()
                        p.op("vector", (lambda tt: lambda e: e.tensor_copy(
                            Btok[:, tt * 64:(tt + 1) * 64], scr[:, 0:64]))(tt))
                        p.op("gpsimd", (lambda tt: lambda e: e.tensor_copy(
                            Ctok[:, tt * 64:(tt + 1) * 64], scr[:, 64:128]))(tt))
                        p.barrier()
                        p.op("scalar", lambda e: e.activation(trig[:, 0:16], scr[:, 128:144], AF.Abs))
                        p.barrier()
                        p.op("scalar", lambda e: e.activation(trig[:, 16:32], trig[:, 0:16],
                                                              AF.Exp, scale=-1.0))
                        p.barrier()
                        p.op("scalar", lambda e: e.activation(trig[:, 32:48], trig[:, 16:32],
                                                              AF.Ln, bias=epst[:, 2:3]))
                        p.barrier()
                        p.op("scalar", lambda e: e.activation(trig[:, 48:64], scr[:, 128:144], AF.Relu))
                        p.barrier()
                        p.op("vector", (lambda tt: lambda e: e.tensor_add(
                            dttok[:, tt * 16:(tt + 1) * 16], trig[:, 32:48], trig[:, 48:64]))(tt))
                        p.barrier()
                        p.op("scalar", (lambda tt: lambda e: e.activation(
                            lamtok[:, tt * 16:(tt + 1) * 16], scr[:, 144:160], AF.Sigmoid))(tt))
                        p.barrier()
                        p.op("vector", (lambda tt: lambda e: e.tensor_mul(
                            atok[:, tt * 16:(tt + 1) * 16], dttok[:, tt * 16:(tt + 1) * 16],
                            lamtok[:, tt * 16:(tt + 1) * 16]))(tt))
                        p.barrier()
                        p.op("vector", (lambda tt: lambda e: e.tensor_sub(
                            trig[:, 0:16], dttok[:, tt * 16:(tt + 1) * 16],
                            atok[:, tt * 16:(tt + 1) * 16]))(tt))
                        p.barrier()
                        p.dma("sync", (lambda tt: lambda e: e.dma_start(
                            out=gam_d[tt * 128:(tt + 1) * 128, :], in_=trig[:, 0:16]))(tt))
                        p.barrier()
        p.op("vector", lambda e: e.memset(trig[:, 0:16], 0.0))
        p.barrier()
        p.dma("sync", lambda e: e.dma_start(out=gam_d[T:T + 128, :], in_=trig[:, 0:16]))
        p.barrier()
        for tt in range(NT):
            p.dma("sync", (lambda tt: lambda e: e.dma_start(
                out=trig[:, 0:16], in_=gam_d[tt * 128 + 1: tt * 128 + 129, :]))(tt))
            p.barrier()
            p.op("vector", (lambda tt: lambda e: e.tensor_add(
                ctok[:, tt * 16:(tt + 1) * 16], atok[:, tt * 16:(tt + 1) * 16], trig[:, 0:16]))(tt))
            p.barrier()

        for tt in range(NT):
            for kc in range(ND):
                p.op("tensor", (lambda tt, kc: lambda e: e.matmul(
                    ps0[:, 0:32], lnfm[:, kc * T + tt * 128: kc * T + tt * 128 + 128],
                    wth[:, kc * 32:(kc + 1) * 32], start=(kc == 0), stop=(kc == ND - 1)))(tt, kc))
            p.barrier()
            p.op("vector", lambda e: e.tensor_copy(thetat[:], ps0[:, 0:32]))
            p.barrier()
            p.op("vector", (lambda tt: lambda e: e.reduce_sum(
                stat[:, 0:1], dttok[:, tt * 16:(tt + 1) * 16], axis=AX.X))(tt))
            p.barrier()
            p.op("vector", lambda e: e.tensor_scalar_mul(stat[:, 0:1], stat[:, 0:1], 1.0 / 16))
            p.barrier()
            p.op("vector", lambda e: e.tensor_scalar(thetat[:], thetat[:], stat[:, 0:1], None, ALU.mult))
            p.barrier()
            p.op("tensor", lambda e: e.transpose(psF[0:32, 0:128], thetat[:], idnf[:]))
            p.barrier()
            p.op("vector", (lambda tt: lambda e: e.tensor_copy(
                phfm[:, tt * 128:(tt + 1) * 128], psF[0:32, 0:128]))(tt))
            p.barrier()
        p.op("vector", lambda e: e.memset(lse[0:32, 0:1], 0.0))
        p.barrier()
        for seg in range(4):
            p.op("vector", (lambda seg: lambda e: e.tensor_tensor_scan(
                Yc[0:32, 0:512], phfm[:, seg * 512:(seg + 1) * 512], zscan[:],
                lse[0:32, 0:1], ALU.add, ALU.add))(seg))
            p.barrier()
            p.op("vector", (lambda seg: lambda e: e.tensor_copy(
                phfm[:, seg * 512:(seg + 1) * 512], Yc[0:32, 0:512]))(seg))
            p.barrier()
            p.op("vector", lambda e: e.tensor_copy(lse[0:32, 0:1], Yc[0:32, 511:512]))
            p.barrier()
        TWO_PI = float(2 * np.pi)
        for tt in range(NT):
            p.op("tensor", (lambda tt: lambda e: e.transpose(
                psF[0:128, 0:32], phfm[:, tt * 128:(tt + 1) * 128], idnf[0:32, 0:32]))(tt))
            p.barrier()
            p.op("vector", lambda e: e.tensor_copy(phtok[:], psF[0:128, 0:32]))
            p.barrier()
            for (dst0, shift) in ((0, 0.0), (32, float(np.pi / 2))):
                p.op("vector", (lambda shift: lambda e: e.tensor_scalar(
                    trig[:, 64:96], phtok[:], 1.0 / TWO_PI, shift / TWO_PI, ALU.mult, ALU.add))(shift))
                p.barrier()
                p.op("vector", lambda e: e.tensor_scalar(trig[:, 64:96], trig[:, 64:96],
                                                         magicc[:, 0:1], magicc[:, 0:1],
                                                         ALU.add, ALU.subtract))
                p.barrier()
                p.op("vector", (lambda shift: lambda e: e.tensor_scalar(
                    trig[:, 96:128], trig[:, 64:96], TWO_PI, -shift, ALU.mult, ALU.add))(shift))
                p.barrier()
                p.op("vector", (lambda dst0: lambda e: e.tensor_sub(
                    trig[:, dst0:dst0 + 32], phtok[:], trig[:, 96:128]))(dst0))
                p.barrier()
                p.op("scalar", (lambda dst0: lambda e: e.activation(
                    trig[:, dst0:dst0 + 32], trig[:, dst0:dst0 + 32], AF.Sin))(dst0))
                p.barrier()
            sin0, cos0 = 0, 32
            for tok in (Btok, Ctok):
                p.op("vector", (lambda tok, tt: lambda e: e.tensor_mul(
                    trig[:, 64:96], tok[:, tt * 64:tt * 64 + 32], trig[:, cos0:cos0 + 32]))(tok, tt))
                p.barrier()
                p.op("vector", (lambda tok, tt: lambda e: e.tensor_mul(
                    trig[:, 96:128], tok[:, tt * 64 + 32:tt * 64 + 64], trig[:, sin0:sin0 + 32]))(tok, tt))
                p.barrier()
                p.op("vector", (lambda tok, tt: lambda e: e.tensor_mul(
                    scr[:, 0:32], tok[:, tt * 64:tt * 64 + 32], trig[:, sin0:sin0 + 32]))(tok, tt))
                p.barrier()
                p.op("vector", (lambda tok, tt: lambda e: e.tensor_mul(
                    scr[:, 32:64], tok[:, tt * 64 + 32:tt * 64 + 64], trig[:, cos0:cos0 + 32]))(tok, tt))
                p.barrier()
                p.op("vector", (lambda tok, tt: lambda e: e.tensor_sub(
                    tok[:, tt * 64:tt * 64 + 32], trig[:, 64:96], trig[:, 96:128]))(tok, tt))
                p.barrier()
                p.op("vector", (lambda tok, tt: lambda e: e.tensor_add(
                    tok[:, tt * 64 + 32:tt * 64 + 64], scr[:, 0:32], scr[:, 32:64]))(tok, tt))
                p.barrier()
            p.op("vector", (lambda tt: lambda e: e.tensor_mul(
                scr[:, 0:64], Btok[:, tt * 64:(tt + 1) * 64], Ctok[:, tt * 64:(tt + 1) * 64]))(tt))
            p.barrier()
            p.op("vector", (lambda tt: lambda e: e.reduce_sum(
                gdiag[:, tt:tt + 1], scr[:, 0:64], axis=AX.X))(tt))
            p.barrier()
            for (tok, fm) in ((Btok, Bfm), (Ctok, Cfm)):
                p.op("vector", (lambda tok, tt: lambda e: e.tensor_copy(
                    bft[:, 0:64], tok[:, tt * 64:(tt + 1) * 64]))(tok, tt))
                p.barrier()
                p.op("tensor", lambda e: e.transpose(psT[0:64, 0:128], bft[:, 0:64], idnb[:]))
                p.barrier()
                p.op("vector", (lambda fm, tt: lambda e: e.tensor_copy(
                    fm[:, tt * 128:(tt + 1) * 128], psT[0:64, 0:128]))(fm, tt))
                p.barrier()
            p.op("vector", (lambda tt: lambda e: e.tensor_mul(
                trig[:, 0:16], dttok[:, tt * 16:(tt + 1) * 16], Abct[:]))(tt))
            p.barrier()
            p.op("tensor", lambda e: e.transpose(psF[0:16, 0:128], trig[:, 0:16], idnf[:]))
            p.barrier()
            p.op("vector", (lambda tt: lambda e: e.tensor_copy(
                logafm[:, tt * 128:(tt + 1) * 128], psF[0:16, 0:128]))(tt))
            p.barrier()
        for cc in range(NT):
            p.op("vector", (lambda cc: lambda e: e.tensor_tensor_scan(
                Yc[0:16, 0:128], logafm[:, cc * 128:(cc + 1) * 128],
                zscan[0:16, 0:128], 0.0, ALU.add, ALU.add))(cc))
            p.barrier()
            p.op("vector", (lambda cc: lambda e: e.tensor_copy(
                logafm[:, cc * 128:(cc + 1) * 128], Yc[0:16, 0:128]))(cc))
            p.barrier()
            p.dma("sync", (lambda cc: lambda e: e.dma_start(
                out=lc_cm_d[cc, :, :], in_=logafm[:, cc * 128:(cc + 1) * 128]))(cc))
            p.barrier()
            p.op("tensor", (lambda cc: lambda e: e.transpose(
                psF[0:128, 0:16], logafm[:, cc * 128:(cc + 1) * 128], idnf[0:16, 0:16]))(cc))
            p.barrier()
            p.op("vector", (lambda cc: lambda e: e.tensor_copy(
                lctok[:, cc * 16:(cc + 1) * 16], psF[0:128, 0:16]))(cc))
            p.barrier()
        for tt in range(NT):
            p.op("scalar", (lambda tt: lambda e: e.activation(
                ftok[:, tt * 16:(tt + 1) * 16], ctok[:, tt * 16:(tt + 1) * 16], AF.Ln))(tt))
            p.barrier()
            p.op("vector", (lambda tt: lambda e: e.tensor_sub(
                ftok[:, tt * 16:(tt + 1) * 16], ftok[:, tt * 16:(tt + 1) * 16],
                lctok[:, tt * 16:(tt + 1) * 16]))(tt))
            p.barrier()
            p.op("vector", (lambda tt: lambda e: e.tensor_scalar(
                coefc[:, tt * 16:(tt + 1) * 16], atok[:, tt * 16:(tt + 1) * 16],
                gdiag[:, tt:tt + 1], None, ALU.mult))(tt))
            p.barrier()
            p.op("vector", (lambda tt: lambda e: e.tensor_add(
                coefc[:, tt * 16:(tt + 1) * 16], coefc[:, tt * 16:(tt + 1) * 16], Dbct[:]))(tt))
            p.barrier()

        p.op("vector", lambda e: e.memset(Sst[:], 0.0))
        p.barrier()
        for cc in range(NT):
            sl = slice(cc * 128, cc * 128 + 128)
            p.dma("sync", (lambda sl: lambda e: e.dma_start(out=Xc[:], in_=xin_d[sl, :]))(sl))
            p.dma("sync", (lambda sl: lambda e: e.dma_start(out=Zc[:], in_=zsil_d[sl, :]))(sl))
            p.dma("sync", (lambda cc: lambda e: e.dma_start(
                out=lcrow.rearrange("o (h t) -> o h t", h=H),
                in_=lc_cm_d[cc, :, :].rearrange("(o h) t -> o h t", o=1)))(cc))
            p.barrier()
            p.op("tensor", (lambda cc: lambda e: e.matmul(
                psF[:, 0:128], Bfm[:, cc * 128:(cc + 1) * 128], Cfm[:, cc * 128:(cc + 1) * 128],
                start=True, stop=True))(cc))
            p.barrier()
            p.op("vector", lambda e: e.tensor_copy(G0m[:], psF[:, 0:128]))
            p.barrier()
            for h in range(H):
                ic = cc * 16 + h
                p.op("tensor", (lambda h: lambda e: e.matmul(
                    psF[:, 0:128], onesf[0:1, 0:128], lcrow[0:1, h * 128:(h + 1) * 128],
                    start=True, stop=True))(h))
                p.barrier()
                p.op("vector", (lambda ic: lambda e: e.tensor_scalar(
                    sqt[:, 0:128], psF[:, 0:128], ftok[:, ic:ic + 1], None, ALU.add))(ic))
                p.barrier()
                p.op("vector", lambda e: e.tensor_add(sqt[:, 0:128], sqt[:, 0:128], tril[:]))
                p.barrier()
                p.op("scalar", lambda e: e.activation(sqt[:, 128:256], sqt[:, 0:128], AF.Exp))
                p.barrier()
                p.op("vector", lambda e: e.tensor_mul(sqt[:, 128:256], sqt[:, 128:256], G0m[:]))
                p.barrier()
                p.op("vector", lambda e: e.tensor_copy(Mt[:], sqt[:, 128:256]))
                p.barrier()
                p.op("tensor", (lambda h: lambda e: e.matmul(
                    psY[:, 0:128], Mt[:], Xc[:, h * 128:(h + 1) * 128],
                    start=True, stop=True))(h))
                p.barrier()
                p.op("vector", (lambda h: lambda e: e.tensor_copy(
                    Sbf[:], Sst[:, h * 128:(h + 1) * 128]))(h))
                p.barrier()
                p.op("tensor", (lambda cc: lambda e: e.matmul(
                    psS[0:64, 0:128] if False else psF[:, 0:128],
                    Cfm[:, cc * 128:(cc + 1) * 128], Sbf[:],
                    start=True, stop=True))(cc))
                p.barrier()
                p.op("vector", (lambda ic, h: lambda e: e.scalar_tensor_tensor(
                    Yc[:, h * 128:(h + 1) * 128], Xc[:, h * 128:(h + 1) * 128],
                    coefc[:, ic:ic + 1], psY[:, 0:128], ALU.mult, ALU.add))(ic, h))
                p.barrier()
                p.op("scalar", (lambda ic: lambda e: e.activation(
                    lse[:, 0:1], lctok[:, ic:ic + 1], AF.Exp))(ic))
                p.barrier()
                p.op("vector", (lambda: lambda e: e.tensor_scalar(
                    sqt[:, 0:128], psF[:, 0:128], lse[:, 0:1], None, ALU.mult))())
                p.barrier()
                p.op("vector", (lambda h: lambda e: e.tensor_add(
                    Yc[:, h * 128:(h + 1) * 128], Yc[:, h * 128:(h + 1) * 128],
                    sqt[:, 0:128]))(h))
                p.barrier()
                p.op("tensor", (lambda h: lambda e: e.matmul(
                    psF[:, 0:1], onesf[0:1, 0:128], lcrow[0:1, h * 128 + 127:h * 128 + 128],
                    start=True, stop=True))(h))
                p.barrier()
                p.op("vector", lambda e: e.tensor_copy(lse[:, 1:2], psF[:, 0:1]))
                p.barrier()
                p.op("scalar", lambda e: e.activation(lse[:, 2:3], lse[:, 1:2], AF.Exp))
                p.barrier()
                p.op("vector", (lambda ic: lambda e: e.tensor_add(
                    lse[:, 3:4], ftok[:, ic:ic + 1], lse[:, 1:2]))(ic))
                p.barrier()
                p.op("scalar", lambda e: e.activation(lse[:, 3:4], lse[:, 3:4], AF.Exp))
                p.barrier()
                p.op("vector", (lambda cc: lambda e: e.tensor_scalar(
                    Bw[:, 0:64], Btok[:, cc * 64:(cc + 1) * 64], lse[:, 3:4], None,
                    ALU.mult))(cc))
                p.barrier()
                p.op("tensor", (lambda h: lambda e: e.matmul(
                    psS[0:64, 0:128], Bw[:, 0:64], Xc[:, h * 128:(h + 1) * 128],
                    start=True, stop=True))(h))
                p.barrier()
                p.op("vector", (lambda h: lambda e: e.scalar_tensor_tensor(
                    Sst[:, h * 128:(h + 1) * 128], Sst[:, h * 128:(h + 1) * 128],
                    lse[0:64, 2:3], psS[0:64, 0:128], ALU.mult, ALU.add))(h))
                p.barrier()
            p.op("vector", lambda e: e.tensor_mul(Yc[:], Yc[:], Zc[:]))
            p.barrier()
            p.op("vector", lambda e: e.tensor_copy(Xc[:], Yc[:]))
            p.barrier()
            p.dma("sync", (lambda sl: lambda e: e.dma_start(out=yz_d[sl, :], in_=Xc[:]))(sl))
            p.barrier()
        for kc in range(NDI):
            p.dma("sync", (lambda kc: lambda e: e.dma_start(
                out=Xc[:], in_=yz_d[:, kc * 128:(kc + 1) * 128], transpose=True))(kc))
            p.barrier()
            p.dma("sync", (lambda kc: lambda e: e.dma_start(out=yzfm_d[kc, :, :], in_=Xc[:]))(kc))
            p.barrier()

        for kc in range(ND):
            p.dma("sync", (lambda kc: lambda e: e.dma_start(
                out=sqt[:], in_=fker[kc * 128:(kc + 1) * 128, :]))(kc))
            p.barrier()
            p.op("vector", (lambda kc: lambda e: e.reduce_max(
                stat[:, 5:6] if kc else stat[:, 4:5],
                sqt[:], axis=AX.X, apply_absolute_value=True))(kc))
            p.barrier()
            if kc:
                p.op("vector", lambda e: e.tensor_max(stat[:, 4:5], stat[:, 4:5], stat[:, 5:6]))
                p.barrier()
        p.op("tensor", lambda e: e.transpose(psF[0:1, 0:128], stat[:, 4:5], idnf[:]))
        p.barrier()
        p.op("vector", lambda e: e.reduce_max(stat[0:1, 6:7], psF[0:1, 0:128], axis=AX.X))
        p.barrier()
        p.op("tensor", lambda e: e.matmul(psF[:, 0:1], onesf[0:1, 0:128], stat[0:1, 6:7],
                                          start=True, stop=True))
        p.barrier()
        p.op("vector", lambda e: e.tensor_scalar_mul(stat[:, 4:5], psF[:, 0:1], 1.0 / 7.0))
        p.barrier()
        p.op("vector", lambda e: e.reciprocal(stat[:, 5:6], stat[:, 4:5]))
        p.barrier()
        for kc in range(ND):
            p.dma("sync", (lambda kc: lambda e: e.dma_start(
                out=sqt[:], in_=fker[kc * 128:(kc + 1) * 128, :]))(kc))
            p.barrier()
            p.op("vector", lambda e: e.tensor_scalar(sqt[:], sqt[:], stat[:, 5:6], None, ALU.mult))
            p.barrier()
            p.op("vector", lambda e: e.tensor_scalar(sqt[:], sqt[:], magicc[:, 0:1], magicc[:, 0:1],
                                                     ALU.add, ALU.subtract))
            p.barrier()
            p.op("vector", lambda e: e.tensor_scalar(sqt[:], sqt[:], -8.0, 7.0, ALU.max, ALU.min))
            p.barrier()
            p.op("vector", (lambda kc: lambda e: e.tensor_scalar(
                wfqb[:, kc * D:(kc + 1) * D], sqt[:], stat[:, 4:5], None, ALU.mult))(kc))
            p.barrier()
        p.op("vector", lambda e: e.tensor_copy(frowb[:], frow[:]))
        p.barrier()

        for tt in range(NT):
            for kc in range(ND):
                p.op("tensor", (lambda tt, kc: lambda e: e.matmul(
                    ps0[:, 0:16], lnfm[:, kc * T + tt * 128: kc * T + tt * 128 + 128],
                    fabf[:, kc * 16:(kc + 1) * 16], start=(kc == 0), stop=(kc == ND - 1)))(tt, kc))
            p.barrier()
            p.op("vector", lambda e: e.tensor_copy(bft[:, 0:16], ps0[:, 0:16]))
            p.barrier()
            p.op("tensor", lambda e: e.transpose(psT[0:16, 0:128], bft[:, 0:16], idnb[:]))
            p.barrier()
            p.op("vector", (lambda tt: lambda e: e.tensor_copy(
                xafm[:, tt * 128:(tt + 1) * 128], psT[0:16, 0:128]))(tt))
            p.barrier()

        for grp in range(4):
            tts = list(range(grp * 4, grp * 4 + 4))
            for ti, tt in enumerate(tts):
                for kc in range(NDI):
                    p.dma("sync", (lambda ti, tt, kc: lambda e: e.dma_start(
                        out=yzg[:, ti * NDI * 128 + kc * 128: ti * NDI * 128 + kc * 128 + 128],
                        in_=yzfm_d[kc, :, tt * 128:(tt + 1) * 128]))(ti, tt, kc))
            p.barrier()
            for nb in range(2):
                for kc in range(NDI):
                    sslot = (kc % 2) * 512
                    p.dma("gpsimd", (lambda kc, nb, sslot: lambda e: e.dma_start(
                        out=wslot[:, sslot:sslot + 512],
                        in_=wout[kc * 128:(kc + 1) * 128, nb * 512:nb * 512 + 512]))(kc, nb, sslot))
                    p.barrier()
                    for ti in range(4):
                        p.op("tensor", (lambda kc, ti, sslot: lambda e: e.matmul(
                            PSA[ti][:, 0:512],
                            yzg[:, ti * NDI * 128 + kc * 128: ti * NDI * 128 + kc * 128 + 128],
                            wslot[:, sslot:sslot + 512],
                            start=(kc == 0), stop=(kc == NDI - 1)))(kc, ti, sslot))
                    p.barrier()
                for ti in range(4):
                    p.op("scalar", (lambda ti, nb: lambda e: e.copy(
                        st_[:, nb * 512:nb * 512 + 512], PSA[ti][:, 0:512]))(ti, nb))
                    p.barrier()
                    tt_ = tts[ti]
                    p.dma("sync", (lambda tt_, nb: lambda e: e.dma_start(
                        out=st_d[tt_ * 128:(tt_ + 1) * 128, nb * 512:nb * 512 + 512],
                        in_=st_[:, nb * 512:nb * 512 + 512]))(tt_, nb))
                    p.barrier()
            for ti, tt in enumerate(tts):
                sl = slice(tt * 128, tt * 128 + 128)
                for nb in range(2):
                    ps = [ps0, ps1][nb]
                    for kc in range(ND):
                        p.op("tensor", (lambda tt, kc, nb, ps: lambda e: e.matmul(
                            ps[:, 0:512],
                            lnfm[:, kc * T + tt * 128: kc * T + tt * 128 + 128],
                            wfqb[:, kc * D + nb * 512: kc * D + nb * 512 + 512],
                            start=(kc == 0), stop=False))(tt, kc, nb, ps))
                    p.op("tensor", (lambda tt, nb, ps: lambda e: e.matmul(
                        ps[:, 0:512], xafm[:, tt * 128:(tt + 1) * 128],
                        fbbf[:, nb * 512:nb * 512 + 512], start=False, stop=False))(tt, nb, ps))
                    p.op("tensor", (lambda nb, ps: lambda e: e.matmul(
                        ps[:, 0:512], onesrow[0:1, 0:128],
                        frowb[0:1, nb * 512:nb * 512 + 512], start=False, stop=True))(nb, ps))
                    p.barrier()
                    p.op("scalar", (lambda nb, ps: lambda e: e.activation(
                        gt_[:, nb * 512:nb * 512 + 512], ps[:, 0:512], AF.Silu))(nb, ps))
                    p.barrier()
                p.dma("sync", (lambda sl: lambda e: e.dma_start(out=xnt[:], in_=xn_d[sl, :]))(sl))
                p.dma("sync", (lambda ap: lambda e: e.dma_start(out=xt[:], in_=ap))(xin_tile(tt)))
                p.barrier()
                p.dma("sync", (lambda sl: lambda e: e.dma_start(out=st_[:], in_=st_d[sl, :]))(sl))
                p.barrier()
                p.op("vector", lambda e: e.tensor_add(st_[:], st_[:], xnt[:]))
                p.barrier()
                p.op("vector", lambda e: e.tensor_mul(st_[:], st_[:], gt_[:]))
                p.barrier()
                p.op("vector", lambda e: e.tensor_add(st_[:], st_[:], xt[:]))
                p.barrier()
                p.dma("sync", (lambda ap: lambda e: e.dma_start(out=ap, in_=st_[:]))(xout_tile(tt)))
                p.barrier()
        if standalone:
            p.build()
    es.close()
    return nc

def build_l2(nc, n_ranks=8, p=None, pfx="", xin_tile=None, xout_tile=None):
    H, DK = 16, 64
    PH, NK, KD, TOPK = 2, 256, 128, 8

    if xin_tile is None:
        _xin = nc.dram_tensor(pfx + "xsh", [T, D], F32, kind="ExternalInput").ap()
        xin_tile = lambda tt: _xin[tt * 128:(tt + 1) * 128, :]
    wqkvT = nc.dram_tensor(pfx + "wqkvT", [D, 3 * D], F32, kind="ExternalInput").ap()
    lnw = nc.dram_tensor(pfx + "lnw", [D], F32, kind="ExternalInput").ap()
    lnb = nc.dram_tensor(pfx + "lnb", [D], F32, kind="ExternalInput").ap()
    woutT = nc.dram_tensor(pfx + "woutT", [D, D], F32, kind="ExternalInput").ap()
    gamma = nc.dram_tensor(pfx + "gamma", [D], F32, kind="ExternalInput").ap()
    beta = nc.dram_tensor(pfx + "beta", [D], F32, kind="ExternalInput").ap()
    wqT = nc.dram_tensor(pfx + "wqT", [D, 2 * D], F32, kind="ExternalInput").ap()
    keysfm = nc.dram_tensor(pfx + "keysfm", [16, KD, NK], F32, kind="ExternalInput").ap()
    key_in = nc.dram_tensor(pfx + "key_in", [65536, D], F32, kind="ExternalInput").ap()
    key_out = nc.dram_tensor(pfx + "key_out", [65536, D], F32, kind="ExternalInput").ap()
    bdmask = nc.dram_tensor(pfx + "bdmask", [128, 128], F32, kind="ExternalInput").ap()

    if xout_tile is None:
        _xout = nc.dram_tensor(pfx + "x3sh", [T, D], F32, kind="ExternalOutput").ap()
        xout_tile = lambda tt: _xout[tt * 128:(tt + 1) * 128, :]

    xn_d = nc.dram_tensor(pfx + "xn_d", [T, D], F32, kind="Internal").ap()
    lnbf_d = nc.dram_tensor(pfx + "lnbf_d", [T, D], BF16, kind="Internal").ap()
    vtok_d = nc.dram_tensor(pfx + "vtok_d", [T, D], BF16, kind="Internal").ap()
    attn_d = nc.dram_tensor(pfx + "attn_d", [T, D], BF16, kind="Internal").ap()
    cc_in = nc.dram_tensor(pfx + "cc_in", [1, 2 * D], F32, kind="Internal")
    cc_out = nc.dram_tensor(pfx + "cc_out", [1, 2 * D], F32, kind="Internal")
    sg_d = nc.dram_tensor(pfx + "sg_d", [T, D], F32, kind="Internal").ap()
    bnc_d = nc.dram_tensor(pfx + "bnc_d", [8, D], F32, kind="Internal").ap()
    ln2bf_d = nc.dram_tensor(pfx + "ln2bf_d", [T, D], BF16, kind="Internal").ap()
    xnfm_d = nc.dram_tensor(pfx + "xnfm_d", [ND, 128, T], BF16, kind="Internal").ap()
    dbg_d = nc.dram_tensor(pfx + "dbg_d", [128, 8192], F32, kind="Internal").ap()

    standalone = p is None
    if standalone:
        p = Prog(nc)
    es = contextlib.ExitStack()
    qfm = es.enter_context(nc.sbuf_tensor(pfx + "qfm", [128, ND * T], BF16))
    kfm = es.enter_context(nc.sbuf_tensor(pfx + "kfm", [128, ND * T], BF16))
    wstr = es.enter_context(nc.sbuf_tensor(pfx + "wstr", [128, 2 * 1024], BF16))
    keyssb = es.enter_context(nc.sbuf_tensor(pfx + "keyssb", [128, 16 * NK], BF16))
    gln = es.enter_context(nc.sbuf_tensor(pfx + "gln", [128, ND], F32))
    bln = es.enter_context(nc.sbuf_tensor(pfx + "bln", [128, ND], BF16))
    gbn = es.enter_context(nc.sbuf_tensor(pfx + "gbn", [128, ND], F32))
    onesrow = es.enter_context(nc.sbuf_tensor(pfx + "onesrow", [1, 512], BF16))
    onesf = es.enter_context(nc.sbuf_tensor(pfx + "onesf", [128, 1], F32))
    epst = es.enter_context(nc.sbuf_tensor(pfx + "epst", [128, 4], F32))
    stat = es.enter_context(nc.sbuf_tensor(pfx + "stat", [128, 8], F32))
    browb = es.enter_context(nc.sbuf_tensor(pfx + "browb", [1, 3 * D], BF16))
    brow2b = es.enter_context(nc.sbuf_tensor(pfx + "brow2b", [1, 2 * D], BF16))
    statrow = es.enter_context(nc.sbuf_tensor(pfx + "statrow", [1, 2 * D], F32))
    xt = es.enter_context(nc.sbuf_tensor(pfx + "xt", [128, D], F32))
    xnt = es.enter_context(nc.sbuf_tensor(pfx + "xnt", [128, D], F32))
    sqt = es.enter_context(nc.sbuf_tensor(pfx + "sqt", [128, D], F32))
    bft = es.enter_context(nc.sbuf_tensor(pfx + "bft", [128, D], BF16))
    xbft = es.enter_context(nc.sbuf_tensor(pfx + "xbft", [128, D], BF16))
    mask2 = es.enter_context(nc.sbuf_tensor(pfx + "mask2", [128, 128], F32))
    swide = es.enter_context(nc.sbuf_tensor(pfx + "swide", [128, H * 128], F32))
    pwide = es.enter_context(nc.sbuf_tensor(pfx + "pwide", [128, H * 128], BF16))
    hmax = es.enter_context(nc.sbuf_tensor(pfx + "hmax", [128, H], F32))
    zrow = es.enter_context(nc.sbuf_tensor(pfx + "zrow", [128, 2 * H], F32))
    pT0 = es.enter_context(nc.sbuf_tensor(pfx + "pT0", [128, 128], BF16))
    idn = es.enter_context(nc.sbuf_tensor(pfx + "idn", [128, 128], BF16))
    ix8 = es.enter_context(nc.sbuf_tensor(pfx + "ix8", [128, 2 * 8 * 8], U32))
    asc = es.enter_context(nc.sbuf_tensor(pfx + "asc", [128, 8 * 64], F32))
    st8 = es.enter_context(nc.sbuf_tensor(pfx + "st8", [128, 8 * 8], F32))
    pk8 = es.enter_context(nc.sbuf_tensor(pfx + "pk8", [128, 8 * 8], U32))
    kxi = es.enter_context(nc.sbuf_tensor(pfx + "kxi", [128, 8 * 8], U32))
    idxf = es.enter_context(nc.sbuf_tensor(pfx + "idxf", [128, 64], U32))
    idxi = es.enter_context(nc.sbuf_tensor(pfx + "idxi", [128, 64], I32))
    iacc = es.enter_context(nc.sbuf_tensor(pfx + "iacc", [128, 64], U32))
    msk = es.enter_context(nc.sbuf_tensor(pfx + "msk", [128, 64], U32))
    cmb = es.enter_context(nc.sbuf_tensor(pfx + "cmb", [128, 64], F32))
    ocol = es.enter_context(nc.sbuf_tensor(pfx + "ocol", [128, 64], F32))
    gath = es.enter_context(nc.sbuf_tensor(pfx + "gath", [128, D], BF16))
    yacc = es.enter_context(nc.sbuf_tensor(pfx + "yacc", [128, D], F32))
    st_ = es.enter_context(nc.sbuf_tensor(pfx + "st_", [128, D], F32))
    gt_ = es.enter_context(nc.sbuf_tensor(pfx + "gt_", [128, D], F32))
    scrj = es.enter_context(nc.sbuf_tensor(pfx + "scrj", [128, 64], F32))
    ps0 = es.enter_context(nc.psum_tensor(pfx + "ps0", [128, 512], F32))
    ps1 = es.enter_context(nc.psum_tensor(pfx + "ps1", [128, 512], F32))
    ps2 = es.enter_context(nc.psum_tensor(pfx + "ps2", [128, 512], F32))
    ps3 = es.enter_context(nc.psum_tensor(pfx + "ps3", [128, 512], F32))
    pss0 = es.enter_context(nc.psum_tensor(pfx + "pss0", [128, 128], F32))
    pss1 = es.enter_context(nc.psum_tensor(pfx + "pss1", [128, 128], F32))
    psT = es.enter_context(nc.psum_tensor(pfx + "psT", [128, 128], BF16))
    psa = es.enter_context(nc.psum_tensor(pfx + "psa", [128, 64], F32))
    PSA = [ps0, ps1, ps2, ps3]
    if True:
        p.dma("sync", lambda e: e.dma_start(out=mask2[:], in_=bdmask[:]))
        p.dma("sync", lambda e: e.dma_start(out=gln.rearrange("p (k o) -> p k o", o=1),
                                            in_=lnw.rearrange("(k p o) -> p k o", p=128, o=1),
                                            allow_slow_non_contiguous=True))
        p.dma("gpsimd", lambda e: e.dma_start(out=bln.rearrange("p (k o) -> p k o", o=1),
                                              in_=lnb.rearrange("(k p o) -> p k o", p=128, o=1),
                                              allow_slow_non_contiguous=True))
        p.dma("gpsimd", lambda e: e.dma_start(out=keyssb.rearrange("p (g n) -> p g n", g=16),
                                              in_=keysfm.rearrange("g p n -> p g n")))
        p.barrier()
        p.op("vector", lambda e: e.memset(onesrow[:], 1.0))
        p.op("gpsimd", lambda e: e.memset(onesf[:], 1.0))
        p.barrier()
        p.op("vector", lambda e: e.memset(epst[:, 0:1], 1e-5))
        p.op("gpsimd", lambda e: e.memset(idn[:], 1.0))
        p.barrier()
        p.op("vector", lambda e: e.memset(epst[:, 3:4], 0.0))
        p.op("gpsimd", lambda e: e.affine_select(idn[:], idn[:], [[-1, 128]], ALU.is_equal,
                                                 0.0, base=0, channel_multiplier=1))
        p.barrier()

        for tt in range(NT):
            sl = slice(tt * 128, tt * 128 + 128)
            p.dma("sync", (lambda ap: lambda e: e.dma_start(out=xt[:], in_=ap))(xin_tile(tt)))
            p.barrier()
            emit_ln(p, xt, xnt, sqt, stat, epst[:, 0:1])
            p.op("vector", lambda e: e.tensor_copy(bft[:], xnt[:]))
            p.barrier()
            p.dma("sync", (lambda sl: lambda e: e.dma_start(out=xn_d[sl, :], in_=xnt[:]))(sl))
            p.dma("sync", (lambda sl: lambda e: e.dma_start(out=lnbf_d[sl, :], in_=bft[:]))(sl))
            p.barrier()
            p.op("tensor", (lambda tt: lambda e: e.matmul(
                ps0[0:1, 0:512], onesf[:, 0:1], xnt[:, 0:512],
                start=(tt == 0), stop=(tt == NT - 1)))(tt))
            p.op("tensor", (lambda tt: lambda e: e.matmul(
                ps1[0:1, 0:512], onesf[:, 0:1], xnt[:, 512:1024],
                start=(tt == 0), stop=(tt == NT - 1)))(tt))
            p.barrier()
            p.op("scalar", lambda e: e.activation(sqt[:], xnt[:], AF.Square))
            p.barrier()
            p.op("tensor", (lambda tt: lambda e: e.matmul(
                ps2[0:1, 0:512], onesf[:, 0:1], sqt[:, 0:512],
                start=(tt == 0), stop=(tt == NT - 1)))(tt))
            p.op("tensor", (lambda tt: lambda e: e.matmul(
                ps3[0:1, 0:512], onesf[:, 0:1], sqt[:, 512:1024],
                start=(tt == 0), stop=(tt == NT - 1)))(tt))
            p.barrier()
            emit_ln(p, xnt, sqt, xt, stat, epst[:, 0:1])
            p.op("vector", lambda e: e.tensor_copy(bft[:], sqt[:]))
            p.barrier()
            p.dma("sync", (lambda sl: lambda e: e.dma_start(out=ln2bf_d[sl, :], in_=bft[:]))(sl))
            p.barrier()
        p.op("vector", lambda e: e.tensor_copy(statrow[0:1, 0:512], ps0[0:1, 0:512]))
        p.barrier()
        p.op("vector", lambda e: e.tensor_copy(statrow[0:1, 512:1024], ps1[0:1, 0:512]))
        p.barrier()
        p.op("vector", lambda e: e.tensor_copy(statrow[0:1, D:D + 512], ps2[0:1, 0:512]))
        p.barrier()
        p.op("vector", lambda e: e.tensor_copy(statrow[0:1, D + 512:2 * D], ps3[0:1, 0:512]))
        p.barrier()
        p.dma("sync", lambda e: e.dma_start(out=cc_in.ap(), in_=statrow[0:1, :]))
        p.barrier()
        p.op("gpsimd", lambda e: e.collective_compute(
            "AllReduce", ALU.add, replica_groups=[list(range(n_ranks))],
            ins=[cc_in.ap().opt()], outs=[cc_out.ap().opt()]))
        p.barrier()
        p.dma("sync", lambda e: e.dma_start(out=statrow[0:1, :], in_=cc_out.ap()))
        p.barrier()
        p.op("vector", lambda e: e.tensor_scalar_mul(statrow[0:1, :], statrow[0:1, :], 1.0 / 16384.0))
        p.barrier()
        p.op("vector", lambda e: e.tensor_mul(sqt[0:1, 0:D], statrow[0:1, 0:D], statrow[0:1, 0:D]))
        p.barrier()
        p.op("vector", lambda e: e.tensor_sub(statrow[0:1, D:2 * D], statrow[0:1, D:2 * D], sqt[0:1, 0:D]))
        p.barrier()
        p.op("scalar", lambda e: e.activation(statrow[0:1, D:2 * D], statrow[0:1, D:2 * D], AF.Ln,
                                              bias=epst[0:1, 0:1]))
        p.barrier()
        p.op("scalar", lambda e: e.activation(statrow[0:1, D:2 * D], statrow[0:1, D:2 * D], AF.Exp, scale=-0.5))
        p.barrier()
        p.dma("sync", lambda e: e.dma_start(out=sqt[0:1, 0:D], in_=gamma.rearrange("(o d) -> o d", o=1)))
        p.barrier()
        p.op("vector", lambda e: e.tensor_mul(statrow[0:1, D:2 * D], statrow[0:1, D:2 * D], sqt[0:1, 0:D]))
        p.barrier()
        p.op("vector", lambda e: e.tensor_mul(statrow[0:1, 0:D], statrow[0:1, 0:D], statrow[0:1, D:2 * D]))
        p.barrier()
        p.dma("sync", lambda e: e.dma_start(out=sqt[0:1, 0:D], in_=beta.rearrange("(o d) -> o d", o=1)))
        p.barrier()
        p.op("vector", lambda e: e.tensor_sub(statrow[0:1, 0:D], sqt[0:1, 0:D], statrow[0:1, 0:D]))
        p.barrier()
        p.dma("sync", lambda e: e.dma_start(out=bnc_d[0:1, 0:D], in_=statrow[0:1, D:2 * D]))
        p.barrier()
        p.dma("sync", lambda e: e.dma_start(out=gbn.rearrange("p (k o) -> p k o", o=1),
                                            in_=bnc_d[0:1, 0:D].rearrange("o (k p) -> p k o", p=128),
                                            allow_slow_non_contiguous=True))
        p.barrier()

        ln2fm = es.enter_context(nc.sbuf_tensor(pfx + "ln2fm", [128, ND * T], BF16))
        for kc in range(ND):
            p.dma("sync", (lambda kc: lambda e: e.dma_start(
                out=ln2fm[:, kc * T:(kc + 1) * T], in_=ln2bf_d[:, kc * 128:(kc + 1) * 128],
                transpose=True))(kc))
            p.barrier()
            p.dma("sync", (lambda kc: lambda e: e.dma_start(
                out=pwide[:], in_=lnbf_d[:, kc * 128:(kc + 1) * 128], transpose=True))(kc))
            p.barrier()
            p.dma("sync", (lambda kc: lambda e: e.dma_start(
                out=xnfm_d[kc, :, :], in_=pwide[:]))(kc))
            p.barrier()

        for bi_ in range(6):
            nb0 = bi_ * 512
            for kc in range(ND):
                p.dma("gpsimd", (lambda kc, nb0: lambda e: e.dma_start(
                    out=wstr[:, 0:512], in_=wqkvT[kc * 128:(kc + 1) * 128, nb0:nb0 + 512]))(kc, nb0))
                p.barrier()
                p.op("tensor", (lambda kc: lambda e: e.matmul(
                    ps0[0:1, 0:512], bln[:, kc:kc + 1], wstr[:, 0:512],
                    start=(kc == 0), stop=(kc == ND - 1)))(kc))
                p.barrier()
            p.op("vector", (lambda nb0: lambda e: e.tensor_copy(
                browb[0:1, nb0:nb0 + 512], ps0[0:1, 0:512]))(nb0))
            p.barrier()

        p.dma("gpsimd", lambda e: e.dma_start(out=bnc_d[2:3, 0:D], in_=browb[0:1, 0:D]))
        p.dma("gpsimd", lambda e: e.dma_start(out=bnc_d[3:4, 0:D], in_=browb[0:1, D:2 * D]))
        p.barrier()
        bcol = es.enter_context(nc.sbuf_tensor(pfx + "bcol", [128, 16], F32))
        p.dma("sync", lambda e: e.dma_start(out=bcol[:, 0:8].rearrange("p (k o) -> p k o", o=1),
                                            in_=bnc_d[2:3, 0:D].rearrange("o (k p) -> p k o", p=128),
                                            allow_slow_non_contiguous=True))
        p.dma("sync", lambda e: e.dma_start(out=bcol[:, 8:16].rearrange("p (k o) -> p k o", o=1),
                                            in_=bnc_d[3:4, 0:D].rearrange("o (k p) -> p k o", p=128),
                                            allow_slow_non_contiguous=True))
        p.barrier()
        p.op("vector", lambda e: e.tensor_scalar_mul(bcol[:, 0:8], bcol[:, 0:8], 1.0 / 32.0))
        p.barrier()

        for part in range(2):
            for oc in range(ND):
                for kc in range(ND):
                    sslot = (kc % 2) * 1024
                    p.dma("gpsimd", (lambda part, oc, kc, sslot: lambda e: e.dma_start(
                        out=wstr[:, sslot:sslot + 128],
                        in_=wqkvT[kc * 128:(kc + 1) * 128,
                                  part * D + oc * 128: part * D + oc * 128 + 128]))(part, oc, kc, sslot))
                    p.barrier()
                    p.op("vector", (lambda kc, sslot: lambda e: e.tensor_scalar(
                        wstr[:, sslot:sslot + 128], wstr[:, sslot:sslot + 128],
                        gln[:, kc:kc + 1], None, ALU.mult))(kc, sslot))
                    p.barrier()
                    for tb in range(4):
                        p.op("tensor", (lambda kc, tb, sslot: lambda e: e.matmul(
                            PSA[tb][:, 0:512], wstr[:, sslot:sslot + 128],
                            ln2fm[:, kc * T + tb * 512: kc * T + tb * 512 + 512],
                            start=(kc == 0), stop=(kc == ND - 1)))(kc, tb, sslot))
                    p.barrier()
                for tb in range(4):
                    dst = qfm if part == 0 else kfm
                    sc = 1.0 / 32.0 if part == 0 else 1.0
                    p.op("scalar", (lambda oc, tb, dst, sc, part: lambda e: e.activation(
                        dst[:, oc * T + tb * 512: oc * T + tb * 512 + 512],
                        PSA[tb][:, 0:512], AF.Identity,
                        bias=bcol[:, part * 8 + oc: part * 8 + oc + 1], scale=sc))(oc, tb, dst, sc, part))
                p.barrier()

        for grp in range(4):
            tts = list(range(grp * 4, grp * 4 + 4))
            for nb in range(2):
                for kc in range(ND):
                    sslot = (kc % 2) * 1024
                    p.dma("gpsimd", (lambda nb, kc, sslot: lambda e: e.dma_start(
                        out=wstr[:, sslot:sslot + 512],
                        in_=wqkvT[kc * 128:(kc + 1) * 128,
                                  2 * D + nb * 512: 2 * D + nb * 512 + 512]))(nb, kc, sslot))
                    p.barrier()
                    p.op("vector", (lambda kc, sslot: lambda e: e.tensor_scalar(
                        wstr[:, sslot:sslot + 512], wstr[:, sslot:sslot + 512],
                        gln[:, kc:kc + 1], None, ALU.mult))(kc, sslot))
                    p.barrier()
                    for ti, tt in enumerate(tts):
                        p.op("tensor", (lambda kc, ti, tt, sslot: lambda e: e.matmul(
                            PSA[ti][:, 0:512],
                            ln2fm[:, kc * T + tt * 128: kc * T + tt * 128 + 128],
                            wstr[:, sslot:sslot + 512],
                            start=(kc == 0), stop=False))(kc, ti, tt, sslot))
                    p.barrier()
                for ti in range(4):
                    p.op("tensor", (lambda nb, ti: lambda e: e.matmul(
                        PSA[ti][:, 0:512], onesrow[0:1, 0:128],
                        browb[0:1, 2 * D + nb * 512: 2 * D + nb * 512 + 512],
                        start=False, stop=True))(nb, ti))
                p.barrier()
                for ti, tt in enumerate(tts):
                    p.op("vector", (lambda ti: lambda e: e.tensor_copy(
                        xbft[:, 0:512], PSA[ti][:, 0:512]))(ti))
                    p.barrier()
                    p.dma("sync", (lambda tt, nb: lambda e: e.dma_start(
                        out=vtok_d[tt * 128:(tt + 1) * 128, nb * 512:nb * 512 + 512],
                        in_=xbft[:, 0:512]))(tt, nb))
                    p.barrier()

        vh = es.enter_context(nc.sbuf_tensor(pfx + "vh", [128, D], BF16))
        for tt in range(NT):
            sl = slice(tt * 128, tt * 128 + 128)
            p.dma("sync", (lambda sl: lambda e: e.dma_start(out=vh[:], in_=vtok_d[sl, :]))(sl))
            p.barrier()
            for h in range(H):
                oc, hb = h // 2, (h % 2) * 64
                pss = [pss0, pss1][h % 2]
                p.op("tensor", (lambda oc, hb, tt, pss: lambda e: e.matmul(
                    pss[:, 0:128], qfm[hb:hb + 64, oc * T + tt * 128: oc * T + tt * 128 + 128],
                    kfm[hb:hb + 64, oc * T + tt * 128: oc * T + tt * 128 + 128],
                    start=True, stop=True))(oc, hb, tt, pss))
                p.barrier()
                p.op("vector", (lambda h, pss: lambda e: e.tensor_add(
                    swide[:, h * 128:(h + 1) * 128], pss[:, 0:128], mask2[:]))(h, pss))
                p.barrier()
            p.op("vector", lambda e: e.reduce_max(hmax[:], swide.rearrange("p (h w) -> p h w", h=H),
                                                  axis=AX.X, negate=True))
            p.barrier()
            p.op("vector", lambda e: e.tensor_tensor(
                swide.rearrange("p (h w) -> p h w", h=H),
                swide.rearrange("p (h w) -> p h w", h=H),
                hmax.rearrange("p (h o) -> p h o", o=1).broadcast_to([128, H, 128]),
                ALU.add))
            p.barrier()
            p.op("scalar", lambda e: e.activation(pwide[:], swide[:], AF.Exp))
            p.barrier()
            p.op("vector", lambda e: e.reduce_sum(zrow[:, 0:H],
                                                  pwide.rearrange("p (h w) -> p h w", h=H), axis=AX.X))
            p.barrier()
            p.op("vector", lambda e: e.reciprocal(zrow[:, H:2 * H], zrow[:, 0:H]))
            p.barrier()
            for h in range(H):
                p.op("tensor", (lambda h: lambda e: e.transpose(
                    psT[:], pwide[:, h * 128:(h + 1) * 128], idn[:]))(h))
                p.barrier()
                p.op("vector", lambda e: e.tensor_copy(pT0[:], psT[:]))
                p.barrier()
                p.op("tensor", (lambda h: lambda e: e.matmul(
                    psa[:, 0:64], pT0[:], vh[:, h * 64:h * 64 + 64],
                    start=True, stop=True))(h))
                p.barrier()
                p.op("vector", (lambda h: lambda e: e.tensor_scalar(
                    st_[:, h * 64:h * 64 + 64], psa[:, 0:64], zrow[:, H + h:H + h + 1],
                    None, ALU.mult))(h))
                p.barrier()
            p.op("vector", lambda e: e.tensor_copy(bft[:], st_[:]))
            p.barrier()
            p.dma("sync", (lambda sl: lambda e: e.dma_start(out=attn_d[sl, :], in_=bft[:]))(sl))
            p.barrier()

        attnfm = qfm
        for kc in range(ND):
            p.dma("sync", (lambda kc: lambda e: e.dma_start(
                out=attnfm[:, kc * T:(kc + 1) * T], in_=attn_d[:, kc * 128:(kc + 1) * 128],
                transpose=True))(kc))
        p.barrier()
        for tt in range(NT):
            sl = slice(tt * 128, tt * 128 + 128)
            for nb in range(2):
                ps = [ps0, ps1][nb]
                for kc in range(ND):
                    sslot = (kc % 2) * 1024
                    p.dma("gpsimd", (lambda kc, nb, sslot: lambda e: e.dma_start(
                        out=wstr[:, sslot:sslot + 512],
                        in_=woutT[kc * 128:(kc + 1) * 128, nb * 512:nb * 512 + 512]))(kc, nb, sslot))
                    p.barrier()
                    p.op("tensor", (lambda tt, kc, nb, ps, sslot: lambda e: e.matmul(
                        ps[:, 0:512],
                        attnfm[:, kc * T + tt * 128: kc * T + tt * 128 + 128],
                        wstr[:, sslot:sslot + 512],
                        start=(kc == 0), stop=(kc == ND - 1)))(tt, kc, nb, ps, sslot))
                    p.barrier()
                p.barrier()
            p.dma("sync", (lambda sl: lambda e: e.dma_start(out=xnt[:], in_=xn_d[sl, :]))(sl))
            p.barrier()
            for nb in range(2):
                ps = [ps0, ps1][nb]
                p.op("vector", (lambda nb, ps: lambda e: e.tensor_add(
                    st_[:, nb * 512:nb * 512 + 512], ps[:, 0:512],
                    xnt[:, nb * 512:nb * 512 + 512]))(nb, ps))
                p.barrier()
            p.dma("sync", (lambda sl: lambda e: e.dma_start(out=sg_d[sl, :], in_=st_[:]))(sl))
            p.barrier()

        p.dma("sync", lambda e: e.dma_start(out=sg_d[1:2, 0:D] if False else sqt[1:2, 0:D],
                                            in_=sg_d[1:2, 0:D]))
        p.barrier()
        bnbc = es.enter_context(nc.sbuf_tensor(pfx + "bnbc", [128, ND], BF16))
        p.dma("sync", lambda e: e.dma_start(out=bnc_d[1:2, 0:D], in_=statrow[0:1, 0:D]))
        p.barrier()
        p.dma("gpsimd", lambda e: e.dma_start(out=bnbc.rearrange("p (k o) -> p k o", o=1),
                                              in_=bnc_d[1:2, 0:D].rearrange("o (k p) -> p k o", p=128),
                                              allow_slow_non_contiguous=True))
        p.barrier()
        for nb in range(4):
            for kc in range(ND):
                sslot = (kc % 2) * 1024
                p.dma("gpsimd", (lambda nb, kc, sslot: lambda e: e.dma_start(
                    out=wstr[:, sslot:sslot + 512],
                    in_=wqT[kc * 128:(kc + 1) * 128, nb * 512:nb * 512 + 512]))(nb, kc, sslot))
                p.barrier()
                p.op("tensor", (lambda kc, sslot: lambda e: e.matmul(
                    ps0[0:1, 0:512], bnbc[:, kc:kc + 1], wstr[:, sslot:sslot + 512],
                    start=(kc == 0), stop=(kc == ND - 1)))(kc, sslot))
                p.barrier()
            p.op("vector", (lambda nb: lambda e: e.tensor_copy(
                brow2b[0:1, nb * 512:nb * 512 + 512], ps0[0:1, 0:512]))(nb))
            p.barrier()
        bcol2 = es.enter_context(nc.sbuf_tensor(pfx + "bcol2", [128, 16], F32))
        p.dma("gpsimd", lambda e: e.dma_start(out=bnc_d[2:3, 0:D], in_=brow2b[0:1, 0:D]))
        p.dma("gpsimd", lambda e: e.dma_start(out=bnc_d[3:4, 0:D], in_=brow2b[0:1, D:2 * D]))
        p.barrier()
        p.dma("sync", lambda e: e.dma_start(out=bcol2[:, 0:8].rearrange("p (k o) -> p k o", o=1),
                                            in_=bnc_d[2:3, 0:D].rearrange("o (k p) -> p k o", p=128),
                                            allow_slow_non_contiguous=True))
        p.dma("sync", lambda e: e.dma_start(out=bcol2[:, 8:16].rearrange("p (k o) -> p k o", o=1),
                                            in_=bnc_d[3:4, 0:D].rearrange("o (k p) -> p k o", p=128),
                                            allow_slow_non_contiguous=True))
        p.barrier()
        q2tb = es.enter_context(nc.sbuf_tensor(pfx + "q2tb", [128, 16 * 512], BF16))
        xsl2 = es.enter_context(nc.sbuf_tensor(pfx + "xsl2", [128, 512], BF16))
        simt = es.enter_context(nc.sbuf_tensor(pfx + "simt2", [128, NK], F32))
        sxb = es.enter_context(nc.sbuf_tensor(pfx + "sxb", [128, 16 * 8], F32))
        gt4 = [es.enter_context(nc.sbuf_tensor(pfx + f"gath{i}", [128, D], BF16)) for i in range(2)]
        for tb in range(4):
            for ocg in range(4):
                for kc in range(ND):
                    p.dma("sync", (lambda kc, tb: lambda e: e.dma_start(
                        out=xsl2[:], in_=xnfm_d[kc, :, tb * 512:(tb + 1) * 512]))(kc, tb))
                    p.barrier()
                    for oi in range(4):
                        oc = ocg * 4 + oi
                        sslot = (oi % 2) * 1024
                        p.dma("gpsimd", (lambda oc, kc, sslot: lambda e: e.dma_start(
                            out=wstr[:, sslot:sslot + 128],
                            in_=wqT[kc * 128:(kc + 1) * 128, oc * 128: oc * 128 + 128]))(oc, kc, sslot))
                        p.barrier()
                        p.op("vector", (lambda kc, sslot: lambda e: e.tensor_scalar(
                            wstr[:, sslot:sslot + 128], wstr[:, sslot:sslot + 128],
                            gbn[:, kc:kc + 1], None, ALU.mult))(kc, sslot))
                        p.barrier()
                        p.op("tensor", (lambda oi, kc, sslot: lambda e: e.matmul(
                            [ps0, ps1, ps2, ps3][oi][:, 0:512], wstr[:, sslot:sslot + 128],
                            xsl2[:], start=(kc == 0), stop=(kc == ND - 1)))(oi, kc, sslot))
                        p.barrier()
                for oi in range(4):
                    oc = ocg * 4 + oi
                    p.op("scalar", (lambda oc, oi: lambda e: e.activation(
                        q2tb[:, oc * 512:(oc + 1) * 512], [ps0, ps1, ps2, ps3][oi][:, 0:512],
                        AF.Identity, bias=bcol2[:, oc:oc + 1], scale=1.0))(oc, oi))
                    p.barrier()
            for rt in range(4):
                tt = tb * 4 + rt
                sl = slice(tt * 128, tt * 128 + 128)
                for oc in range(16):
                    pss = [pss0, pss1][oc % 2]
                    p.op("tensor", (lambda oc, rt, pss: lambda e: e.matmul(
                        pss[:, 0:128],
                        q2tb[:, oc * 512 + rt * 128: oc * 512 + rt * 128 + 128],
                        keyssb[:, oc * NK: oc * NK + 128], start=True, stop=True))(oc, rt, pss))
                    p.barrier()
                    p.op("vector", (lambda pss: lambda e: e.tensor_copy(
                        simt[:, 0:128], pss[:, 0:128]))(pss))
                    p.barrier()
                    p.op("tensor", (lambda oc, rt, pss: lambda e: e.matmul(
                        pss[:, 0:128],
                        q2tb[:, oc * 512 + rt * 128: oc * 512 + rt * 128 + 128],
                        keyssb[:, oc * NK + 128: oc * NK + 256], start=True, stop=True))(oc, rt, pss))
                    p.barrier()
                    p.op("vector", (lambda pss: lambda e: e.tensor_copy(
                        simt[:, 128:256], pss[:, 0:128]))(pss))
                    p.barrier()
                    p.op("vector", (lambda oc: lambda e: e.max(
                        sxb[:, oc * 8:(oc + 1) * 8], simt[:, 0:NK]))(oc))
                    p.barrier()
                    p.op("vector", (lambda oc: lambda e: e.max_index(
                        ix8[:, oc * 8:(oc + 1) * 8], sxb[:, oc * 8:(oc + 1) * 8],
                        simt[:, 0:NK]))(oc))
                    p.barrier()
                for h in range(8):
                    p.op("vector", (lambda h: lambda e: e.tensor_tensor(
                        asc.rearrange("p (h a b) -> p h a b", h=8, a=8)[:, h, :, :],
                        sxb[:, h * 8:(h + 1) * 8].rearrange("p (a b) -> p a b", b=1).broadcast_to([128, 8, 8]),
                        sxb[:, 64 + h * 8: 64 + (h + 1) * 8].rearrange("p (a b) -> p a b", a=1).broadcast_to([128, 8, 8]),
                        ALU.add))(h))
                    p.barrier()
                for h in range(8):
                    p.op("vector", (lambda h: lambda e: e.max(
                        st8[:, h * 8:(h + 1) * 8], asc[:, h * 64:(h + 1) * 64]))(h))
                    p.barrier()
                    p.op("vector", (lambda h: lambda e: e.max_index(
                        pk8[:, h * 8:(h + 1) * 8], st8[:, h * 8:(h + 1) * 8],
                        asc[:, h * 64:(h + 1) * 64]))(h))
                    p.barrier()
                p.op("vector", lambda e: e.tensor_scalar(kxi[:], pk8[:], 3, None, ALU.logical_shift_right))
                p.barrier()
                p.op("vector", lambda e: e.tensor_scalar(pk8[:], pk8[:], 7, None, ALU.bitwise_and))
                p.barrier()
                p.op("vector", lambda e: e.memset(idxf[:], 0))
                p.barrier()
                for j0 in range(8):
                    p.op("vector", (lambda j0: lambda e: e.tensor_scalar(
                        msk[:], kxi[:], j0, None, ALU.is_equal))(j0))
                    p.barrier()
                    p.op("vector", (lambda j0: lambda e: e.tensor_tensor(
                        iacc.rearrange("p (h k) -> p h k", h=8), msk.rearrange("p (h k) -> p h k", h=8),
                        ix8.rearrange("p (s h k) -> p s h k", s=2, h=8)[:, 0, :, j0:j0 + 1].broadcast_to([128, 8, 8]),
                        ALU.mult))(j0))
                    p.barrier()
                    p.op("vector", lambda e: e.tensor_add(idxf[:], idxf[:], iacc[:]))
                    p.barrier()
                p.op("vector", lambda e: e.tensor_scalar(idxf[:], idxf[:], 8, None, ALU.logical_shift_left))
                p.barrier()
                for j0 in range(8):
                    p.op("vector", (lambda j0: lambda e: e.tensor_scalar(
                        msk[:], pk8[:], j0, None, ALU.is_equal))(j0))
                    p.barrier()
                    p.op("vector", (lambda j0: lambda e: e.tensor_tensor(
                        iacc.rearrange("p (h k) -> p h k", h=8), msk.rearrange("p (h k) -> p h k", h=8),
                        ix8.rearrange("p (s h k) -> p s h k", s=2, h=8)[:, 1, :, j0:j0 + 1].broadcast_to([128, 8, 8]),
                        ALU.mult))(j0))
                    p.barrier()
                    p.op("vector", lambda e: e.tensor_add(idxf[:], idxf[:], iacc[:]))
                    p.barrier()
                p.op("vector", lambda e: e.tensor_copy(idxi[:], idxf[:]))
                p.barrier()
                if tt == 0:
                    p.op("vector", lambda e: e.tensor_copy(sqt[:, 0:64], idxf[:]))
                    p.barrier()
                    p.dma("sync", lambda e: e.dma_start(out=dbg_d[:, 1024:1088], in_=sqt[:, 0:64]))
                    p.barrier()
                    p.op("vector", lambda e: e.tensor_copy(sqt[:, 0:64], st8[:]))
                    p.barrier()
                    p.dma("sync", lambda e: e.dma_start(out=dbg_d[:, 1088:1152], in_=sqt[:, 0:64]))
                    p.barrier()
                p.op("vector", lambda e: e.tensor_tensor(
                    cmb.rearrange("p (h k) -> p h k", h=8),
                    st8.rearrange("p (h k) -> p h k", h=8),
                    st8.rearrange("p (h k) -> p h k", h=8)[:, :, 0:1].broadcast_to([128, 8, 8]),
                    ALU.subtract))
                p.barrier()
                p.op("scalar", lambda e: e.activation(cmb[:], cmb[:], AF.Exp))
                p.barrier()
                p.op("vector", lambda e: e.reduce_sum(scrj[:, 0:8], cmb.rearrange("p (h k) -> p h k", h=8),
                                                      axis=AX.X))
                p.barrier()
                p.op("vector", lambda e: e.reciprocal(scrj[:, 8:16], scrj[:, 0:8]))
                p.barrier()
                p.op("vector", lambda e: e.tensor_tensor(
                    cmb.rearrange("p (h k) -> p h k", h=8),
                    cmb.rearrange("p (h k) -> p h k", h=8),
                    scrj[:, 8:16].rearrange("p (h o) -> p h o", o=1).broadcast_to([128, 8, 8]),
                    ALU.mult))
                p.barrier()
                p.dma("sync", (lambda sl: lambda e: e.dma_start(out=xbft[:], in_=lnbf_d[sl, :]))(sl))
                p.barrier()
                for sgrp in range(32):
                    for j in range(2):
                        slot = sgrp * 2 + j
                        p.dma("gpsimd", (lambda slot, j: lambda e: e.indirect_dma_start(
                            out=gt4[j][:], out_offset=None, in_=key_in[:, :],
                            in_offset=IndirectOffsetOnAxis(idxi[:, slot:slot + 1], 0)))(slot, j))
                    p.barrier()
                    for j in range(2):
                        slot = sgrp * 2 + j
                        p.op("vector", (lambda slot, j: lambda e: e.scalar_tensor_tensor(
                            bft[:], gt4[j][:], 1.0, xbft[:], ALU.mult, ALU.mult,
                            accum_out=ocol[:, slot:slot + 1]))(slot, j))
                        p.barrier()
                if tt == 0:
                    p.dma("sync", lambda e: e.dma_start(out=dbg_d[:, 1152:1216], in_=ocol[:]))
                    p.barrier()
                    p.dma("sync", lambda e: e.dma_start(out=dbg_d[:, 1216:1280], in_=cmb[:]))
                    p.barrier()
                p.op("scalar", lambda e: e.activation(ocol[:], ocol[:], AF.Gelu))
                p.barrier()
                p.op("vector", lambda e: e.tensor_mul(cmb[:], cmb[:], ocol[:]))
                p.barrier()
                if tt == 0:
                    p.dma("sync", lambda e: e.dma_start(out=dbg_d[:, 1280:1344], in_=ocol[:]))
                    p.barrier()
                    p.dma("sync", lambda e: e.dma_start(out=dbg_d[:, 1344:1408], in_=cmb[:]))
                    p.barrier()
                p.op("vector", lambda e: e.memset(yacc[:], 0.0))
                p.barrier()
                for sgrp in range(32):
                    for j in range(2):
                        slot = sgrp * 2 + j
                        p.dma("gpsimd", (lambda slot, j: lambda e: e.indirect_dma_start(
                            out=gt4[j][:], out_offset=None, in_=key_out[:, :],
                            in_offset=IndirectOffsetOnAxis(idxi[:, slot:slot + 1], 0)))(slot, j))
                    p.barrier()
                    for j in range(2):
                        slot = sgrp * 2 + j
                        p.op("vector", (lambda slot, j: lambda e: e.scalar_tensor_tensor(
                            yacc[:], gt4[j][:], cmb[:, slot:slot + 1], yacc[:],
                            ALU.mult, ALU.add))(slot, j))
                        p.barrier()
                if tt == 0:
                    p.dma("sync", lambda e: e.dma_start(out=dbg_d[:, 1408:2432], in_=yacc[:]))
                    p.barrier()
                p.op("scalar", lambda e: e.activation(gt_[:], yacc[:], AF.Silu))
                p.barrier()
                p.dma("sync", (lambda sl: lambda e: e.dma_start(out=st_[:], in_=sg_d[sl, :]))(sl))
                p.dma("sync", (lambda ap: lambda e: e.dma_start(out=xt[:], in_=ap))(xin_tile(tt)))
                p.barrier()
                p.op("vector", lambda e: e.tensor_mul(st_[:], st_[:], gt_[:]))
                p.barrier()
                p.op("vector", lambda e: e.tensor_add(st_[:], st_[:], xt[:]))
                p.barrier()
                p.dma("sync", (lambda ap: lambda e: e.dma_start(out=ap, in_=st_[:]))(xout_tile(tt)))
                p.barrier()
        if standalone:
            p.build()
    es.close()
    return nc

def build_l3(nc, p=None, pfx="", xin_tile=None, xout_tile=None):
    H, HD, MIX = 16, 64, 0.05
    E, I2 = 32, 512

    if xin_tile is None:
        _xin = nc.dram_tensor(pfx + "x3", [T, D], F32, kind="ExternalInput").ap()
        xin_tile = lambda tt: _xin[tt * 128:(tt + 1) * 128, :]
    wqT = nc.dram_tensor(pfx + "wqT", [D, D], F32, kind="ExternalInput").ap()
    wkT = nc.dram_tensor(pfx + "wkT", [D, D], F32, kind="ExternalInput").ap()
    wvT = nc.dram_tensor(pfx + "wvT", [D, D], F32, kind="ExternalInput").ap()
    wgT = nc.dram_tensor(pfx + "wgT", [D, D], F32, kind="ExternalInput").ap()
    woT = nc.dram_tensor(pfx + "woT", [D, D], F32, kind="ExternalInput").ap()
    wbfsT = nc.dram_tensor(pfx + "wbfsT", [D, 48], F32, kind="ExternalInput").ap()
    bfs = nc.dram_tensor(pfx + "bfs", [48], F32, kind="ExternalInput").ap()
    gatewT = nc.dram_tensor(pfx + "gatewT", [D, E], F32, kind="ExternalInput").ap()
    gateb = nc.dram_tensor(pfx + "gateb", [E], F32, kind="ExternalInput").ap()
    wgate = nc.dram_tensor(pfx + "wgate", [E, D, I2], F32, kind="ExternalInput").ap()
    wup = nc.dram_tensor(pfx + "wup", [E, D, I2], F32, kind="ExternalInput").ap()
    wdownT = nc.dram_tensor(pfx + "wdownT", [E, I2, D], F32, kind="ExternalInput").ap()
    shgT = nc.dram_tensor(pfx + "shgT", [D, D], F32, kind="ExternalInput").ap()
    shuT = nc.dram_tensor(pfx + "shuT", [D, D], F32, kind="ExternalInput").ap()
    shdT = nc.dram_tensor(pfx + "shdT", [D, D], F32, kind="ExternalInput").ap()

    if xout_tile is None:
        _xout = nc.dram_tensor(pfx + "x4", [T, D], F32, kind="ExternalOutput").ap()
        xout_tile = lambda tt: _xout[tt * 128:(tt + 1) * 128, :]

    xn_d = nc.dram_tensor(pfx + "xn_d", [T, D], F32, kind="Internal").ap()
    lnbf_d = nc.dram_tensor(pfx + "lnbf_d", [T, D], BF16, kind="Internal").ap()
    q_d = nc.dram_tensor(pfx + "q_d", [T, D], BF16, kind="Internal").ap()
    k_d = nc.dram_tensor(pfx + "k_d", [T, D], BF16, kind="Internal").ap()
    v_d = nc.dram_tensor(pfx + "v_d", [T, D], BF16, kind="Internal").ap()
    gg_d = nc.dram_tensor(pfx + "gg_d", [T, D], BF16, kind="Internal").ap()
    fsb_d = nc.dram_tensor(pfx + "fsb_d", [T, 48], F32, kind="Internal").ap()
    o_d = nc.dram_tensor(pfx + "o_d", [T, D], BF16, kind="Internal").ap()
    vin_d = nc.dram_tensor(pfx + "vin_d", [128, H, 2, HD], F32, kind="Internal").ap()
    vcar_d = nc.dram_tensor(pfx + "vcar_d", [128, H, 2, HD], F32, kind="Internal").ap()
    og_d = nc.dram_tensor(pfx + "og_d", [8, 128, T], BF16, kind="Internal").ap()
    s3_d = nc.dram_tensor(pfx + "s3_d", [T, D], F32, kind="Internal").ap()

    standalone = p is None
    if standalone:
        p = Prog(nc)
    es = contextlib.ExitStack()
    es2 = contextlib.ExitStack()
    xnfm = es.enter_context(nc.sbuf_tensor(pfx + "xnfm", [128, ND * T], BF16))
    wstr = es.enter_context(nc.sbuf_tensor(pfx + "wstr", [128, 2 * 1024], BF16))
    onesrow = es.enter_context(nc.sbuf_tensor(pfx + "onesrow", [1, 512], BF16))
    onesf = es.enter_context(nc.sbuf_tensor(pfx + "onesf", [1, 128], F32))
    epst = es.enter_context(nc.sbuf_tensor(pfx + "epst", [128, 4], F32))
    stat = es.enter_context(nc.sbuf_tensor(pfx + "stat", [128, 8], F32))
    xt = es.enter_context(nc.sbuf_tensor(pfx + "xt", [128, D], F32))
    xnt = es.enter_context(nc.sbuf_tensor(pfx + "xnt", [128, D], F32))
    sqt = es.enter_context(nc.sbuf_tensor(pfx + "sqt", [128, D], F32))
    bft = es.enter_context(nc.sbuf_tensor(pfx + "bft", [128, D], BF16))
    bfsrow = es.enter_context(nc.sbuf_tensor(pfx + "bfsrow", [1, 64], BF16))
    qg = es.enter_context(nc.sbuf_tensor(pfx + "qg", [128, 16 * 256], BF16))
    ug = es.enter_context(nc.sbuf_tensor(pfx + "ug", [128, 16 * 256], BF16))
    idnf = es.enter_context(nc.sbuf_tensor(pfx + "idnf", [128, 128], F32))
    idnb = es.enter_context(nc.sbuf_tensor(pfx + "idnb", [128, 128], BF16))
    cmb32 = es.enter_context(nc.sbuf_tensor(pfx + "cmb32", [128, NT * E], F32))
    grow = es.enter_context(nc.sbuf_tensor(pfx + "grow", [128, E], F32))
    st_ = es.enter_context(nc.sbuf_tensor(pfx + "st_", [128, D], F32))
    gt_ = es.enter_context(nc.sbuf_tensor(pfx + "gt_", [128, D], F32))
    ps0 = es.enter_context(nc.psum_tensor(pfx + "ps0", [128, 512], F32))
    ps1 = es.enter_context(nc.psum_tensor(pfx + "ps1", [128, 512], F32))
    ps2 = es.enter_context(nc.psum_tensor(pfx + "ps2", [128, 512], F32))
    ps3 = es.enter_context(nc.psum_tensor(pfx + "ps3", [128, 512], F32))
    psF = es.enter_context(nc.psum_tensor(pfx + "psF", [128, 128], F32))
    PSA = [ps0, ps1, ps2, ps3]

    wbf_sb = es.enter_context(nc.sbuf_tensor(pfx + "wbf_sb", [128, ND * 48], BF16))
    og = es2.enter_context(nc.sbuf_tensor(pfx + "og", [128, 16 * 256], F32))
    kg = es2.enter_context(nc.sbuf_tensor(pfx + "kg", [128, 16 * 256], BF16))
    fst = es2.enter_context(nc.sbuf_tensor(pfx + "fst", [128, 16 * 48], F32))
    Sfc = es2.enter_context(nc.sbuf_tensor(pfx + "Sfc", [128, 256], F32))
    Ssc = es2.enter_context(nc.sbuf_tensor(pfx + "Ssc", [128, 256], F32))
    t1g = es2.enter_context(nc.sbuf_tensor(pfx + "t1g", [128, 256], F32))
    t2g = es2.enter_context(nc.sbuf_tensor(pfx + "t2g", [128, 256], F32))
    t3g = es2.enter_context(nc.sbuf_tensor(pfx + "t3g", [128, 256], F32))
    M11 = es2.enter_context(nc.sbuf_tensor(pfx + "M11", [128, 16], F32))
    M12 = es2.enter_context(nc.sbuf_tensor(pfx + "M12", [128, 16], F32))
    M21 = es2.enter_context(nc.sbuf_tensor(pfx + "M21", [128, 16], F32))
    M22 = es2.enter_context(nc.sbuf_tensor(pfx + "M22", [128, 16], F32))
    Ma = es2.enter_context(nc.sbuf_tensor(pfx + "Ma", [128, 16], F32))
    Mb = es2.enter_context(nc.sbuf_tensor(pfx + "Mb", [128, 16], F32))
    r1 = es2.enter_context(nc.sbuf_tensor(pfx + "r1", [128, 16 * 16], F32))
    r2 = es2.enter_context(nc.sbuf_tensor(pfx + "r2", [128, 16 * 16], F32))
    T11 = es2.enter_context(nc.sbuf_tensor(pfx + "T11", [128, 16], F32))
    T12 = es2.enter_context(nc.sbuf_tensor(pfx + "T12", [128, 16], F32))
    T21 = es2.enter_context(nc.sbuf_tensor(pfx + "T21", [128, 16], F32))
    T22 = es2.enter_context(nc.sbuf_tensor(pfx + "T22", [128, 16], F32))
    T11c = es2.enter_context(nc.sbuf_tensor(pfx + "T11c", [16, 128], F32))
    T12c = es2.enter_context(nc.sbuf_tensor(pfx + "T12c", [16, 128], F32))
    T21c = es2.enter_context(nc.sbuf_tensor(pfx + "T21c", [16, 128], F32))
    T22c = es2.enter_context(nc.sbuf_tensor(pfx + "T22c", [16, 128], F32))
    Sfh = es2.enter_context(nc.sbuf_tensor(pfx + "Sfh", [16, HD], F32))
    Ssh = es2.enter_context(nc.sbuf_tensor(pfx + "Ssh", [16, HD], F32))
    vstg = es2.enter_context(nc.sbuf_tensor(pfx + "vstg", [16, 8 * 2 * HD], F32))
    cstg = es2.enter_context(nc.sbuf_tensor(pfx + "cstg", [16, 8 * 2 * HD], F32))
    if True:
        p.op("vector", lambda e: e.memset(onesrow[:], 1.0))
        p.op("gpsimd", lambda e: e.memset(onesf[:], 1.0))
        p.op("scalar", lambda e: e.memzero(epst[:]))
        p.barrier()
        p.op("vector", lambda e: e.memset(epst[:, 0:1], 1e-5))
        p.op("gpsimd", lambda e: e.memset(idnb[:], 1.0))
        p.barrier()
        p.op("gpsimd", lambda e: e.affine_select(idnb[:], idnb[:], [[-1, 128]], ALU.is_equal,
                                                 0.0, base=0, channel_multiplier=1))
        p.barrier()
        p.op("vector", lambda e: e.tensor_copy(idnf[:], idnb[:]))
        p.barrier()
        p.dma("gpsimd", lambda e: e.dma_start(out=bfsrow[0:1, 0:48],
                                              in_=bfs.rearrange("(o d) -> o d", o=1)))
        p.barrier()

        for tt in range(NT):
            sl = slice(tt * 128, tt * 128 + 128)
            p.dma("sync", (lambda ap: lambda e: e.dma_start(out=xt[:], in_=ap))(xin_tile(tt)))
            p.barrier()
            emit_ln(p, xt, xnt, sqt, stat, epst[:, 0:1])
            p.op("vector", lambda e: e.tensor_copy(bft[:], xnt[:]))
            p.barrier()
            p.dma("sync", (lambda sl: lambda e: e.dma_start(out=xn_d[sl, :], in_=xnt[:]))(sl))
            p.dma("sync", (lambda sl: lambda e: e.dma_start(out=lnbf_d[sl, :], in_=bft[:]))(sl))
            p.barrier()
        for kc in range(ND):
            p.dma("sync", (lambda kc: lambda e: e.dma_start(
                out=xnfm[:, kc * T:(kc + 1) * T], in_=lnbf_d[:, kc * 128:(kc + 1) * 128],
                transpose=True))(kc))
        p.barrier()

        def proj_tok(wT, out_d, act):
            for grp in range(4):
                tts = list(range(grp * 4, grp * 4 + 4))
                for nb in range(2):
                    for kc in range(ND):
                        sslot = (kc % 2) * 1024
                        p.dma("gpsimd", (lambda kc, nb, sslot, wT: lambda e: e.dma_start(
                            out=wstr[:, sslot:sslot + 512],
                            in_=wT[kc * 128:(kc + 1) * 128, nb * 512:nb * 512 + 512]))(kc, nb, sslot, wT))
                        p.barrier()
                        for ti, tt in enumerate(tts):
                            p.op("tensor", (lambda kc, ti, tt, sslot: lambda e: e.matmul(
                                PSA[ti][:, 0:512],
                                xnfm[:, kc * T + tt * 128: kc * T + tt * 128 + 128],
                                wstr[:, sslot:sslot + 512],
                                start=(kc == 0), stop=(kc == ND - 1)))(kc, ti, tt, sslot))
                        p.barrier()
                    for ti, tt in enumerate(tts):
                        p.op("scalar", (lambda ti, tt, nb, act: lambda e: e.activation(
                            bft[:, 0:512], PSA[ti][:, 0:512], act))(ti, tt, nb, act))
                        p.barrier()
                        p.dma("sync", (lambda tt, nb, out_d: lambda e: e.dma_start(
                            out=out_d[tt * 128:(tt + 1) * 128, nb * 512:nb * 512 + 512],
                            in_=bft[:, 0:512]))(tt, nb, out_d))
                        p.barrier()

        proj_tok(wqT, q_d, AF.Silu)
        proj_tok(wkT, k_d, AF.Silu)
        proj_tok(wvT, v_d, AF.Silu)
        proj_tok(wgT, gg_d, AF.Sigmoid)

        p.dma("gpsimd", lambda e: e.dma_start(out=wbf_sb.rearrange("p (k n) -> p k n", k=ND),
                                              in_=wbfsT.rearrange("(k p) n -> p k n", p=128)))
        p.barrier()
        for tt in range(NT):
            for kc in range(ND):
                p.op("tensor", (lambda tt, kc: lambda e: e.matmul(
                    ps0[:, 0:48], xnfm[:, kc * T + tt * 128: kc * T + tt * 128 + 128],
                    wbf_sb[:, kc * 48:(kc + 1) * 48], start=(kc == 0), stop=False))(tt, kc))
            p.op("tensor", lambda e: e.matmul(ps0[:, 0:48], onesrow[0:1, 0:128],
                                              bfsrow[0:1, 0:48], start=False, stop=True))
            p.barrier()
            p.op("scalar", lambda e: e.activation(sqt[:, 0:48], ps0[:, 0:48], AF.Sigmoid))
            p.barrier()
            p.dma("sync", (lambda tt: lambda e: e.dma_start(
                out=fsb_d[tt * 128:(tt + 1) * 128, :], in_=sqt[:, 0:48]))(tt))
            p.barrier()

        for g4 in range(4):
            hs = g4 * 4
            cols = slice(hs * 64, hs * 64 + 256)
            p.dma("sync", (lambda cols: lambda e: e.dma_start(
                out=qg.rearrange("p (t n) -> p t n", t=16),
                in_=q_d[:, cols].rearrange("(c t) n -> c t n", t=16)))(cols))
            p.dma("sync", (lambda cols: lambda e: e.dma_start(
                out=kg.rearrange("p (t n) -> p t n", t=16),
                in_=k_d[:, cols].rearrange("(c t) n -> c t n", t=16)))(cols))
            p.dma("sync", (lambda cols: lambda e: e.dma_start(
                out=ug.rearrange("p (t n) -> p t n", t=16),
                in_=v_d[:, cols].rearrange("(c t) n -> c t n", t=16)))(cols))
            p.dma("sync", lambda e: e.dma_start(
                out=fst.rearrange("p (t n) -> p t n", t=16),
                in_=fsb_d.rearrange("(c t) n -> c t n", t=16)))
            p.barrier()
            p.op("vector", lambda e: e.tensor_mul(ug[:], ug[:], kg[:]))
            p.barrier()
            p.op("vector", (lambda hs: lambda e: e.tensor_tensor(
                ug.rearrange("p (t h d) -> p t h d", t=16, h=4),
                ug.rearrange("p (t h d) -> p t h d", t=16, h=4),
                fst.rearrange("p (t n) -> p t n", t=16)[:, :, hs:hs + 4]
                   .rearrange("p t (h o) -> p t h o", o=1).broadcast_to([128, 16, 4, 64]),
                ALU.mult))(hs))
            p.barrier()
            p.op("vector", lambda e: e.tensor_scalar_mul(ug[:], ug[:], 1.0 + MIX))
            p.barrier()
            p.op("vector", lambda e: e.memset(Sfc[:], 0.0))
            p.op("gpsimd", lambda e: e.memset(Ssc[:], 0.0))
            p.barrier()
            p.op("vector", lambda e: e.memset(r1[:], 0.0))
            p.op("gpsimd", lambda e: e.memset(r2[:], 0.0))
            p.barrier()
            p.op("vector", lambda e: e.memset(M11[:, hs:hs + 4] if False else M11[:, 0:4], 1.0))
            p.op("gpsimd", lambda e: e.memset(M12[:, 0:4], 0.0))
            p.barrier()
            p.op("vector", lambda e: e.memset(M21[:, 0:4], 0.0))
            p.op("gpsimd", lambda e: e.memset(M22[:, 0:4], 1.0))
            p.barrier()
            for t in range(16):
                fcol = fst.rearrange("p (t n) -> p t n", t=16)[:, t, 16 + hs:16 + hs + 4]
                scol = fst.rearrange("p (t n) -> p t n", t=16)[:, t, 32 + hs:32 + hs + 4]
                p.op("vector", (lambda t, fcol, scol, hs: lambda e: e.tensor_mul(
                    Ma[:, 0:4], fcol, M11[:, 0:4]))(t, fcol, scol, hs))
                p.barrier()
                p.op("vector", (lambda t, fcol, scol: lambda e: e.tensor_mul(
                    Mb[:, 0:4], scol, M21[:, 0:4]))(t, fcol, scol))
                p.barrier()
                p.op("vector", (lambda t: lambda e: e.tensor_add(
                    r1.rearrange("p (t n) -> p t n", t=16)[:, t, 0:4], Ma[:, 0:4], Mb[:, 0:4]))(t))
                p.barrier()
                p.op("vector", (lambda t, fcol: lambda e: e.tensor_mul(Ma[:, 0:4], fcol, M12[:, 0:4]))(t, fcol))
                p.barrier()
                p.op("vector", (lambda t, scol: lambda e: e.tensor_mul(Mb[:, 0:4], scol, M22[:, 0:4]))(t, scol))
                p.barrier()
                p.op("vector", (lambda t: lambda e: e.tensor_add(
                    r2.rearrange("p (t n) -> p t n", t=16)[:, t, 0:4], Ma[:, 0:4], Mb[:, 0:4]))(t))
                p.barrier()
                p.op("vector", (lambda fcol: lambda e: e.tensor_tensor(
                    t1g.rearrange("p (h d) -> p h d", h=4), Sfc.rearrange("p (h d) -> p h d", h=4),
                    fcol.rearrange("p (h o) -> p h o", o=1).broadcast_to([128, 4, 64]),
                    ALU.mult))(fcol))
                p.barrier()
                p.op("vector", (lambda scol: lambda e: e.tensor_tensor(
                    t2g.rearrange("p (h d) -> p h d", h=4), Ssc.rearrange("p (h d) -> p h d", h=4),
                    scol.rearrange("p (h o) -> p h o", o=1).broadcast_to([128, 4, 64]),
                    ALU.mult))(scol))
                p.barrier()
                p.op("vector", lambda e: e.tensor_add(t3g[:], t1g[:], t2g[:]))
                p.barrier()
                p.op("vector", (lambda t: lambda e: e.scalar_tensor_tensor(
                    og.rearrange("p (t n) -> p t n", t=16)[:, t, :],
                    qg.rearrange("p (t n) -> p t n", t=16)[:, t, :], 0.5, t3g[:],
                    ALU.mult, ALU.mult))(t))
                p.barrier()
                p.op("vector", (lambda t: lambda e: e.scalar_tensor_tensor(
                    Sfc[:], t2g[:], MIX, t1g[:], ALU.mult, ALU.add))(t))
                p.barrier()
                p.op("vector", (lambda t: lambda e: e.tensor_add(
                    Sfc[:], Sfc[:], ug.rearrange("p (t n) -> p t n", t=16)[:, t, :]))(t))
                p.barrier()
                p.op("vector", (lambda t: lambda e: e.scalar_tensor_tensor(
                    Ssc[:], t1g[:], MIX, t2g[:], ALU.mult, ALU.add))(t))
                p.barrier()
                p.op("vector", (lambda t: lambda e: e.tensor_add(
                    Ssc[:], Ssc[:], ug.rearrange("p (t n) -> p t n", t=16)[:, t, :]))(t))
                p.barrier()
                for (ca, cb) in ((M11, M21), (M12, M22)):
                    p.op("vector", (lambda fcol, ca: lambda e: e.tensor_mul(Ma[:, 0:4], fcol, ca[:, 0:4]))(fcol, ca))
                    p.barrier()
                    p.op("vector", (lambda scol, cb: lambda e: e.tensor_mul(Mb[:, 0:4], scol, cb[:, 0:4]))(scol, cb))
                    p.barrier()
                    p.op("vector", (lambda ca: lambda e: e.scalar_tensor_tensor(
                        ca[:, 0:4], Mb[:, 0:4], MIX, Ma[:, 0:4], ALU.mult, ALU.add))(ca))
                    p.barrier()
                    p.op("vector", (lambda cb: lambda e: e.scalar_tensor_tensor(
                        cb[:, 0:4], Ma[:, 0:4], MIX, Mb[:, 0:4], ALU.mult, ALU.add))(cb))
                    p.barrier()
            p.op("vector", (lambda hs: lambda e: e.tensor_copy(T11[:, hs:hs + 4], M11[:, 0:4]))(hs))
            p.op("gpsimd", (lambda hs: lambda e: e.tensor_copy(T12[:, hs:hs + 4], M12[:, 0:4]))(hs))
            p.barrier()
            p.op("vector", (lambda hs: lambda e: e.tensor_copy(T21[:, hs:hs + 4], M21[:, 0:4]))(hs))
            p.op("gpsimd", (lambda hs: lambda e: e.tensor_copy(T22[:, hs:hs + 4], M22[:, 0:4]))(hs))
            p.barrier()
            p.dma("sync", (lambda hs: lambda e: e.dma_start(
                out=vin_d[:, hs:hs + 4, 0, :], in_=Sfc.rearrange("p (h d) -> p h d", h=4)))(hs))
            p.dma("sync", (lambda hs: lambda e: e.dma_start(
                out=vin_d[:, hs:hs + 4, 1, :], in_=Ssc.rearrange("p (h d) -> p h d", h=4)))(hs))
            p.barrier()
            p.op("vector", lambda e: e.tensor_copy(qg[:], og[:]))
            p.barrier()
            p.dma("sync", (lambda cols: lambda e: e.dma_start(
                out=o_d[:, cols].rearrange("(c t) n -> c t n", t=16),
                in_=qg.rearrange("p (t n) -> p t n", t=16)))(cols))
            p.barrier()
            p.dma("sync", (lambda g4: lambda e: e.dma_start(
                out=fsb_d[0:0 + 128, 0:0 + 0] if False else vcar_d[:, g4, 0, 0:32].rearrange("c (a b) -> c a b", a=2) if False else s3_d[g4 * 512:g4 * 512 + 128, 0:256],
                in_=r1[:]))(g4))
            p.dma("gpsimd", (lambda g4: lambda e: e.dma_start(
                out=s3_d[g4 * 512 + 128: g4 * 512 + 256, 0:256], in_=r2[:]))(g4))
            p.barrier()

        for (src_, dst_) in ((T11, T11c), (T12, T12c), (T21, T21c), (T22, T22c)):
            p.op("tensor", (lambda src_: lambda e: e.transpose(
                psF[0:16, 0:128], src_[:], idnf[:]))(src_))
            p.barrier()
            p.op("vector", (lambda dst_: lambda e: e.tensor_copy(dst_[:], psF[0:16, 0:128]))(dst_))
            p.barrier()
        p.op("vector", lambda e: e.memset(Sfh[:], 0.0))
        p.op("gpsimd", lambda e: e.memset(Ssh[:], 0.0))
        p.barrier()
        for cb8 in range(16):
            p.dma("sync", (lambda cb8: lambda e: e.dma_start(
                out=vstg.rearrange("h (c u d) -> h c u d", c=8, u=2),
                in_=vin_d[cb8 * 8:(cb8 + 1) * 8, :, :, :].rearrange("c h u d -> h c u d"),
                allow_slow_non_contiguous=True))(cb8))
            p.barrier()
            for ci in range(8):
                c = cb8 * 8 + ci
                p.op("vector", (lambda ci: lambda e: e.tensor_copy(
                    cstg.rearrange("h (c u d) -> h c u d", c=8, u=2)[:, ci, 0, :], Sfh[:]))(ci))
                p.barrier()
                p.op("vector", (lambda ci: lambda e: e.tensor_copy(
                    cstg.rearrange("h (c u d) -> h c u d", c=8, u=2)[:, ci, 1, :], Ssh[:]))(ci))
                p.barrier()
                p.op("vector", (lambda c: lambda e: e.tensor_scalar(
                    t1g[0:16, 0:HD], Sfh[:], T11c[:, c:c + 1], None, ALU.mult))(c))
                p.barrier()
                p.op("vector", (lambda c: lambda e: e.tensor_scalar(
                    t2g[0:16, 0:HD], Ssh[:], T12c[:, c:c + 1], None, ALU.mult))(c))
                p.barrier()
                p.op("vector", (lambda c: lambda e: e.tensor_scalar(
                    t3g[0:16, 0:HD], Sfh[:], T21c[:, c:c + 1], None, ALU.mult))(c))
                p.barrier()
                p.op("vector", (lambda c: lambda e: e.tensor_scalar(
                    Sfh[:], Ssh[:], T22c[:, c:c + 1], None, ALU.mult))(c))
                p.barrier()
                p.op("vector", lambda e: e.tensor_add(Ssh[:], t3g[0:16, 0:HD], Sfh[:]))
                p.barrier()
                p.op("vector", lambda e: e.tensor_add(Sfh[:], t1g[0:16, 0:HD], t2g[0:16, 0:HD]))
                p.barrier()
                p.op("vector", (lambda ci: lambda e: e.tensor_add(
                    Sfh[:], Sfh[:],
                    vstg.rearrange("h (c u d) -> h c u d", c=8, u=2)[:, ci, 0, :]))(ci))
                p.barrier()
                p.op("vector", (lambda ci: lambda e: e.tensor_add(
                    Ssh[:], Ssh[:],
                    vstg.rearrange("h (c u d) -> h c u d", c=8, u=2)[:, ci, 1, :]))(ci))
                p.barrier()
            p.dma("sync", (lambda cb8: lambda e: e.dma_start(
                out=vcar_d[cb8 * 8:(cb8 + 1) * 8, :, :, :].rearrange("c h u d -> h c u d"),
                in_=cstg.rearrange("h (c u d) -> h c u d", c=8, u=2),
                allow_slow_non_contiguous=True))(cb8))
            p.barrier()

        for g4 in range(4):
            hs = g4 * 4
            cols = slice(hs * 64, hs * 64 + 256)
            p.dma("sync", (lambda cols: lambda e: e.dma_start(
                out=qg.rearrange("p (t n) -> p t n", t=16),
                in_=q_d[:, cols].rearrange("(c t) n -> c t n", t=16)))(cols))
            p.dma("sync", (lambda cols: lambda e: e.dma_start(
                out=kg.rearrange("p (t n) -> p t n", t=16),
                in_=o_d[:, cols].rearrange("(c t) n -> c t n", t=16)))(cols))
            p.dma("gpsimd", (lambda g4: lambda e: e.dma_start(
                out=r1[:], in_=s3_d[g4 * 512:g4 * 512 + 128, 0:256]))(g4))
            p.dma("gpsimd", (lambda g4: lambda e: e.dma_start(
                out=r2[:], in_=s3_d[g4 * 512 + 128:g4 * 512 + 256, 0:256]))(g4))
            p.dma("sync", (lambda hs: lambda e: e.dma_start(
                out=Sfc.rearrange("p (h d) -> p h d", h=4), in_=vcar_d[:, hs:hs + 4, 0, :]))(hs))
            p.dma("sync", (lambda hs: lambda e: e.dma_start(
                out=Ssc.rearrange("p (h d) -> p h d", h=4), in_=vcar_d[:, hs:hs + 4, 1, :]))(hs))
            p.barrier()
            p.op("vector", lambda e: e.tensor_tensor(
                og.rearrange("p (t h d) -> p t h d", t=16, h=4),
                r1.rearrange("p (t h o) -> p t h o", t=16, o=1)[:, :, 0:4, :].broadcast_to([128, 16, 4, 64]),
                Sfc.rearrange("p (o h d) -> p o h d", o=1, h=4).broadcast_to([128, 16, 4, 64]),
                ALU.mult))
            p.barrier()
            p.op("vector", lambda e: e.tensor_tensor(
                ug.rearrange("p (t h d) -> p t h d", t=16, h=4),
                r2.rearrange("p (t h o) -> p t h o", t=16, o=1)[:, :, 0:4, :].broadcast_to([128, 16, 4, 64]),
                Ssc.rearrange("p (o h d) -> p o h d", o=1, h=4).broadcast_to([128, 16, 4, 64]),
                ALU.mult))
            p.barrier()
            p.op("vector", lambda e: e.tensor_add(og[:], og[:], ug[:]))
            p.barrier()
            p.op("vector", lambda e: e.tensor_mul(og[:], og[:], qg[:]))
            p.barrier()
            p.op("vector", lambda e: e.tensor_scalar_mul(og[:], og[:], 0.5))
            p.barrier()
            p.op("vector", lambda e: e.tensor_add(og[:], og[:], kg[:]))
            p.barrier()
            p.op("vector", lambda e: e.tensor_copy(qg[:], og[:]))
            p.barrier()
            p.dma("sync", (lambda cols: lambda e: e.dma_start(
                out=o_d[:, cols].rearrange("(c t) n -> c t n", t=16),
                in_=qg.rearrange("p (t n) -> p t n", t=16)))(cols))
            p.barrier()


        es2.close()
        for tt in range(NT):
            sl = slice(tt * 128, tt * 128 + 128)
            p.dma("sync", (lambda sl: lambda e: e.dma_start(out=bft[:], in_=o_d[sl, :]))(sl))
            p.dma("gpsimd", (lambda sl: lambda e: e.dma_start(out=sqt[:], in_=gg_d[sl, :]))(sl))
            p.barrier()
            p.op("vector", lambda e: e.tensor_mul(bft[:], bft[:], sqt[:]))
            p.barrier()
            p.dma("sync", (lambda sl: lambda e: e.dma_start(out=o_d[sl, :], in_=bft[:]))(sl))
            p.barrier()
        for kc in range(ND):
            p.dma("sync", (lambda kc: lambda e: e.dma_start(
                out=bft[:, 0:T] if False else qg[:, 0:T], in_=o_d[:, kc * 128:(kc + 1) * 128],
                transpose=True))(kc))
            p.barrier()
            p.dma("sync", (lambda kc: lambda e: e.dma_start(out=og_d[kc, :, :], in_=qg[:, 0:T]))(kc))
            p.barrier()
        for grp in range(4):
            tts = list(range(grp * 4, grp * 4 + 4))
            for ti, tt in enumerate(tts):
                for kc in range(ND):
                    p.dma("sync", (lambda ti, tt, kc: lambda e: e.dma_start(
                        out=ug[:, ti * 1024 + kc * 128: ti * 1024 + kc * 128 + 128],
                        in_=og_d[kc, :, tt * 128:(tt + 1) * 128]))(ti, tt, kc))
            p.barrier()
            for nb in range(2):
                for kc in range(ND):
                    sslot = (kc % 2) * 1024
                    p.dma("gpsimd", (lambda kc, nb, sslot: lambda e: e.dma_start(
                        out=wstr[:, sslot:sslot + 512],
                        in_=woT[kc * 128:(kc + 1) * 128, nb * 512:nb * 512 + 512]))(kc, nb, sslot))
                    p.barrier()
                    for ti in range(4):
                        p.op("tensor", (lambda kc, ti, sslot: lambda e: e.matmul(
                            PSA[ti][:, 0:512],
                            ug[:, ti * 1024 + kc * 128: ti * 1024 + kc * 128 + 128],
                            wstr[:, sslot:sslot + 512],
                            start=(kc == 0), stop=(kc == ND - 1)))(kc, ti, sslot))
                    p.barrier()
                for ti, tt in enumerate(tts):
                    p.op("vector", (lambda ti: lambda e: e.tensor_copy(st_[:, 0:512], PSA[ti][:, 0:512]))(ti))
                    p.barrier()
                    p.dma("sync", (lambda tt, nb: lambda e: e.dma_start(
                        out=s3_d[tt * 128:(tt + 1) * 128, nb * 512:nb * 512 + 512],
                        in_=st_[:, 0:512]))(tt, nb))
                    p.barrier()

        gwsb = es.enter_context(nc.sbuf_tensor(pfx + "gwsb", [128, ND * E], BF16))
        p.dma("gpsimd", lambda e: e.dma_start(out=gwsb.rearrange("p (k n) -> p k n", k=ND),
                                              in_=gatewT.rearrange("(k p) n -> p k n", p=128)))
        p.dma("sync", lambda e: e.dma_start(out=stat[0:1, 0:8] if False else sqt[0:1, 0:E],
                                            in_=gateb.rearrange("(o d) -> o d", o=1)))
        p.barrier()
        p.op("tensor", lambda e: e.matmul(psF[:, 0:E], onesf[0:1, 0:128], sqt[0:1, 0:E],
                                          start=True, stop=True))
        p.barrier()
        p.op("vector", lambda e: e.tensor_copy(grow[:], psF[:, 0:E]))
        p.barrier()
        m8 = es.enter_context(nc.sbuf_tensor(pfx + "m8", [128, 8], F32))
        m8b = es.enter_context(nc.sbuf_tensor(pfx + "m8b", [128, 8], F32))
        scM = es.enter_context(nc.sbuf_tensor(pfx + "scM", [128, E], F32))
        scS = es.enter_context(nc.sbuf_tensor(pfx + "scS", [128, E], F32))
        for tt in range(NT):
            for kc in range(ND):
                p.op("tensor", (lambda tt, kc: lambda e: e.matmul(
                    ps0[:, 0:E], xnfm[:, kc * T + tt * 128: kc * T + tt * 128 + 128],
                    gwsb[:, kc * E:(kc + 1) * E], start=(kc == 0), stop=(kc == ND - 1)))(tt, kc))
            p.barrier()
            p.op("scalar", lambda e: e.activation(scS[:], ps0[:, 0:E], AF.Sigmoid))
            p.barrier()
            p.op("vector", lambda e: e.tensor_add(scM[:], scS[:], grow[:]))
            p.barrier()
            p.op("vector", lambda e: e.reduce_max(m8[:], scM.rearrange("p (g r) -> p g r", g=8),
                                                  axis=AX.X))
            p.barrier()
            p.op("vector", lambda e: e.tensor_tensor(
                sqt[:, 0:E].rearrange("p (g r) -> p g r", g=8),
                scM.rearrange("p (g r) -> p g r", g=8),
                m8.rearrange("p (g o) -> p g o", o=1).broadcast_to([128, 8, 4]),
                ALU.is_lt))
            p.barrier()
            p.op("vector", lambda e: e.tensor_scalar(sqt[:, E:2 * E], sqt[:, 0:E], 1.0, 2.0e30,
                                                     ALU.subtract, ALU.mult))
            p.barrier()
            p.op("vector", lambda e: e.tensor_add(sqt[:, E:2 * E], sqt[:, E:2 * E], scM[:]))
            p.barrier()
            p.op("vector", lambda e: e.reduce_max(m8b[:], sqt[:, E:2 * E].rearrange("p (g r) -> p g r", g=8),
                                                  axis=AX.X))
            p.barrier()
            p.op("vector", lambda e: e.tensor_add(m8[:], m8[:], m8b[:]))
            p.barrier()
            p.op("vector", lambda e: e.max(m8b[:], m8[:]))
            p.barrier()
            p.op("vector", lambda e: e.tensor_scalar(m8[:], m8[:], m8b[:, 3:4], None, ALU.is_ge))
            p.barrier()
            p.op("vector", lambda e: e.tensor_tensor(
                scM.rearrange("p (g r) -> p g r", g=8),
                scM.rearrange("p (g r) -> p g r", g=8),
                m8.rearrange("p (g o) -> p g o", o=1).broadcast_to([128, 8, 4]),
                ALU.mult))
            p.barrier()
            p.op("vector", lambda e: e.max(m8b[:], scM[:]))
            p.barrier()
            p.op("vector", lambda e: e.tensor_scalar(sqt[:, 0:E], scM[:], m8b[:, 7:8], None, ALU.is_ge))
            p.barrier()
            p.op("vector", lambda e: e.tensor_scalar(sqt[:, E:2 * E], scM[:], 0.0, None, ALU.not_equal))
            p.barrier()
            p.op("vector", lambda e: e.tensor_mul(sqt[:, 0:E], sqt[:, 0:E], sqt[:, E:2 * E]))
            p.barrier()
            p.op("vector", lambda e: e.tensor_mul(sqt[:, 0:E], sqt[:, 0:E], scS[:]))
            p.barrier()
            p.op("vector", lambda e: e.reduce_sum(stat[:, 0:1], sqt[:, 0:E], axis=AX.X))
            p.barrier()
            p.op("vector", lambda e: e.tensor_scalar_add(stat[:, 0:1], stat[:, 0:1], 1e-20))
            p.barrier()
            p.op("vector", lambda e: e.reciprocal(stat[:, 1:2], stat[:, 0:1]))
            p.barrier()
            p.op("vector", lambda e: e.tensor_scalar_mul(stat[:, 1:2], stat[:, 1:2], 2.5))
            p.barrier()
            p.op("vector", (lambda tt: lambda e: e.tensor_scalar(
                cmb32[:, tt * E:(tt + 1) * E], sqt[:, 0:E], stat[:, 1:2], None, ALU.mult))(tt))
            p.barrier()

        wge = es.enter_context(nc.sbuf_tensor(pfx + "wge", [128, ND * I2], BF16))
        wue = es.enter_context(nc.sbuf_tensor(pfx + "wue", [128, ND * I2], BF16))
        wde = es.enter_context(nc.sbuf_tensor(pfx + "wde", [128, 4 * D], BF16))
        zfm = es.enter_context(nc.sbuf_tensor(pfx + "zfm", [128, 4 * T], BF16))
        ufm = es.enter_context(nc.sbuf_tensor(pfx + "ufm", [128, 4 * T], BF16))
        yaccL = es.enter_context(nc.sbuf_tensor(pfx + "yaccL", [128, NT * D], F32))
        for tt in range(NT):
            p.op("vector", (lambda tt: lambda e: e.memset(yaccL[:, tt * D:(tt + 1) * D], 0.0))(tt))
        p.barrier()
        for e_ in range(E):
            p.dma("gpsimd", (lambda e_: lambda e: e.dma_start(
                out=wge.rearrange("p (k n) -> p k n", k=ND),
                in_=wgate[e_].rearrange("(k p) n -> p k n", p=128)))(e_))
            p.dma("gpsimd", (lambda e_: lambda e: e.dma_start(
                out=wue.rearrange("p (k n) -> p k n", k=ND),
                in_=wup[e_].rearrange("(k p) n -> p k n", p=128)))(e_))
            p.dma("gpsimd", (lambda e_: lambda e: e.dma_start(
                out=wde.rearrange("p (k n) -> p k n", k=4),
                in_=wdownT[e_].rearrange("(k p) n -> p k n", p=128)))(e_))
            p.barrier()
            for (wsb_, dstfm, act) in ((wge, zfm, AF.Silu), (wue, ufm, AF.Identity)):
                for rnd in range(2):
                    pairs = [(ic, tb) for ic in range(4) for tb in range(4)][rnd * 8:(rnd + 1) * 8]
                    for pi, (ic, tb) in enumerate(pairs):
                        for kc in range(ND):
                            p.op("tensor", (lambda pi, ic, tb, kc, wsb_: lambda e: e.matmul(
                                PSA[pi % 4][:, 0:512] if pi < 4 else [psF, ps0, ps1, ps2][0][:, 0:512] if False else PSA[pi % 4][:, 0:512],
                                wsb_[:, kc * I2 + ic * 128: kc * I2 + ic * 128 + 128],
                                xnfm[:, kc * T + tb * 512: kc * T + tb * 512 + 512],
                                start=(kc == 0), stop=(kc == ND - 1)))(pi, ic, tb, kc, wsb_))
                        if pi == 3 or pi == 7:
                            p.barrier()
                            for pj in range(pi - 3, pi + 1):
                                ic2, tb2 = pairs[pj]
                                if act == AF.Silu:
                                    p.op("scalar", (lambda pj, ic2, tb2, dstfm: lambda e: e.activation(
                                        dstfm[:, ic2 * T + tb2 * 512: ic2 * T + tb2 * 512 + 512],
                                        PSA[pj % 4][:, 0:512], AF.Silu))(pj, ic2, tb2, dstfm))
                                else:
                                    p.op("vector", (lambda pj, ic2, tb2, dstfm: lambda e: e.tensor_copy(
                                        dstfm[:, ic2 * T + tb2 * 512: ic2 * T + tb2 * 512 + 512],
                                        PSA[pj % 4][:, 0:512]))(pj, ic2, tb2, dstfm))
                            p.barrier()
            p.op("vector", lambda e: e.tensor_mul(zfm[:], zfm[:], ufm[:]))
            p.barrier()
            for rnd in range(8):
                quads = [(tt, nb) for tt in range(NT) for nb in range(2)][rnd * 4:(rnd + 1) * 4]
                for qi, (tt, nb) in enumerate(quads):
                    for ic in range(4):
                        p.op("tensor", (lambda qi, tt, nb, ic: lambda e: e.matmul(
                            PSA[qi][:, 0:512],
                            zfm[:, ic * T + tt * 128: ic * T + tt * 128 + 128],
                            wde[:, ic * D + nb * 512: ic * D + nb * 512 + 512],
                            start=(ic == 0), stop=(ic == 3)))(qi, tt, nb, ic))
                p.barrier()
                for qi, (tt, nb) in enumerate(quads):
                    p.op("vector", (lambda qi, tt, nb, e_: lambda e: e.scalar_tensor_tensor(
                        yaccL[:, tt * D + nb * 512: tt * D + nb * 512 + 512],
                        PSA[qi][:, 0:512], cmb32[:, tt * E + e_: tt * E + e_ + 1],
                        yaccL[:, tt * D + nb * 512: tt * D + nb * 512 + 512],
                        ALU.mult, ALU.add))(qi, tt, nb, e_))
                p.barrier()

        for tt in range(NT):
            sl = slice(tt * 128, tt * 128 + 128)
            for nb in range(2):
                ps = [ps1, ps2][nb]
                for kc in range(ND):
                    sslot = (kc % 2) * 1024
                    p.dma("gpsimd", (lambda kc, nb, sslot: lambda e: e.dma_start(
                        out=wstr[:, sslot:sslot + 512],
                        in_=shgT[kc * 128:(kc + 1) * 128, nb * 512:nb * 512 + 512]))(kc, nb, sslot))
                    p.barrier()
                    p.op("tensor", (lambda kc, tt, ps, sslot: lambda e: e.matmul(
                        ps[:, 0:512], xnfm[:, kc * T + tt * 128: kc * T + tt * 128 + 128],
                        wstr[:, sslot:sslot + 512],
                        start=(kc == 0), stop=(kc == ND - 1)))(kc, tt, ps, sslot))
                    p.barrier()
                p.op("scalar", (lambda nb, ps: lambda e: e.activation(
                    st_[:, nb * 512:nb * 512 + 512], ps[:, 0:512], AF.Silu))(nb, ps))
                p.barrier()
            for nb in range(2):
                ps = [ps1, ps2][nb]
                for kc in range(ND):
                    sslot = (kc % 2) * 1024
                    p.dma("gpsimd", (lambda kc, nb, sslot: lambda e: e.dma_start(
                        out=wstr[:, sslot:sslot + 512],
                        in_=shuT[kc * 128:(kc + 1) * 128, nb * 512:nb * 512 + 512]))(kc, nb, sslot))
                    p.barrier()
                    p.op("tensor", (lambda kc, tt, ps, sslot: lambda e: e.matmul(
                        ps[:, 0:512], xnfm[:, kc * T + tt * 128: kc * T + tt * 128 + 128],
                        wstr[:, sslot:sslot + 512],
                        start=(kc == 0), stop=(kc == ND - 1)))(kc, tt, ps, sslot))
                    p.barrier()
                p.op("vector", (lambda nb, ps: lambda e: e.tensor_mul(
                    st_[:, nb * 512:nb * 512 + 512], st_[:, nb * 512:nb * 512 + 512],
                    ps[:, 0:512]))(nb, ps))
                p.barrier()
            p.op("vector", lambda e: e.tensor_copy(bft[:], st_[:]))
            p.barrier()
            p.dma("sync", (lambda sl: lambda e: e.dma_start(out=o_d[sl, :], in_=bft[:]))(sl))
            p.barrier()
        for tt in range(NT):
            sl = slice(tt * 128, tt * 128 + 128)
            for kc in range(ND):
                p.dma("sync", (lambda tt, kc: lambda e: e.dma_start(
                    out=ug[:, kc * 128:(kc + 1) * 128],
                    in_=o_d[tt * 128:(tt + 1) * 128, kc * 128:(kc + 1) * 128],
                    transpose=True))(tt, kc))
            p.barrier()
            for nb in range(2):
                ps = [ps1, ps2][nb]
                for kc in range(ND):
                    sslot = (kc % 2) * 1024
                    p.dma("gpsimd", (lambda kc, nb, sslot: lambda e: e.dma_start(
                        out=wstr[:, sslot:sslot + 512],
                        in_=shdT[kc * 128:(kc + 1) * 128, nb * 512:nb * 512 + 512]))(kc, nb, sslot))
                    p.barrier()
                    p.op("tensor", (lambda kc, ps, sslot: lambda e: e.matmul(
                        ps[:, 0:512], ug[:, kc * 128:(kc + 1) * 128],
                        wstr[:, sslot:sslot + 512],
                        start=(kc == 0), stop=(kc == ND - 1)))(kc, ps, sslot))
                    p.barrier()
                p.op("vector", (lambda tt, nb, ps: lambda e: e.tensor_add(
                    gt_[:, nb * 512:nb * 512 + 512],
                    yaccL[:, tt * D + nb * 512: tt * D + nb * 512 + 512],
                    ps[:, 0:512]))(tt, nb, ps))
                p.barrier()
            p.op("scalar", lambda e: e.activation(gt_[:], gt_[:], AF.Silu))
            p.barrier()
            p.dma("sync", (lambda sl: lambda e: e.dma_start(out=st_[:], in_=s3_d[sl, :]))(sl))
            p.dma("sync", (lambda ap: lambda e: e.dma_start(out=xt[:], in_=ap))(xin_tile(tt)))
            p.barrier()
            p.op("vector", lambda e: e.tensor_mul(st_[:], st_[:], gt_[:]))
            p.barrier()
            p.op("vector", lambda e: e.tensor_add(st_[:], st_[:], xt[:]))
            p.barrier()
            p.dma("sync", (lambda ap: lambda e: e.dma_start(out=ap, in_=st_[:]))(xout_tile(tt)))
            p.barrier()
        if standalone:
            p.build()
    es.close()
    return nc
